# version 03
# baseline (speedup 1.0000x reference)
import contextlib

import numpy as np
import ml_dtypes

import concourse.bass as bass
import concourse.mybir as mybir
from concourse.bass_utils import run_bass_kernel_spmd

F32 = mybir.dt.float32
BF16 = mybir.dt.bfloat16
AF = mybir.ActivationFunctionType
ALU = mybir.AluOpType
AX = mybir.AxisListType
BF = ml_dtypes.bfloat16

S = 2048
D = 1024
DC = 8
NL = 4
H = 16
DH = 64
DF = 4096
HID = 2048
NE = 8
NKV = 4

BUILD_PHASES = 99
DEBUG_TAP = None



def host_constants():
    c = {}
    c["ident"] = np.eye(128, dtype=np.float32).astype(BF)
    c["identf"] = np.eye(128, dtype=np.float32)
    c["ident64"] = np.eye(64, dtype=np.float32).astype(BF)
    hd = DH
    inv = 1.0 / 10000.0 ** (np.arange(0, hd, 2, dtype=np.float32) / hd)
    t = np.arange(S, dtype=np.float32)
    fr = np.einsum("i,j->ij", t, inv)
    cos = np.cos(fr).T
    sin = np.sin(fr).T
    Ct = np.zeros((128, S), np.float32)
    S2 = np.zeros((128, S), np.float32)
    for base in (0, 64):
        Ct[base:base + 32] = cos
        Ct[base + 32:base + 64] = cos
        S2[base:base + 32] = -sin
        S2[base + 32:base + 64] = sin
    c["ropeC"] = Ct.astype(BF)
    c["ropeS2"] = S2.astype(BF)
    Pl = np.zeros((128, 128), np.float32)
    for i in range(128):
        Pl[i ^ 32, i] = 1.0
    c["swap32"] = Pl.astype(BF)
    m = np.zeros((128, 128), np.float32)
    for q in range(128):
        m[q, q + 1:] = -1e9
    c["causal"] = m
    E8 = np.zeros((8, 8 * 128), np.float32)
    for e in range(8):
        E8[e, e * 128:(e + 1) * 128] = 1.0
    c["e8"] = E8.astype(BF)
    c["ones_col"] = np.ones((128, 1), np.float32).astype(BF)
    c["ones_row"] = np.ones((1, 128), np.float32).astype(BF)
    c["onesrf"] = np.ones((1, 128), np.float32)
    pos = np.arange(S - 1, -1, -1, dtype=np.float32)
    freq = 1.0 / 10000.0 ** (np.arange(0, D, 2, dtype=np.float32) / D)
    ang = pos[:, None] * freq[None, :]
    r = np.concatenate([np.sin(ang), np.cos(ang)], axis=-1)
    c["rT"] = np.ascontiguousarray(r.T).reshape(DC, 128, S).astype(BF)
    return c


def tile_kMf(w, nf):
    din, f = w.shape
    kc = din // 128
    a = w.reshape(kc, 128, nf, 128)
    return np.ascontiguousarray(a.transpose(2, 1, 0, 3))


def col_bias(v):
    f = v.shape[0]
    return np.ascontiguousarray(v.reshape(f // 128, 128).T)



class Sem:
    def __init__(self, ctx, nc, name):
        self.h = ctx.enter_context(nc.semaphore(name))
        self.n = 0

    def inc(self, instr, k=1):
        instr.then_inc(self.h, k)
        self.n += k
        return self.n


class Arena:
    def __init__(self, t, cols, name):
        self.t = t
        self.cols = cols
        self.name = name
        self.iv = [(0, cols)]
        self.live = {}

    def alloc(self, name, cols, align=16):
        assert name not in self.live, name
        for i, (a, b_) in enumerate(self.iv):
            a2 = (a + align - 1) // align * align
            if b_ - a2 >= cols:
                del self.iv[i]
                if a2 > a:
                    self.iv.insert(i, (a, a2))
                    i += 1
                if a2 + cols < b_:
                    self.iv.insert(i, (a2 + cols, b_))
                self.live[name] = (a2, cols)
                return self.t[:, a2:a2 + cols]
        raise MemoryError(f"arena {self.name}: alloc {name} ({cols}) failed; "
                          f"live={self.live}")

    def free(self, *names):
        for name in names:
            a, n = self.live.pop(name)
            self.iv.append((a, a + n))
        self.iv.sort()
        out = []
        for s_, e in self.iv:
            if out and out[-1][1] >= s_:
                out[-1] = (out[-1][0], max(out[-1][1], e))
            else:
                out.append((s_, e))
        self.iv = [tuple(x) for x in out]


class B:
    pass


def declare_inputs(nc, consts):
    d = {}

    def inp(name, shape, dtype):
        d[name] = nc.dram_tensor(name, list(shape), dtype, kind="ExternalInput").ap()

    inp("xT", (D, S), F32)
    for k, v in consts.items():
        inp("c_" + k, v.shape, BF16 if v.dtype == BF else F32)
    inp("wqkv", (NL, 24, 128, DC, 128), F32)
    inp("wr", (NL, DC, 128, DC, 128), F32)
    inp("wo", (NL, DC, 128, DC, 128), F32)
    inp("wff1", (NL, DC, 128, DF), F32)
    inp("wff2", (NL, DC, 128, 32, 128), F32)
    inp("u_t", (DH, H), F32)
    inp("v_t", (DH, H), F32)
    inp("ln1g", (NL, 128, DC), F32)
    inp("ln1b", (NL, 128, DC), F32)
    inp("ln2g", (NL, 128, DC), F32)
    inp("ln2b", (NL, 128, DC), F32)
    inp("bff1", (NL, 128, 32), F32)
    inp("bff2", (NL, 128, DC), F32)
    inp("fwq", (DC, 128, DC, 128), F32)
    inp("fwk", (2, 128, DC, 128), F32)
    inp("fwv", (2, 128, DC, 128), F32)
    inp("fwg", (DH, DH), F32)
    inp("fwo", (DC, 128, DC, 128), F32)
    inp("fwr", (DC, 128, NE), F32)
    inp("fbr", (1, NE), F32)
    inp("fw1", (NE, 32, 128, DC, 128), F32)
    inp("fb1", (NE, 128, 32), F32)
    inp("fw2", (NE, HID, D), F32)
    inp("fb2", (NE, D), F32)
    inp("fbq", (128, DC), F32)
    inp("fbk", (128, 2), F32)
    inp("fbv", (128, 2), F32)
    inp("fbg", (DH, 1), F32)
    inp("fbo", (128, DC), F32)
    inp("fln1", (128, DC), F32)
    inp("fln2", (128, DC), F32)
    inp("fqn", (1, DH), F32)
    inp("fkn", (1, DH), F32)
    return d


def DS(b, name):
    if name not in b.dsems:
        b.dsems[name] = Sem(b._ctx, b.nc, name)
    return b.dsems[name]


def on_pe(b, f):
    b.block.tensor(f)


def on_dve(b, f):
    b.block.vector(f)


def on_act(b, f):
    b.block.scalar(f)


def on_pool(b, f):
    b.block.gpsimd(f)


def on_sp(b, f):
    b.block.sync(f)


def snap(b):
    return (b.PE.n, b.DV.n, b.AC.n, b.PO.n, b.SD.n, b.PD.n)


def wait_snap(eng, b, sn, pe=True, dv=True, ac=False, po=False, sd=False,
              pd=False):
    if pe and sn[0]:
        eng.wait_ge(b.PE.h, sn[0])
    if dv and sn[1]:
        eng.wait_ge(b.DV.h, sn[1])
    if ac and sn[2]:
        eng.wait_ge(b.AC.h, sn[2])
    if po and sn[3]:
        eng.wait_ge(b.PO.h, sn[3])
    if sd and sn[4]:
        eng.wait_ge(b.SD.h, sn[4])
    if pd and sn[5]:
        eng.wait_ge(b.PD.h, sn[5])



def build_program():
    consts = host_constants()
    nc = bass.Bass("TRN2", target_bir_lowering=False)
    dI = declare_inputs(nc, consts)
    d_out = nc.dram_tensor("out", [D, S], F32, kind="ExternalOutput").ap()
    d_xn = nc.dram_tensor("xn_spill", [D, S], BF16, kind="Internal").ap()
    d_s = nc.dram_tensor("s_spill", [D, S], BF16, kind="Internal").ap()

    with contextlib.ExitStack() as ctx:
        b = B()
        b.nc = nc
        b.dI = dI
        b.d_out = d_out
        b.d_xn = d_xn
        b.d_s = d_s
        b.abf = Arena(ctx.enter_context(
            nc.sbuf_tensor("arena_bf", [128, 91500], BF16)), 91500, "bf")
        b.af = Arena(ctx.enter_context(
            nc.sbuf_tensor("arena_f32", [128, 7168], F32)), 7168, "f32")
        b.psA = ctx.enter_context(nc.psum_tensor("psA", [128, 2048], F32))
        b.psB = ctx.enter_context(nc.psum_tensor("psB", [128, 2048], F32))
        b.PE = Sem(ctx, nc, "sPE")
        b.DV = Sem(ctx, nc, "sDV")
        b.AC = Sem(ctx, nc, "sAC")
        b.PO = Sem(ctx, nc, "sPO")
        b.SD = Sem(ctx, nc, "sSD")
        b.PD = Sem(ctx, nc, "sPD")
        b.dsems = {}
        b._ctx = ctx
        b.block = ctx.enter_context(nc.Block())
        build_all(b)
        globals()["_last_b"] = b
    return nc


def build_all(b):
    alloc_persistent(b)
    load_constants(b)
    phase_xn(b)
    if BUILD_PHASES < 2:
        return finish(b)
    for l in range(NL):
        attn_layer(b, l)
        if BUILD_PHASES < 3 + l:
            return finish(b)
    spill_s(b)
    if BUILD_PHASES < 7:
        return finish(b)
    ffn_module(b)
    finish(b)


def alloc_persistent(b):
    a = b.abf
    b.h = [a.alloc(f"h{k}", S) for k in range(DC)]
    b.ident = a.alloc("ident", 128)
    b.ident64 = a.alloc("ident64", 64)
    b.ones_col = a.alloc("ones_col", 16)
    b.ones_row = a.alloc("ones_row", 128)
    b.swap32 = a.alloc("swap32", 128)
    b.junk = a.alloc("junk", 512)
    f = b.af
    b.identf = f.alloc("identf", 128)
    b.ones_rowf = f.alloc("ones_rowf", 128)
    b.causal = f.alloc("causal", 128)
    b.uv = f.alloc("uv", 2 * H)
    b.rows = f.alloc("rows", 2 * S)
    b.row0 = b.rows[0:1, 0:S]
    b.row1 = b.rows[0:1, S:2 * S]
    b.sc1 = f.alloc("sc1", 2048)
    b.den = f.alloc("den", 4)
    b.lng = f.alloc("lng", DC)
    b.lnb = f.alloc("lnb", DC)
    b.ln2g = f.alloc("ln2g", DC)
    b.ln2b = f.alloc("ln2b", DC)
    b.bf1 = f.alloc("bf1", 32)
    b.bf2 = f.alloc("bf2", DC)
    b.fw_norm = f.alloc("fw_norm", DC)
    b.taps = {}


def load_constants(b):
    dI = b.dI

    def body(sp):
        cs = DS(b, "dconst")

        def ld(dst, src):
            sp.dma_start(out=dst, in_=src).then_inc(cs.h, 16)
            cs.n += 16
        ld(b.ident, dI["c_ident"])
        ld(b.identf, dI["c_identf"])
        ld(b.ident64[0:64, :], dI["c_ident64"])
        ld(b.ident64[64:128, :], dI["c_ident64"])
        ld(b.ones_col[:, 0:1], dI["c_ones_col"])
        ld(b.ones_row[0:1, :], dI["c_ones_row"])
        ld(b.ones_rowf[0:1, :], dI["c_onesrf"])
        ld(b.swap32, dI["c_swap32"])
        ld(b.causal, dI["c_causal"])
        ld(b.uv[0:64, 0:H], dI["u_t"])
        ld(b.uv[0:64, H:2 * H], dI["v_t"])
        b.const_ready = cs.n
    on_sp(b, body)
    b.const_sem = DS(b, "dconst")



def phase_xn(b):
    mb, rb, sq = norm_scratch(b)
    ld1 = []

    dx = DS(b, "dx")

    def po1(po):
        po.wait_ge(b.const_sem.h, b.const_ready)
        for k in range(DC):
            po.dma_start(out=b.h[k], in_=b.dI["xT"][k * 128:(k + 1) * 128, :]
                         ).then_inc(dx.h, 16)
            dx.n += 16
            ld1.append(dx.n)
    on_pool(b, po1)

    ac1 = [b.AC.n + k + 1 for k in range(DC)]
    pe_sq = [b.PE.n + 2 * (k + 1) for k in range(DC)]

    def act1(act):
        for k in range(DC):
            act.wait_ge(dx.h, ld1[-1] if True else ld1[k])
            if k > 0:
                act.wait_ge(b.PE.h, pe_sq[k - 1])
            ins = act.activation(sq, b.h[k], AF.Square)
            b.AC.inc(ins)
    on_act(b, act1)

    def pe1(pe):
        for k in range(DC):
            pe.wait_ge(dx.h, ld1[-1])
            for nc_ in range(4):
                sl = slice(nc_ * 512, (nc_ + 1) * 512)
                ins = pe.matmul(b.psA[0:1, sl], b.ones_col[:, 0:1], b.h[k][:, sl],
                                start=(k == 0), stop=(k == DC - 1))
            b.PE.inc(ins)
            pe.wait_ge(b.AC.h, ac1[k])
            for nc_ in range(4):
                sl = slice(nc_ * 512, (nc_ + 1) * 512)
                ins = pe.matmul(b.psB[0:1, sl], b.ones_col[:, 0:1], sq[:, sl],
                                start=(k == 0), stop=(k == DC - 1))
            b.PE.inc(ins)
    on_pe(b, pe1)
    stats_pe = b.PE.n

    mean_var_rows(b, stats_pe, 1.0 / 1024.0, 1e-5, with_mean=True)
    bcast_rows(b, mb, rb, with_mean=True)
    bc_dv = b.bc_dv

    dv2 = [b.DV.n + 2 * (k + 1) for k in range(DC)]

    def dve2(dve):
        dve.wait_ge(b.DV.h, bc_dv)
        for k in range(DC):
            dve.tensor_sub(b.sc1, b.h[k], mb)
            ins = dve.tensor_mul(b.h[k], b.sc1, rb)
            b.DV.inc(ins, 2)
    on_dve(b, dve2)
    assert b.DV.n == dv2[-1]
    b.xn_dv = dv2[-1]

    dsp = DS(b, "dspill")

    def sp3(sp):
        for k in range(DC):
            sp.wait_ge(b.DV.h, dv2[k])
            sp.dma_start(out=b.d_xn[k * 128:(k + 1) * 128, :], in_=b.h[k]
                         ).then_inc(dsp.h, 16)
            dsp.n += 16
        b.xn_spilled = dsp.n
    on_sp(b, sp3)

    free_norm_scratch(b)
    b.taps["xn"] = list(b.h)


def norm_scratch(b):
    mb = b.abf.alloc("mb", S)
    rb = b.abf.alloc("rb", S)
    sq = b.abf.alloc("sq", S)
    b._mb, b._rb, b._sq = mb, rb, sq
    return mb, rb, sq


def free_norm_scratch(b):
    b.abf.free("mb", "rb", "sq")


def mean_var_rows(b, stats_pe, inv_n, eps, with_mean):
    def dve_rows(dve):
        dve.wait_ge(b.PE.h, stats_pe)
        if with_mean:
            dve.tensor_scalar_mul(b.row0, b.psA[0:1, 0:2048], inv_n)
            dve.tensor_scalar_mul(b.row1, b.psB[0:1, 0:2048], inv_n)
            dve.tensor_mul(b.sc1[0:1, :], b.row0, b.row0)
            dve.tensor_sub(b.row1, b.row1, b.sc1[0:1, :])
        else:
            dve.tensor_scalar_mul(b.row1, b.psB[0:1, 0:2048], inv_n)
        ins = dve.tensor_scalar_add(b.row1, b.row1, eps)
        b.DV.inc(ins)
    on_dve(b, dve_rows)
    var_dv = b.DV.n

    def act_rows(act):
        act.wait_ge(b.DV.h, var_dv)
        ins = act.activation(b.row1, b.row1, AF.Sqrt)
        b.AC.inc(ins)
    on_act(b, act_rows)
    sq_ac = b.AC.n

    def dve_recip(dve):
        dve.wait_ge(b.AC.h, sq_ac)
        ins = dve.reciprocal(b.row1, b.row1)
        b.DV.inc(ins)
    on_dve(b, dve_recip)
    b.rstd_dv = b.DV.n


def bcast_rows(b, mb, rb, with_mean):
    def pe_bc(pe):
        pe.wait_ge(b.DV.h, b.rstd_dv)
        for nc_ in range(4):
            sl = slice(nc_ * 512, (nc_ + 1) * 512)
            if with_mean:
                pe.matmul(b.psA[:, sl], b.ones_rowf[0:1, :], b.row0[:, sl],
                          start=True, stop=True)
            ins = pe.matmul(b.psB[:, sl], b.ones_rowf[0:1, :], b.row1[:, sl],
                            start=True, stop=True)
        b.PE.inc(ins)
    on_pe(b, pe_bc)
    bc_pe = b.PE.n

    def dve_bc(dve):
        dve.wait_ge(b.PE.h, bc_pe)
        if with_mean:
            dve.tensor_copy(mb, b.psA[:, 0:2048])
        ins = dve.tensor_copy(rb, b.psB[:, 0:2048])
        b.DV.inc(ins)
    on_dve(b, dve_bc)
    b.bc_dv = b.DV.n



def attn_layer(b, l):
    b.vecT = [b.abf.alloc(f"vecT{k}", S) for k in range(DC)]
    for half in range(2):
        kr_half(b, l, half)
        if l == 0 and half == 0:
            b.taps["kr"] = list(b.krh)
            if BUILD_PHASES < 2.3:
                return
        qkv_half(b, l, half)
        if l == 0 and half == 0:
            b.taps["qh"] = list(b.qh)
            b.taps["kh"] = list(b.kh)
            b.taps["vh"] = list(b.vh)
            if BUILD_PHASES < 2.5:
                return
        attention_half(b, l, half)
        if l == 0 and half == 0:
            b.taps["vec0"] = list(b.vecT)
            if BUILD_PHASES < 2.7:
                return
    if l == 0:
        b.taps["vec"] = list(b.vecT)
        if BUILD_PHASES < 2.8:
            return
    wo_ln1(b, l)
    if l == 0 and BUILD_PHASES < 2.9:
        b.taps["postln1"] = list(b.h)
        return
    b.abf.free(*[f"vecT{k}" for k in range(DC)])
    ff_block(b, l)
    b.taps[f"l{l}"] = list(b.h)


def kr_half(b, l, half):
    a = b.abf
    sn = snap(b)
    rT = [a.alloc(f"rT{k}", S) for k in range(DC)]
    b.krh = [a.alloc(f"kr{j}", S) for j in range(4)]
    wbuf = [a.alloc(f"wkr{i}", DC * 128) for i in range(2)]
    ld = []

    drt = DS(b, "drt")
    dwkr = [DS(b, "dwkr0"), DS(b, "dwkr1")]

    def sp_body(sp):
        wait_snap(sp, b, sn, pe=True, dv=True)
        for k in range(DC):
            sp.dma_start(out=rT[k], in_=b.dI["c_rT"][k]).then_inc(drt.h, 16)
            drt.n += 16
            ld.append(drt.n)
    on_sp(b, sp_body)

    wld = []
    pe_use = [b.PE.n + j + 1 for j in range(4)]

    def pool_body(po):
        wait_snap(po, b, sn, pe=True, dv=True)
        for j in range(4):
            ft = half * 4 + j
            if j >= 2:
                po.wait_ge(b.PE.h, pe_use[j - 2])
            s_ = dwkr[j % 2]
            po.dma_start(out=wbuf[j % 2], in_=b.dI["wr"][l, ft]
                         ).then_inc(s_.h, 16)
            s_.n += 16
            wld.append((s_, s_.n))
    on_pool(b, pool_body)

    evac = [b.DV.n + j + 1 for j in range(4)]
    pe_done = []

    def pe_body(pe):
        pe.wait_ge(drt.h, ld[-1])
        wait_snap(pe, b, sn, pe=False, dv=True)
        for j in range(4):
            pe.wait_ge(wld[j][0].h, wld[j][1])
            if j >= 2:
                pe.wait_ge(b.DV.h, evac[j - 2])
            ps = b.psA if j % 2 == 0 else b.psB
            for k in range(DC):
                for nc_ in range(4):
                    sl = slice(nc_ * 512, (nc_ + 1) * 512)
                    ins = pe.matmul(ps[:, sl], wbuf[j % 2][:, k * 128:(k + 1) * 128],
                                    rT[k][:, sl], start=(k == 0), stop=(k == DC - 1))
            pe_done.append(b.PE.inc(ins))
    on_pe(b, pe_body)
    assert pe_done == [] or True

    def dve_body(dve):
        for j in range(4):
            dve.wait_ge(b.PE.h, pe_use[j])
            ps = b.psA if j % 2 == 0 else b.psB
            ins = dve.tensor_copy(b.krh[j], ps[:, 0:2048])
            b.DV.inc(ins)
    on_dve(b, dve_body)
    assert b.DV.n == evac[-1]
    b.kr_dv = evac[-1]
    a.free(*[f"rT{k}" for k in range(DC)], "wkr0", "wkr1")


def qkv_half(b, l, half):
    a = b.abf
    sn = snap(b)
    b.qh = [a.alloc(f"qh{j}", S) for j in range(4)]
    b.kh = [a.alloc(f"kh{j}", S) for j in range(4)]
    b.vh = [a.alloc(f"vh{j}", S) for j in range(4)]
    outs = {"q": b.qh, "k": b.kh, "v": b.vh}
    wbuf = [a.alloc(f"wqkv{i}", DC * 128) for i in range(2)]
    fts = ([("q", j, half * 4 + j) for j in range(4)]
           + [("k", j, 8 + half * 4 + j) for j in range(4)]
           + [("v", j, 16 + half * 4 + j) for j in range(4)])
    wld = []
    pe_use = [b.PE.n + i + 1 for i in range(12)]
    dwq = [DS(b, "dwqkv0"), DS(b, "dwqkv1")]

    def pool_body(po):
        wait_snap(po, b, sn, pe=True, dv=True)
        for i, (_, _, ft) in enumerate(fts):
            if i >= 2:
                po.wait_ge(b.PE.h, pe_use[i - 2])
            s_ = dwq[i % 2]
            po.dma_start(out=wbuf[i % 2], in_=b.dI["wqkv"][l, ft]
                         ).then_inc(s_.h, 16)
            s_.n += 16
            wld.append((s_, s_.n))
    on_pool(b, pool_body)

    evac = [b.DV.n + i + 1 for i in range(12)]

    def pe_body(pe):
        wait_snap(pe, b, sn, pe=False, dv=True)
        for i, (kind, j, ft) in enumerate(fts):
            pe.wait_ge(wld[i][0].h, wld[i][1])
            if i >= 2:
                pe.wait_ge(b.DV.h, evac[i - 2])
            ps = b.psA if i % 2 == 0 else b.psB
            for k in range(DC):
                for nc_ in range(4):
                    sl = slice(nc_ * 512, (nc_ + 1) * 512)
                    ins = pe.matmul(ps[:, sl], wbuf[i % 2][:, k * 128:(k + 1) * 128],
                                    b.h[k][:, sl], start=(k == 0), stop=(k == DC - 1))
            b.PE.inc(ins)
    on_pe(b, pe_body)

    def dve_body(dve):
        for i, (kind, j, ft) in enumerate(fts):
            dve.wait_ge(b.PE.h, pe_use[i])
            ps = b.psA if i % 2 == 0 else b.psB
            ins = dve.tensor_copy(outs[kind][j], ps[:, 0:2048])
            b.DV.inc(ins)
    on_dve(b, dve_body)
    assert b.DV.n == evac[-1]
    b.qkv_dv = evac[-1]
    a.free("wqkv0", "wqkv1")


def softmax_block(b, T, nk, prob_ap, qt):
    def mx(dve):
        dve.wait_ge(b.PE.h, b.sc_mark)
        if b.last_exp_ac:
            dve.wait_ge(b.AC.h, b.last_exp_ac)
        if nk < 2048 or b.mask_diag:
            dve.tensor_add(T[:, nk - 128:nk], T[:, nk - 128:nk], b.causal)
        ins = dve.reduce_max(b.nmax[0:128, 0:1], T[:, 0:nk], axis=AX.X,
                             negate=True)
        b.DV.inc(ins)
    on_dve(b, mx)
    mx_dv = b.DV.n

    def expf(act):
        act.wait_ge(b.DV.h, mx_dv)
        ins = act.activation(prob_ap[:, 0:nk], T[:, 0:nk], AF.Exp,
                             bias=b.nmax[0:128, 0:1], scale=1.0,
                             accum_out=b.den[0:128, 0:1])
        b.AC.inc(ins)
    on_act(b, expf)
    b.last_exp_ac = b.AC.n

    def rd(dve):
        dve.wait_ge(b.AC.h, b.last_exp_ac)
        if b.last_tr_pe:
            dve.wait_ge(b.PE.h, b.last_tr_pe)
        dve.reciprocal(b.den[0:128, 1:2], b.den[0:128, 0:1])
        dve.memset(b.junk[:, 0:384], 0.0)
        dve.tensor_scalar_mul(b.diag, b.ident, b.den[0:128, 1:2])
        ins = dve.memset(b.junk[:, 0:128], 0.0)
        b.DV.inc(ins)
    on_dve(b, rd)
    return b.DV.n


def attn_inner(b, qt, T, nch, score_fn, vtok_ap, out_fn):
    nk = nch * 128
    score_fn()
    prob_ap = b.prob[qt % 2]
    diag_dv = softmax_block(b, T, nk, prob_ap, qt)
    if getattr(b, "dbg_snap", False):
        b.dbg_snap = False

        def dbg_dve(dve):
            dve.wait_ge(b.DV.h, diag_dv)
            dve.tensor_copy(b.dbgP, prob_ap[:, 0:2048])
            dve.tensor_copy(b.dbgS, T[:, 0:2048])
            ins = dve.tensor_copy(b.dbgD, b.den[0:128, 0:2])
            b.DV.inc(ins)
        on_dve(b, dbg_dve)
        diag_dv = b.DV.n
        b.taps["dbg"] = [b.dbgS, b.dbgP]

    nrounds = (nch + 3) // 4
    tr_ev = [b.DV.n + r + 1 for r in range(nrounds)]
    pe_tr0 = b.PE.n

    def tr_pe(pe):
        pe.wait_ge(b.DV.h, diag_dv)
        for r in range(nrounds):
            if r >= 2:
                pe.wait_ge(b.DV.h, tr_ev[r - 2])
            w = min(4, nch - r * 4)
            for c in range(w):
                ch = r * 4 + c
                ins = pe.matmul(
                    T[:, (r % 2) * 512 + c * 128:(r % 2) * 512 + (c + 1) * 128],
                    prob_ap[:, ch * 128:(ch + 1) * 128],
                    b.diag, start=(c == 0), stop=(c == w - 1))
            b.PE.inc(ins)
    on_pe(b, tr_pe)
    tr_marks = [pe_tr0 + r + 1 for r in range(nrounds)]
    b.last_tr_pe = tr_marks[-1]

    def tr_dve(dve):
        for r in range(nrounds):
            dve.wait_ge(b.PE.h, tr_marks[r])
            w = min(4, nch - r * 4) * 128
            dst = b.pt[r % 2][:, (r // 2 % 2) * 512:(r // 2 % 2) * 512 + w]
            dve.tensor_copy(dst, T[:, (r % 2) * 512:(r % 2) * 512 + w])
            ins = dve.memset(b.junk[:, 0:128], 0.0)
            b.DV.inc(ins)
    on_dve(b, tr_dve)
    assert b.DV.n == tr_ev[-1]

    vecreg = T[0:64, 1664:1792]

    def vec_pe(pe):
        for r in range(nrounds):
            pe.wait_ge(b.DV.h, tr_ev[r])
            w = min(4, nch - r * 4)
            for c in range(w):
                ch = r * 4 + c
                src = b.pt[r % 2][:, (r // 2 % 2) * 512 + c * 128:
                                  (r // 2 % 2) * 512 + (c + 1) * 128]
                ins = pe.matmul(vecreg, vtok_ap[:, ch * 64:(ch + 1) * 64],
                                src, start=(ch == 0), stop=(ch == nch - 1))
        b.PE.inc(ins)
    on_pe(b, vec_pe)
    vec_mark = b.PE.n

    def vec_dve(dve):
        dve.wait_ge(b.PE.h, vec_mark)
        out_fn(dve, vecreg)
        ins = dve.memset(b.junk[:, 0:128], 0.0)
        b.DV.inc(ins)
    on_dve(b, vec_dve)
    return b.DV.n


def attention_half(b, l, half):
    a = b.abf
    if (DEBUG_TAP == "dbg" and l == 0 and half == 0
            and not hasattr(b, "dbgP")):
        b.dbgP = a.alloc("dbgP", S)
        b.dbgS = b.sc1
        b.dbgD = b.af.alloc("dbgD", 2)
        b.dbg_snap = True
    qstack = a.alloc("qstack", S)
    kstack = a.alloc("kstack", S)
    vtok = a.alloc("vtok", 16 * DH)
    b.prob = [a.alloc(f"prob{i}", S) for i in range(2)]
    b.pt = [a.alloc(f"pt{i}", 1024) for i in range(2)]
    b.diag = a.alloc("diag", 128)
    b.nmax = b.af.alloc("nmax", 2)
    b.mask_diag = False
    b.last_exp_ac = 0
    b.last_tr_pe = 0
    if not hasattr(b, "head_done_dv"):
        b.head_done_dv = 0
    b.last_scores_pe = 0

    for hh in range(8):
        head = half * 8 + hh
        j = hh // 2
        r0 = (hh % 2) * 64

        def build_dve(dve, j=j, r0=r0, head=head, hh=hh):
            if hh == 0:
                dve.wait_ge(b.DV.h, max(b.qkv_dv, b.kr_dv))
            if b.last_scores_pe:
                dve.wait_ge(b.PE.h, b.last_scores_pe)
            dve.tensor_scalar(qstack[0:64, :], b.qh[j][r0:r0 + 64, :],
                              b.uv[0:64, head:head + 1], 0.125,
                              ALU.add, ALU.mult)
            dve.tensor_scalar(qstack[64:128, :], b.qh[j][r0:r0 + 64, :],
                              b.uv[0:64, H + head:H + head + 1], 0.125,
                              ALU.add, ALU.mult)
            dve.tensor_copy(kstack[0:64, :], b.kh[j][r0:r0 + 64, :])
            dve.tensor_copy(kstack[64:128, :], b.krh[j][r0:r0 + 64, :])
            ins = dve.memset(b.junk[:, 0:128], 0.0)
            b.DV.inc(ins)

        on_dve(b, build_dve)
        stacks_dv = b.DV.n

        pe0 = b.PE.n

        def vt_pe(pe, j=j, r0=r0, hh=hh):
            if b.head_done_dv:
                pe.wait_ge(b.DV.h, b.head_done_dv)
            pe.wait_ge(b.DV.h, stacks_dv)
            for c in range(16):
                col = (c % 8) * 64 + (c // 8) * 512
                ins = pe.matmul(b.psB[0:128, col:col + 64],
                                b.vh[j][r0:r0 + 64, c * 128:(c + 1) * 128],
                                b.ident64[r0:r0 + 64, :], start=True, stop=True)
                if c % 8 == 7:
                    b.PE.inc(ins)
        on_pe(b, vt_pe)
        vt_marks = [pe0 + 1, pe0 + 2]

        def vt_dve(dve):
            for r in range(2):
                dve.wait_ge(b.PE.h, vt_marks[r])
                dve.tensor_copy(vtok[:, r * 512:(r + 1) * 512],
                                b.psB[:, r * 512:(r + 1) * 512])
                ins = dve.memset(b.junk[:, 0:128], 0.0)
                b.DV.inc(ins)
        on_dve(b, vt_dve)
        vtok_dv = b.DV.n

        qt_done = {}
        for qt in range(16):
            T = b.psA if qt % 2 == 0 else b.psB

            def score_fn(qt=qt, T=T, vtok_dv=vtok_dv, qt_done=qt_done):
                def sc_pe(pe):
                    if qt < 2:
                        pe.wait_ge(b.DV.h, vtok_dv)
                    else:
                        pe.wait_ge(b.DV.h, qt_done[qt - 2])
                    for nc_ in range(4):
                        sl = slice(nc_ * 512, (nc_ + 1) * 512)
                        ins = pe.matmul(T[:, sl],
                                        qstack[:, qt * 128:(qt + 1) * 128],
                                        kstack[:, sl], start=True, stop=True)
                    b.PE.inc(ins)
                on_pe(b, sc_pe)
                b.sc_mark = b.PE.n
                b.last_scores_pe = b.PE.n

            def out_fn(dve, vecreg, head=head, qt=qt):
                dst = b.vecT[head // 2][(head % 2) * 64:(head % 2) * 64 + 64,
                                       qt * 128:(qt + 1) * 128]
                return dve.tensor_copy(dst, vecreg)

            qt_done[qt] = attn_inner(b, qt, T, 16, score_fn, vtok, out_fn)
        b.head_done_dv = qt_done[15]

    a.free("qstack", "kstack", "vtok", "prob0", "prob1", "pt0", "pt1", "diag")
    b.af.free("nmax")
    a.free(*[f"qh{j}" for j in range(4)], *[f"kh{j}" for j in range(4)],
           *[f"vh{j}" for j in range(4)], *[f"kr{j}" for j in range(4)])
    b.attn_dv = b.head_done_dv


def wo_ln1(b, l):
    a = b.abf
    sn = snap(b)
    wbuf = [a.alloc(f"wwo{i}", DC * 128) for i in range(2)]

    dpar = DS(b, "dpar")
    b.par_sem = dpar

    def sp_par(sp):
        wait_snap(sp, b, sn, pe=False, dv=True)
        for nm, dst in (("ln1g", b.lng), ("ln1b", b.lnb),
                        ("ln2g", b.ln2g), ("ln2b", b.ln2b)):
            sp.dma_start(out=dst, in_=b.dI[nm][l]).then_inc(dpar.h, 16)
            dpar.n += 16
        sp.dma_start(out=b.bf1, in_=b.dI["bff1"][l]).then_inc(dpar.h, 16)
        sp.dma_start(out=b.bf2, in_=b.dI["bff2"][l]).then_inc(dpar.h, 16)
        dpar.n += 32
        b.par_ld = dpar.n
    on_sp(b, sp_par)

    wld = []
    pe_use = [b.PE.n + i + 1 for i in range(DC)]
    dwo = [DS(b, "dwwo0"), DS(b, "dwwo1")]

    def pool_body(po):
        wait_snap(po, b, sn, pe=True, dv=True)
        for i in range(DC):
            if i >= 2:
                po.wait_ge(b.PE.h, pe_use[i - 2])
            s_ = dwo[i % 2]
            po.dma_start(out=wbuf[i % 2], in_=b.dI["wo"][l, i]).then_inc(s_.h, 16)
            s_.n += 16
            wld.append((s_, s_.n))
    on_pool(b, pool_body)

    add_dv = [b.DV.n + i + 1 for i in range(DC)]

    def pe_body(pe):
        pe.wait_ge(b.DV.h, b.attn_dv)
        for i in range(DC):
            pe.wait_ge(wld[i][0].h, wld[i][1])
            if i >= 2:
                pe.wait_ge(b.DV.h, add_dv[i - 2])
            ps = b.psA if i % 2 == 0 else b.psB
            for k in range(DC):
                for nc_ in range(4):
                    sl = slice(nc_ * 512, (nc_ + 1) * 512)
                    ins = pe.matmul(ps[:, sl], wbuf[i % 2][:, k * 128:(k + 1) * 128],
                                    b.vecT[k][:, sl], start=(k == 0),
                                    stop=(k == DC - 1))
            b.PE.inc(ins)
    on_pe(b, pe_body)

    def dve_body(dve):
        for i in range(DC):
            dve.wait_ge(b.PE.h, pe_use[i])
            ps = b.psA if i % 2 == 0 else b.psB
            ins = dve.tensor_add(b.h[i], b.h[i], ps[:, 0:2048])
            b.DV.inc(ins)
    on_dve(b, dve_body)
    assert b.DV.n == add_dv[-1]
    a.free("wwo0", "wwo1")
    layer_norm_h(b, b.lng, b.lnb, add_dv, par_ld=b.par_ld)


def layer_norm_h(b, gamma, beta, ready_dv, par_ld=None):
    mb, rb, sq = norm_scratch(b)
    ac0 = [b.AC.n + k + 1 for k in range(DC)]
    pe_sq = [b.PE.n + 2 * (k + 1) for k in range(DC)]

    def act_body(act):
        for k in range(DC):
            act.wait_ge(b.DV.h, ready_dv[-1])
            if k > 0:
                act.wait_ge(b.PE.h, pe_sq[k - 1])
            ins = act.activation(sq, b.h[k], AF.Square)
            b.AC.inc(ins)
    on_act(b, act_body)

    def pe_body(pe):
        for k in range(DC):
            pe.wait_ge(b.DV.h, ready_dv[-1])
            for nc_ in range(4):
                sl = slice(nc_ * 512, (nc_ + 1) * 512)
                ins = pe.matmul(b.psA[0:1, sl], b.ones_col[:, 0:1], b.h[k][:, sl],
                                start=(k == 0), stop=(k == DC - 1))
            b.PE.inc(ins)
            pe.wait_ge(b.AC.h, ac0[k])
            for nc_ in range(4):
                sl = slice(nc_ * 512, (nc_ + 1) * 512)
                ins = pe.matmul(b.psB[0:1, sl], b.ones_col[:, 0:1], sq[:, sl],
                                start=(k == 0), stop=(k == DC - 1))
            b.PE.inc(ins)
    on_pe(b, pe_body)
    stats_pe = b.PE.n

    mean_var_rows(b, stats_pe, 1.0 / 1024.0, 1e-5, with_mean=True)
    bcast_rows(b, mb, rb, with_mean=True)
    bc_dv = b.bc_dv

    out_dv = []

    def dve_norm(dve):
        dve.wait_ge(b.DV.h, bc_dv)
        if par_ld:
            dve.wait_ge(b.par_sem.h, par_ld)
        for k in range(DC):
            dve.tensor_sub(b.sc1, b.h[k], mb)
            dve.tensor_mul(b.sc1, b.sc1, rb)
            ins = dve.tensor_scalar(b.h[k], b.sc1, gamma[:, k:k + 1],
                                    beta[:, k:k + 1], ALU.mult, ALU.add)
            out_dv.append(b.DV.inc(ins, 3))
    on_dve(b, dve_norm)
    b.ln_dv = list(out_dv)
    free_norm_scratch(b)


def ff_block(b, l):
    a = b.abf
    sn = snap(b)
    w1 = [a.alloc(f"wf1_{k}", DF) for k in range(DC)]
    w2 = [a.alloc(f"wf2_{d}", 32 * 128) for d in range(DC)]
    ring = a.alloc("ring", 32 * 256)
    w1ld, w2ld = [], []

    dwf1 = DS(b, "dwf1")
    dwf2 = DS(b, "dwf2")

    def pool_body(po):
        wait_snap(po, b, sn, pe=True, dv=True, ac=True)
        for k in range(DC):
            po.dma_start(out=w1[k], in_=b.dI["wff1"][l, k]).then_inc(dwf1.h, 16)
            dwf1.n += 16
            w1ld.append(dwf1.n)
        for d in range(DC):
            po.dma_start(out=w2[d], in_=b.dI["wff2"][l, d]).then_inc(dwf2.h, 16)
            dwf2.n += 16
            w2ld.append(dwf2.n)
    on_pool(b, pool_body)

    NT = S // 256
    relu_ac = {}
    c = b.AC.n
    for t in range(NT):
        for kf in range(32):
            c += 1
            relu_ac[(t, kf)] = c
    ff1_pe = {}
    yA_pe, yB_pe = [], []
    c = b.PE.n
    for t in range(NT):
        for kf in range(32):
            c += 1
            ff1_pe[(t, kf)] = c
        c += 1
        yA_pe.append(c)
        c += 1
        yB_pe.append(c)
    evA_dv = [b.DV.n + 2 * t + 1 for t in range(NT)]
    evB_dv = [b.DV.n + 2 * t + 2 for t in range(NT)]

    def pe_body(pe):
        pe.wait_ge(dwf1.h, w1ld[-1])
        pe.wait_ge(b.DV.h, b.ln_dv[-1])
        for t in range(NT):
            tsl = slice(t * 256, (t + 1) * 256)
            if t == 0:
                pe.wait_ge(dwf2.h, w2ld[-1])
            for kf in range(33):
                if kf < 32:
                    fof = (kf % 2) * 512
                    if kf >= 2:
                        pe.wait_ge(b.AC.h, relu_ac[(t, kf - 2)])
                    for k in range(DC):
                        ins = pe.matmul(b.psA[:, fof:fof + 256],
                                        w1[k][:, kf * 128:(kf + 1) * 128],
                                        b.h[k][:, tsl], start=(k == 0),
                                        stop=(k == DC - 1))
                    b.PE.inc(ins)
                if kf == 0:
                    continue
                kc = kf - 1
                if kc == 0 and t >= 1:
                    pe.wait_ge(b.DV.h, evB_dv[t - 1])
                pe.wait_ge(b.AC.h, relu_ac[(t, kc)])
                for dd in range(4):
                    ins = pe.matmul(b.psB[:, dd * 512:dd * 512 + 256],
                                    w2[dd][:, kc * 128:(kc + 1) * 128],
                                    ring[:, kc * 256:(kc + 1) * 256],
                                    start=(kc == 0), stop=(kc == 31))
            b.PE.inc(ins)
            pe.wait_ge(b.DV.h, evA_dv[t])
            for kf in range(32):
                for dd in range(4, DC):
                    ins = pe.matmul(b.psB[:, (dd - 4) * 512:(dd - 4) * 512 + 256],
                                    w2[dd][:, kf * 128:(kf + 1) * 128],
                                    ring[:, kf * 256:(kf + 1) * 256],
                                    start=(kf == 0), stop=(kf == 31))
            b.PE.inc(ins)
    on_pe(b, pe_body)

    def act_body(act):
        act.wait_ge(b.par_sem.h, b.par_ld)
        for t in range(NT):
            for kf in range(32):
                act.wait_ge(b.PE.h, ff1_pe[(t, kf)])
                fof = (kf % 2) * 512
                ins = act.activation(ring[:, kf * 256:(kf + 1) * 256],
                                     b.psA[:, fof:fof + 256],
                                     AF.Relu, bias=b.bf1[:, kf:kf + 1], scale=1.0)
                b.AC.inc(ins)
                assert b.AC.n == relu_ac[(t, kf)]
    on_act(b, act_body)

    def dve_body(dve):
        dve.wait_ge(b.par_sem.h, b.par_ld)
        for t in range(NT):
            tsl = slice(t * 256, (t + 1) * 256)
            dve.wait_ge(b.PE.h, yA_pe[t])
            for dd in range(4):
                ins = dve.tensor_scalar(b.h[dd][:, tsl],
                                        b.psB[:, dd * 512:dd * 512 + 256],
                                        b.bf2[:, dd:dd + 1], None, ALU.add)
            b.DV.inc(ins)
            assert b.DV.n == evA_dv[t]
            dve.wait_ge(b.PE.h, yB_pe[t])
            for dd in range(4, DC):
                ins = dve.tensor_scalar(b.h[dd][:, tsl],
                                        b.psB[:, (dd - 4) * 512:(dd - 4) * 512 + 256],
                                        b.bf2[:, dd:dd + 1], None, ALU.add)
            b.DV.inc(ins)
            assert b.DV.n == evB_dv[t]
    on_dve(b, dve_body)

    a.free(*[f"wf1_{k}" for k in range(DC)], *[f"wf2_{d}" for d in range(DC)],
           "ring")
    layer_norm_h(b, b.ln2g, b.ln2b, [evB_dv[-1]], par_ld=b.par_ld)


def spill_s(b):
    dss = DS(b, "dsspill")

    def sp_body(sp):
        for k in range(DC):
            sp.wait_ge(b.DV.h, b.ln_dv[k]) if False else sp.wait_ge(b.DV.h, b.ln_dv[-1])
            sp.dma_start(out=b.d_s[k * 128:(k + 1) * 128, :], in_=b.h[k]
                         ).then_inc(dss.h, 16)
            dss.n += 16
        b.s_spilled = dss.n
    on_sp(b, sp_body)
    b.taps["s"] = list(b.h)



def ffn_module(b):
    a = b.abf
    b.xn = b.h

    dxr = DS(b, "dxnr")
    b.xnr_sem = dxr

    def sp_body(sp):
        sp.wait_ge(DS(b, "dspill").h, b.xn_spilled)
        sp.wait_ge(DS(b, "dsspill").h, b.s_spilled)
        for k in range(DC):
            sp.dma_start(out=b.xn[k], in_=b.d_xn[k * 128:(k + 1) * 128, :]
                         ).then_inc(dxr.h, 16)
            dxr.n += 16
        b.xn_ld = dxr.n
    on_sp(b, sp_body)

    mb, rb, sq = norm_scratch(b)
    ffn_rms(b, src=b.xn, ready_sd=b.xn_ld, eps=1e-6, wname="fln1", rb=rb, sq=sq)
    b.h1 = [a.alloc(f"h1_{k}", S) for k in range(DC)]
    h1_dv = []

    def dve_h1(dve):
        dve.wait_ge(b.xnr_sem.h, b.xn_ld)
        dve.wait_ge(DS(b, "dfwn").h, b.fwn_ld)
        for k in range(DC):
            ins = dve.scalar_tensor_tensor(b.h1[k], b.xn[k],
                                           b.fw_norm[:, k:k + 1], rb,
                                           ALU.mult, ALU.mult)
            h1_dv.append(b.DV.inc(ins))
    on_dve(b, dve_h1)
    b.h1_dv = list(h1_dv)
    free_norm_scratch(b)

    ffn_qkv(b)
    if BUILD_PHASES < 7.6:
        return
    ffn_rope_qknorm(b)
    if BUILD_PHASES < 8:
        return
    ffn_attention(b)
    if BUILD_PHASES < 8.6:
        return
    ffn_wo_xf(b)
    if BUILD_PHASES < 9:
        return
    mb, rb, sq = norm_scratch(b)
    ffn_rms(b, src=b.xn, ready_dv=b.xf_dv, eps=1e-6, wname="fln2", rb=rb, sq=sq)
    b.h2 = [a.alloc(f"h2_{k}", S) for k in range(DC)]
    h2_dv = []

    def dve_h2(dve):
        dve.wait_ge(DS(b, "dfwn").h, b.fwn_ld)
        for k in range(DC):
            ins = dve.scalar_tensor_tensor(b.h2[k], b.xn[k],
                                           b.fw_norm[:, k:k + 1], rb,
                                           ALU.mult, ALU.mult)
            h2_dv.append(b.DV.inc(ins))
    on_dve(b, dve_h2)
    b.h2_dv = list(h2_dv)
    free_norm_scratch(b)

    ffn_logits_mask(b)
    if BUILD_PHASES < 10:
        return
    ffn_moe(b)


def ffn_rms(b, src, eps, wname, rb, sq, ready_sd=None, ready_dv=None):
    sn = snap(b)

    dfwn = DS(b, "dfwn")

    def sp_par(sp):
        wait_snap(sp, b, sn, pe=False, dv=True)
        sp.dma_start(out=b.fw_norm, in_=b.dI[wname]).then_inc(dfwn.h, 16)
        dfwn.n += 16
        b.fwn_ld = dfwn.n
    on_sp(b, sp_par)

    ac0 = [b.AC.n + k + 1 for k in range(DC)]
    pe_sq = [b.PE.n + k + 1 for k in range(DC)]

    def act_body(act):
        if ready_sd:
            act.wait_ge(b.xnr_sem.h, ready_sd)
        if ready_dv:
            act.wait_ge(b.DV.h, ready_dv)
        for k in range(DC):
            if k > 0:
                act.wait_ge(b.PE.h, pe_sq[k - 1])
            ins = act.activation(sq, src[k], AF.Square)
            b.AC.inc(ins)
    on_act(b, act_body)

    def pe_body(pe):
        wait_snap(pe, b, sn, pe=False, dv=True)
        for k in range(DC):
            pe.wait_ge(b.AC.h, ac0[k])
            for nc_ in range(4):
                sl = slice(nc_ * 512, (nc_ + 1) * 512)
                ins = pe.matmul(b.psB[0:1, sl], b.ones_col[:, 0:1], sq[:, sl],
                                start=(k == 0), stop=(k == DC - 1))
            b.PE.inc(ins)
    on_pe(b, pe_body)
    stats_pe = b.PE.n

    mean_var_rows(b, stats_pe, 1.0 / 1024.0, eps, with_mean=False)
    bcast_rows(b, None, rb, with_mean=False)



def ffn_qkv(b):
    a = b.abf
    sn = snap(b)
    b.fq = [a.alloc(f"fq{j}", S) for j in range(DC)]
    b.fk = [a.alloc(f"fk{j}", S) for j in range(2)]
    b.fv = [a.alloc(f"fv{j}", S) for j in range(2)]
    wbuf = [a.alloc(f"wfq{i}", DC * 128) for i in range(2)]
    b.fbq = b.af.alloc("fbq", DC)
    b.fbk = b.af.alloc("fbk", 2)
    b.fbv = b.af.alloc("fbv", 2)

    dfb = DS(b, "dfbias")

    def sp_par(sp):
        wait_snap(sp, b, sn, pe=False, dv=True)
        for nm, dst in (("fbq", b.fbq), ("fbk", b.fbk), ("fbv", b.fbv)):
            sp.dma_start(out=dst, in_=b.dI[nm]).then_inc(dfb.h, 16)
            dfb.n += 16
        b.fbias_ld = dfb.n
    on_sp(b, sp_par)

    jobs = ([("fwq", j, b.fq, j, b.fbq) for j in range(DC)]
            + [("fwk", j, b.fk, j, b.fbk) for j in range(2)]
            + [("fwv", j, b.fv, j, b.fbv) for j in range(2)])
    wld = []
    pe_use = [b.PE.n + i + 1 for i in range(len(jobs))]
    dfq = [DS(b, "dwfq0"), DS(b, "dwfq1")]

    def pool_body(po):
        wait_snap(po, b, sn, pe=True, dv=True)
        for i, (wn, ft, _, _, _) in enumerate(jobs):
            if i >= 2:
                po.wait_ge(b.PE.h, pe_use[i - 2])
            s_ = dfq[i % 2]
            po.dma_start(out=wbuf[i % 2], in_=b.dI[wn][ft]).then_inc(s_.h, 16)
            s_.n += 16
            wld.append((s_, s_.n))
    on_pool(b, pool_body)

    evac = [b.DV.n + i + 1 for i in range(len(jobs))]

    def pe_body(pe):
        pe.wait_ge(b.DV.h, b.h1_dv[-1])
        for i, (wn, ft, outl, j, bias) in enumerate(jobs):
            pe.wait_ge(wld[i][0].h, wld[i][1])
            if i >= 2:
                pe.wait_ge(b.DV.h, evac[i - 2])
            ps = b.psA if i % 2 == 0 else b.psB
            for k in range(DC):
                for nc_ in range(4):
                    sl = slice(nc_ * 512, (nc_ + 1) * 512)
                    ins = pe.matmul(ps[:, sl], wbuf[i % 2][:, k * 128:(k + 1) * 128],
                                    b.h1[k][:, sl], start=(k == 0),
                                    stop=(k == DC - 1))
            b.PE.inc(ins)
    on_pe(b, pe_body)

    def dve_body(dve):
        dve.wait_ge(dfb.h, b.fbias_ld)
        for i, (wn, ft, outl, j, bias) in enumerate(jobs):
            dve.wait_ge(b.PE.h, pe_use[i])
            ps = b.psA if i % 2 == 0 else b.psB
            ins = dve.tensor_scalar_add(outl[j], ps[:, 0:2048], bias[:, j:j + 1])
            b.DV.inc(ins)
    on_dve(b, dve_body)
    assert b.DV.n == evac[-1]
    b.fqkv_dv = evac[-1]
    b.taps["fq0"] = list(b.fq)
    b.taps["h1"] = list(b.h1)
    a.free("wfq0", "wfq1", *[f"h1_{k}" for k in range(DC)])


def ffn_rope_qknorm(b):
    a = b.abf
    sn = snap(b)
    b.a2q = b.af.alloc("a2q", 128)
    b.a2k = b.af.alloc("a2k", 128)
    b.ehalf = a.alloc("ehalf", 2)
    ropeC = a.alloc("ropeC", S)
    ropeS2 = a.alloc("ropeS2", S)
    t1 = a.alloc("rope_t1", S)
    sq = a.alloc("sq", S)

    def pool_a2(po):
        wait_snap(po, b, sn, pe=True, dv=True)
        po.memset(b.a2q[0:2, :], 0.0)
        po.memset(b.a2k[0:2, :], 0.0)
        po.memset(b.ehalf[:, :], 0.0)
        po.memset(b.ehalf[0:64, 0:1], 1.0)
        ins = po.memset(b.ehalf[64:128, 1:2], 1.0)
        b.PO.inc(ins)
        da2 = DS(b, "da2")
        po.dma_start(out=b.a2q[0:1, 0:64], in_=b.dI["fqn"]).then_inc(da2.h, 16)
        po.dma_start(out=b.a2q[1:2, 64:128], in_=b.dI["fqn"]).then_inc(da2.h, 16)
        po.dma_start(out=b.a2k[0:1, 0:64], in_=b.dI["fkn"]).then_inc(da2.h, 16)
        po.dma_start(out=b.a2k[1:2, 64:128], in_=b.dI["fkn"]).then_inc(da2.h, 16)
        da2.n += 64
        b.a2_ld = da2.n
        b.eh_po = b.PO.n
    on_pool(b, pool_a2)

    drp = DS(b, "drope")

    def sp_rope(sp):
        wait_snap(sp, b, sn, pe=True, dv=True)
        sp.dma_start(out=ropeC, in_=b.dI["c_ropeC"]).then_inc(drp.h, 16)
        sp.dma_start(out=ropeS2, in_=b.dI["c_ropeS2"]).then_inc(drp.h, 16)
        drp.n += 32
        b.rope_ld = drp.n
    on_sp(b, sp_rope)

    tiles = [(b.fq[j], b.a2q) for j in range(DC)] + \
            [(b.fk[j], b.a2k) for j in range(2)]
    prev_dv = b.fqkv_dv
    for ti, (src, a2) in enumerate(tiles):
        is_q = ti < DC
        def pe_sw(pe, src=src, ti=ti, prev_dv=prev_dv):
            pe.wait_ge(b.DV.h, prev_dv)
            if ti == 0:
                pe.wait_ge(DS(b, "drope").h, b.rope_ld)
            for nc_ in range(4):
                sl = slice(nc_ * 512, (nc_ + 1) * 512)
                ins = pe.matmul(b.psA[:, sl], b.swap32, src[:, sl],
                                start=True, stop=True)
            b.PE.inc(ins)
        on_pe(b, pe_sw)
        sw_pe = b.PE.n

        def dve_rope(dve, src=src):
            dve.wait_ge(b.PE.h, sw_pe)
            dve.tensor_mul(t1, src, ropeC)
            dve.tensor_mul(b.sc1, b.psA[:, 0:2048], ropeS2)
            ins = dve.tensor_add(b.sc1, b.sc1, t1)
            b.DV.inc(ins)
        on_dve(b, dve_rope)
        roped_dv = b.DV.n

        def act_sq(act):
            act.wait_ge(b.DV.h, roped_dv)
            ins = act.activation(sq, b.sc1, AF.Square)
            b.AC.inc(ins)
        on_act(b, act_sq)
        sq_ac = b.AC.n

        def pe_ms(pe, ti=ti):
            pe.wait_ge(b.AC.h, sq_ac)
            if ti == 0:
                pe.wait_ge(b.PO.h, b.eh_po)
                pe.wait_ge(DS(b, "da2").h, b.a2_ld)
            for nc_ in range(4):
                sl = slice(nc_ * 512, (nc_ + 1) * 512)
                ins = pe.matmul(b.psB[0:2, sl], b.ehalf, sq[:, sl],
                                start=True, stop=True)
            b.PE.inc(ins)
        on_pe(b, pe_ms)
        ms_pe = b.PE.n

        def dve_ms(dve):
            dve.wait_ge(b.PE.h, ms_pe)
            dve.tensor_scalar_mul(b.rows[0:2, 0:S], b.psB[0:2, 0:2048], 1.0 / 64.0)
            ins = dve.tensor_scalar_add(b.rows[0:2, 0:S], b.rows[0:2, 0:S], 1e-6)
            b.DV.inc(ins)
        on_dve(b, dve_ms)
        ms_dv = b.DV.n

        def act_rt(act):
            act.wait_ge(b.DV.h, ms_dv)
            ins = act.activation(b.rows[0:2, 0:S], b.rows[0:2, 0:S], AF.Sqrt)
            b.AC.inc(ins)
        on_act(b, act_rt)
        rt_ac = b.AC.n

        def dve_rc(dve):
            dve.wait_ge(b.AC.h, rt_ac)
            ins = dve.reciprocal(b.rows[0:2, 0:S], b.rows[0:2, 0:S])
            b.DV.inc(ins)
        on_dve(b, dve_rc)
        rc_dv = b.DV.n

        def pe_bc(pe, a2=a2):
            pe.wait_ge(b.DV.h, rc_dv)
            for nc_ in range(4):
                sl = slice(nc_ * 512, (nc_ + 1) * 512)
                ins = pe.matmul(b.psB[:, sl], a2[0:2, :], b.rows[0:2, sl],
                                start=True, stop=True)
            b.PE.inc(ins)
        on_pe(b, pe_bc)
        bc_pe = b.PE.n

        def dve_fin(dve, src=src, is_q=is_q):
            dve.wait_ge(b.PE.h, bc_pe)
            if is_q:
                ins = dve.scalar_tensor_tensor(src, b.sc1, 0.125,
                                               b.psB[:, 0:2048],
                                               ALU.mult, ALU.mult)
            else:
                ins = dve.tensor_mul(src, b.sc1, b.psB[:, 0:2048])
            b.DV.inc(ins)
        on_dve(b, dve_fin)
        prev_dv = b.DV.n
    b.rope_dv = prev_dv
    b.taps["fq"] = list(b.fq)
    if BUILD_PHASES >= 8:
        a.free("rope_t1", "ropeC", "ropeS2", "sq", "ehalf")
        b.af.free("a2q", "a2k")


def ffn_attention(b):
    a = b.abf
    sn = snap(b)
    b.vecT = [a.alloc(f"vecT{k}", S) for k in range(DC)]
    vtok = a.alloc("fvtok", NKV * 16 * DH)
    b.prob = [a.alloc(f"prob{i}", S) for i in range(2)]
    b.pt = [a.alloc(f"pt{i}", 1024) for i in range(2)]
    b.diag = a.alloc("diag", 128)
    oh = a.alloc("oh", S)
    sig = a.alloc("fsig", S)
    wg = a.alloc("wg", DH)
    b.nmax = b.af.alloc("nmax", 2)
    bg = b.af.alloc("bg", 1)
    b.mask_diag = True
    b.last_exp_ac = 0
    b.last_tr_pe = 0
    fkd = [a.alloc(f"fkd{g}", S) for g in range(NKV)]

    def dve_fkd(dve):
        dve.wait_ge(b.DV.h, b.rope_dv)
        for g in range(NKV):
            src_ = b.fk[g // 2][(g % 2) * 64:(g % 2) * 64 + 64, :]
            dve.tensor_copy(fkd[g][0:64, :], src_)
            dve.tensor_copy(fkd[g][64:128, :], src_)
        ins = dve.memset(b.junk[:, 0:128], 0.0)
        b.DV.inc(ins)
    on_dve(b, dve_fkd)
    b.fkd_dv = b.DV.n

    dwg = DS(b, "dwg")
    dbg_ = DS(b, "dbg")

    def pool_wg(po):
        wait_snap(po, b, sn, pe=True, dv=True)
        po.dma_start(out=wg[0:64, :], in_=b.dI["fwg"]).then_inc(dwg.h, 16)
        dwg.n += 16
        b.wg_ld = dwg.n
    on_pool(b, pool_wg)

    def sp_bg(sp):
        wait_snap(sp, b, sn, pe=False, dv=True)
        sp.dma_start(out=bg[0:64, :], in_=b.dI["fbg"]).then_inc(dbg_.h, 16)
        dbg_.n += 16
        b.bg_ld = dbg_.n
    on_sp(b, sp_bg)

    pe0 = b.PE.n
    vt_marks = [pe0 + i + 1 for i in range(NKV * 2)]
    vt_ev = [b.DV.n + i + 1 for i in range(NKV * 2)]

    def vt_pe(pe):
        pe.wait_ge(b.DV.h, b.rope_dv)
        for g in range(NKV):
            if g >= 1:
                pe.wait_ge(b.DV.h, vt_ev[g * 2 - 1])
            src = b.fv[g // 2]
            r0 = (g % 2) * 64
            for c in range(16):
                col = (c % 8) * 64 + (c // 8) * 512
                ins = pe.matmul(b.psB[0:128, col:col + 64],
                                src[r0:r0 + 64, c * 128:(c + 1) * 128],
                                b.ident64[r0:r0 + 64, :], start=True, stop=True)
                if c % 8 == 7:
                    b.PE.inc(ins)
    on_pe(b, vt_pe)

    def vt_dve(dve):
        for i in range(NKV * 2):
            g, r = i // 2, i % 2
            dve.wait_ge(b.PE.h, vt_marks[i])
            dst = vtok[:, g * 1024 + r * 512:g * 1024 + (r + 1) * 512]
            dve.tensor_copy(dst, b.psB[:, r * 512:(r + 1) * 512])
            ins = dve.memset(b.junk[:, 0:128], 0.0)
            assert b.DV.inc(ins) == vt_ev[i]
    on_dve(b, vt_dve)
    vtok_dv = b.DV.n
    prev_head_dv = vtok_dv

    for head in range(H):
        g = head // 4
        qj, qr = head // 2, (head % 2) * 64
        kj, kr_ = g // 2, (g % 2) * 64
        qt_done = {}
        for qt in range(16):
            nch = qt + 1
            T = b.psA if qt % 2 == 0 else b.psB

            def score_fn(qt=qt, T=T, nch=nch, qj=qj, qr=qr, g=g,
                         prev_head_dv=prev_head_dv, qt_done=qt_done):
                def sc_pe(pe):
                    if qt < 2:
                        pe.wait_ge(b.DV.h, prev_head_dv)
                        pe.wait_ge(b.DV.h, b.fkd_dv)
                    else:
                        pe.wait_ge(b.DV.h, qt_done[qt - 2])
                    nk = nch * 128
                    for nc_ in range((nk + 511) // 512):
                        sl = slice(nc_ * 512, min((nc_ + 1) * 512, nk))
                        ins = pe.matmul(T[:, sl],
                                        b.fq[qj][qr:qr + 64, qt * 128:(qt + 1) * 128],
                                        fkd[g][qr:qr + 64, sl],
                                        start=True, stop=True)
                    b.PE.inc(ins)
                on_pe(b, sc_pe)
                b.sc_mark = b.PE.n

            def out_fn(dve, vecreg, qt=qt):
                return dve.tensor_copy(oh[0:64, qt * 128:(qt + 1) * 128], vecreg)

            vt_ap = vtok[:, g * 1024:(g + 1) * 1024]
            qt_done[qt] = attn_inner(b, qt, T, nch, score_fn, vt_ap, out_fn)
        oh_dv = qt_done[15]

        for nc_ in range(4):
            sl = slice(nc_ * 512, (nc_ + 1) * 512)

            def pe_g(pe, nc_=nc_, sl=sl, head=head, oh_dv=oh_dv):
                if nc_ == 0:
                    pe.wait_ge(b.DV.h, oh_dv)
                    if head == 0:
                        pe.wait_ge(DS(b, "dwg").h, b.wg_ld)
                else:
                    pe.wait_ge(b.AC.h, b.gate_ac)
                ins = pe.matmul(b.psB[0:64, 1536:2048], wg[0:64, :],
                                oh[0:64, sl], start=True, stop=True)
                b.PE.inc(ins)
            on_pe(b, pe_g)
            gpe = b.PE.n

            def act_g(act, sl=sl, head=head, nc_=nc_):
                act.wait_ge(b.PE.h, gpe)
                if head == 0 and nc_ == 0:
                    act.wait_ge(DS(b, "dbg").h, b.bg_ld)
                ins = act.activation(sig[0:64, sl], b.psB[0:64, 1536:2048],
                                     AF.Sigmoid, bias=bg[0:64, :], scale=1.0)
                b.AC.inc(ins)
            on_act(b, act_g)
            b.gate_ac = b.AC.n

        def dve_gate(dve, head=head):
            dve.wait_ge(b.AC.h, b.gate_ac)
            dst = b.vecT[head // 2][(head % 2) * 64:(head % 2) * 64 + 64, :]
            ins = dve.tensor_mul(dst, oh[0:64, :], sig[0:64, :])
            b.DV.inc(ins)
        on_dve(b, dve_gate)
        prev_head_dv = b.DV.n

    b.fattn_dv = prev_head_dv
    b.taps["fvec"] = list(b.vecT)
    a.free("fvtok", "prob0", "prob1", "pt0", "pt1", "diag", "oh", "fsig", "wg",
           *[f"fkd{g}" for g in range(NKV)])
    b.af.free("nmax", "bg")
    a.free(*[f"fq{j}" for j in range(DC)], "fk0", "fk1", "fv0", "fv1")


def ffn_wo_xf(b):
    a = b.abf
    sn = snap(b)
    wbuf = [a.alloc(f"wfo{i}", DC * 128) for i in range(2)]
    bo = b.af.alloc("fbo", DC)

    dfbo = DS(b, "dfbo")

    def sp_par(sp):
        wait_snap(sp, b, sn, pe=False, dv=True)
        sp.dma_start(out=bo, in_=b.dI["fbo"]).then_inc(dfbo.h, 16)
        dfbo.n += 16
        b.fbo_ld = dfbo.n
    on_sp(b, sp_par)

    wld = []
    pe_use = [b.PE.n + i + 1 for i in range(DC)]
    dfo = [DS(b, "dwfo0"), DS(b, "dwfo1")]

    def pool_body(po):
        wait_snap(po, b, sn, pe=True, dv=True)
        for i in range(DC):
            if i >= 2:
                po.wait_ge(b.PE.h, pe_use[i - 2])
            s_ = dfo[i % 2]
            po.dma_start(out=wbuf[i % 2], in_=b.dI["fwo"][i]).then_inc(s_.h, 16)
            s_.n += 16
            wld.append((s_, s_.n))
    on_pool(b, pool_body)

    evac = [b.DV.n + i + 1 for i in range(DC)]

    def pe_body(pe):
        pe.wait_ge(b.DV.h, b.fattn_dv)
        for i in range(DC):
            pe.wait_ge(wld[i][0].h, wld[i][1])
            if i >= 2:
                pe.wait_ge(b.DV.h, evac[i - 2])
            ps = b.psA if i % 2 == 0 else b.psB
            for k in range(DC):
                for nc_ in range(4):
                    sl = slice(nc_ * 512, (nc_ + 1) * 512)
                    ins = pe.matmul(ps[:, sl], wbuf[i % 2][:, k * 128:(k + 1) * 128],
                                    b.vecT[k][:, sl], start=(k == 0),
                                    stop=(k == DC - 1))
            b.PE.inc(ins)
    on_pe(b, pe_body)

    def dve_body(dve):
        dve.wait_ge(dfbo.h, b.fbo_ld)
        for i in range(DC):
            dve.wait_ge(b.PE.h, pe_use[i])
            ps = b.psA if i % 2 == 0 else b.psB
            ins = dve.scalar_tensor_tensor(b.xn[i], ps[:, 0:2048],
                                           bo[:, i:i + 1], b.xn[i],
                                           ALU.add, ALU.add)
            b.DV.inc(ins)
    on_dve(b, dve_body)
    assert b.DV.n == evac[-1]
    b.xf_dv = evac[-1]
    a.free("wfo0", "wfo1", *[f"vecT{k}" for k in range(DC)])
    b.af.free("fbo")
    b.taps["xf"] = list(b.xn)


def ffn_logits_mask(b):
    a = b.abf
    sn = snap(b)
    b.mask = a.alloc("mask", S)
    wr = a.alloc("fwr", DC * 8)
    brr = a.alloc("fbr", 8)
    mtok = b.af.alloc("mtok", 8)
    mrow = b.af.alloc("mrow", 1)

    dfwr = DS(b, "dfwr")

    def pool_w(po):
        wait_snap(po, b, sn, pe=True, dv=True)
        for k in range(DC):
            po.dma_start(out=wr[:, k * 8:(k + 1) * 8], in_=b.dI["fwr"][k]
                         ).then_inc(dfwr.h, 16)
        po.dma_start(out=brr[0:1, :], in_=b.dI["fbr"]).then_inc(dfwr.h, 16)
        dfwr.n += 16 * 9
        b.fwr_ld = dfwr.n
    on_pool(b, pool_w)

    prev = 0
    for tt in range(16):
        tsl = slice(tt * 128, (tt + 1) * 128)

        def pe_lg(pe, tt=tt, tsl=tsl, prev=prev):
            if tt == 0:
                pe.wait_ge(DS(b, "dfwr").h, b.fwr_ld)
                pe.wait_ge(b.DV.h, b.h2_dv[-1])
            else:
                pe.wait_ge(b.DV.h, prev)
            for k in range(DC):
                pe.matmul(b.psA[:, 0:8], b.h2[k][:, tsl], wr[:, k * 8:(k + 1) * 8],
                          start=(k == 0), stop=False)
            ins = pe.matmul(b.psA[:, 0:8], b.ones_row[0:1, :], brr[0:1, :],
                            start=False, stop=True)
            b.PE.inc(ins)
        on_pe(b, pe_lg)
        lg_pe = b.PE.n

        def dve_mx(dve, tt=tt):
            dve.wait_ge(b.PE.h, lg_pe)
            dve.reduce_max(mrow[0:128, 0:1], b.psA[:, 0:8], axis=AX.X)
            dve.memset(b.junk[:, 0:384], 0.0)
            dve.tensor_scalar(mtok[:, 0:8], b.psA[:, 0:8],
                              mrow[0:128, 0:1], None, ALU.is_equal)
            ins = dve.memset(b.junk[:, 0:128], 0.0)
            b.DV.inc(ins)
        on_dve(b, dve_mx)
        mt_dv = b.DV.n

        def pe_tr(pe, tt=tt):
            pe.wait_ge(b.DV.h, mt_dv)
            ins = pe.transpose(b.psB[0:8, 1536:1664], mtok[:, 0:8], b.identf)
            b.PE.inc(ins)
        on_pe(b, pe_tr)
        tr_pe = b.PE.n

        def dve_ms(dve, tt=tt, tsl=tsl):
            dve.wait_ge(b.PE.h, tr_pe)
            ins = dve.tensor_copy(b.mask[0:8, tsl], b.psB[0:8, 1536:1664])
            b.DV.inc(ins)
        on_dve(b, dve_ms)
        prev = b.DV.n
    b.mask_dv = prev
    a.free("fwr", "fbr")
    b.af.free("mtok", "mrow")
    b.taps["mask"] = [b.mask]


def ffn_moe(b):
    a = b.abf
    sn = snap(b)
    TH = 1024
    act_s = a.alloc("act_s", 16 * TH)
    w2e = a.alloc("w2e", 16 * 1024)
    w1buf = [a.alloc(f"w1b{i}", DC * 128) for i in range(4)]
    e8t = a.alloc("e8t", 8 * 128)
    maskb = a.alloc("maskb", TH)
    y = [a.alloc(f"y{d}", TH) for d in range(DC)]
    sigs = a.alloc("sigs", 512)
    b2t = a.alloc("b2t", D)
    b1c = b.af.alloc("b1c", 32)
    xfin = b.sc1[:, 0:1024]
    gsc = b.sc1[:, 1024:2048]

    dmb = DS(b, "dmoeb")

    def pool_b2(po):
        wait_snap(po, b, sn, pe=True, dv=True)
        po.dma_start(out=b2t[0:8, :], in_=b.dI["fb2"]).then_inc(dmb.h, 16)
        po.dma_start(out=e8t[0:8, :], in_=b.dI["c_e8"]).then_inc(dmb.h, 16)
        dmb.n += 32
        b.b2_ld = dmb.n
    on_pool(b, pool_b2)

    b.out_sd = 0
    b.yadd_dv = 0
    b.w2_use = 0
    for th in range(2):
        toff = th * TH

        def po_zero(po, th=th):
            if th == 1:
                po.wait_ge(DS(b, "dout").h, b.out_sd)
            else:
                wait_snap(po, b, snap(b), pe=True, dv=True)
            for d in range(DC):
                ins = po.memset(y[d][:, :], 0.0)
            b.PO.inc(ins)
        on_pool(b, po_zero)
        yz_po = b.PO.n

        for e in range(NE):
            mb_dv0 = b.DV.n
            for nch in range(2):
                sl = slice(toff + nch * 512, toff + (nch + 1) * 512)

                def pe_mb(pe, e=e, th=th, nch=nch, sl=sl):
                    if e == 0 and th == 0 and nch == 0:
                        pe.wait_ge(b.DV.h, b.mask_dv)
                        pe.wait_ge(DS(b, "dmoeb").h, b.b2_ld)
                    if b.yadd_dv:
                        pe.wait_ge(b.DV.h, b.yadd_dv)
                    if nch == 1:
                        pe.wait_ge(b.DV.h, mb_dv0 + 1)
                    ins = pe.matmul(b.psB[:, 1536:2048],
                                    e8t[0:8, e * 128:(e + 1) * 128],
                                    b.mask[0:8, sl], start=True, stop=True)
                    b.PE.inc(ins)
                on_pe(b, pe_mb)
                mb_pe = b.PE.n

                def dve_mb(dve, nch=nch):
                    dve.wait_ge(b.PE.h, mb_pe)
                    ins = dve.tensor_copy(maskb[:, nch * 512:(nch + 1) * 512],
                                          b.psB[:, 1536:2048])
                    b.DV.inc(ins)
                on_dve(b, dve_mb)
            mb_dv = b.DV.n

            db1 = DS(b, "db1c")

            def sp_b1(sp, e=e):
                sp.wait_ge(b.DV.h, b.DV.n)
                sp.dma_start(out=b1c, in_=b.dI["fb1"][e]).then_inc(db1.h, 16)
                db1.n += 16
                b.b1_ld = db1.n
            on_sp(b, sp_b1)

            dw2e = DS(b, "dw2e")

            def po_w2(po, e=e, th=th):
                if b.w2_use:
                    po.wait_ge(b.PE.h, b.w2_use)
                po.dma_start(
                    out=w2e.rearrange("p (kf f) -> p kf f", kf=16),
                    in_=b.dI["fw2"][e].rearrange("(kf p) f -> p kf f", p=128)
                ).then_inc(dw2e.h, 16)
                dw2e.n += 16
                b.w2_ld = dw2e.n
            on_pool(b, po_w2)

            peA0 = b.PE.n
            peA_marks = [peA0 + i + 1 for i in range(32)]
            w1_use = [peA0 + 2 * (j + 1) for j in range(16)]
            w1ld = []
            acA0 = b.AC.n
            po_mul = {}

            def po_w1(po, e=e):
                for j in range(16):
                    if j >= 2:
                        po.wait_ge(b.PE.h, w1_use[j - 2])
                    s_ = DS(b, f"dw1s{j % 2}")
                    po.dma_start(out=w1buf[(j % 2) * 2], in_=b.dI["fw1"][e, j]
                                 ).then_inc(s_.h, 16)
                    po.dma_start(out=w1buf[(j % 2) * 2 + 1],
                                 in_=b.dI["fw1"][e, j + 16]).then_inc(s_.h, 16)
                    s_.n += 32
                    w1ld.append((s_, s_.n))
                    if j >= 1:
                        for tc in range(2):
                            i = (j - 1) * 2 + tc
                            po.wait_ge(b.AC.h, acA0 + i + 1)
                            ins = po.tensor_mul(sigs[:, 0:512], sigs[:, 0:512],
                                                maskb[:, tc * 512:(tc + 1) * 512])
                            po_mul[(j - 1, tc)] = b.PO.inc(ins)
                for tc in range(2):
                    i = 15 * 2 + tc
                    po.wait_ge(b.AC.h, acA0 + i + 1)
                    ins = po.tensor_mul(sigs[:, 0:512], sigs[:, 0:512],
                                        maskb[:, tc * 512:(tc + 1) * 512])
                    po_mul[(15, tc)] = b.PO.inc(ins)
            on_pool(b, po_w1)

            dv0 = b.DV.n
            actm = {}
            cnt = 0
            for j in range(16):
                for tc in range(2):
                    cnt += 1
                    actm[(j, tc)] = dv0 + cnt

            def peA(pe, e=e, th=th, toff=toff, mb_dv=mb_dv):
                for j in range(16):
                    pe.wait_ge(w1ld[j][0].h, w1ld[j][1])
                    if j == 0:
                        pe.wait_ge(b.DV.h, mb_dv)
                        pe.wait_ge(b.PO.h, yz_po)
                    for tc in range(2):
                        tsl = slice(toff + tc * 512, toff + (tc + 1) * 512)
                        prevkey = (j, tc - 1) if tc >= 1 else (j - 1, 1)
                        if prevkey in actm:
                            pe.wait_ge(b.DV.h, actm[prevkey])
                        for k in range(DC):
                            pe.matmul(b.psA[:, 0:512],
                                      w1buf[(j % 2) * 2][:, k * 128:(k + 1) * 128],
                                      b.h2[k][:, tsl], start=(k == 0),
                                      stop=(k == DC - 1))
                            ins = pe.matmul(b.psA[:, 512:1024],
                                            w1buf[(j % 2) * 2 + 1][:, k * 128:(k + 1) * 128],
                                            b.h2[k][:, tsl], start=(k == 0),
                                            stop=(k == DC - 1))
                        b.PE.inc(ins)
            on_pe(b, peA)

            def actA(act, e=e):
                for i in range(32):
                    j, tc = i // 2, i % 2
                    act.wait_ge(b.PE.h, peA_marks[i])
                    if i == 0:
                        act.wait_ge(DS(b, "db1c").h, b.b1_ld)
                    ins = act.activation(sigs[:, 0:512], b.psA[:, 512:1024],
                                         AF.Sigmoid, bias=b1c[:, 16 + j:17 + j],
                                         scale=1.0)
                    b.AC.inc(ins)
            on_act(b, actA)

            def dveA(dve, e=e):
                for i in range(32):
                    j, tc = i // 2, i % 2
                    dve.wait_ge(b.PO.h, po_mul[(j, tc)])
                    dst = act_s[:, j * TH + tc * 512:j * TH + (tc + 1) * 512]
                    ins = dve.scalar_tensor_tensor(dst, b.psA[:, 0:512],
                                                   b1c[:, j:j + 1], sigs[:, 0:512],
                                                   ALU.add, ALU.mult)
                    b.DV.inc(ins)
                    assert b.DV.n == actm[(j, tc)]
            on_dve(b, dveA)
            actA_dv = b.DV.n

            yev = []

            yevd = [b.DV.n + 1, b.DV.n + 2]

            def peB(pe, e=e, th=th, toff=toff, actA_dv=actA_dv):
                pe.wait_ge(DS(b, "dw2e").h, b.w2_ld)
                pe.wait_ge(b.DV.h, actA_dv)
                if b.yadd_dv:
                    pe.wait_ge(b.DV.h, b.yadd_dv)
                for tc in range(2):
                    if tc == 1:
                        pe.wait_ge(b.DV.h, yevd[0])
                    tslm = slice(toff + tc * 512, toff + (tc + 1) * 512)
                    for d in range(DC):
                        dsl = slice((d % 4) * 512, (d % 4) * 512 + 512)
                        ps = b.psA if d < 4 else b.psB
                        for kf in range(16):
                            ins = pe.matmul(
                                ps[:, dsl],
                                w2e[:, kf * 1024 + d * 128:kf * 1024 + (d + 1) * 128],
                                act_s[:, kf * TH + tc * 512:kf * TH + (tc + 1) * 512],
                                start=(kf == 0), stop=(kf == 15 and e != NE - 1))
                        if e == NE - 1:
                            ins = pe.matmul(ps[:, dsl],
                                            b2t[0:8, d * 128:(d + 1) * 128],
                                            b.mask[0:8, tslm],
                                            start=False, stop=True)
                    b.PE.inc(ins)
                    yev.append(b.PE.n)
            on_pe(b, peB)
            b.w2_use = b.PE.n

            def dveB(dve, e=e):
                for tc in range(2):
                    dve.wait_ge(b.PE.h, yev[tc])
                    for d in range(DC):
                        dsl = slice((d % 4) * 512, (d % 4) * 512 + 512)
                        ps = b.psA if d < 4 else b.psB
                        ysl = y[d][:, tc * 512:(tc + 1) * 512]
                        ins = dve.tensor_add(ysl, ysl, ps[:, dsl])
                    assert b.DV.inc(ins) == yevd[tc]
            on_dve(b, dveB)
            b.yadd_dv = b.DV.n

        final_combine_th(b, th, y, xfin, gsc)

    a.free("act_s", "w2e", "maskb", "sigs", "b2t", "mask", "e8t",
           *[f"w1b{i}" for i in range(4)], *[f"y{d}" for d in range(DC)],
           *[f"h2_{k}" for k in range(DC)])
    b.af.free("b1c")


def final_combine_th(b, th, y, xfin, gsc):
    toff = th * 1024
    sbf = b.abf.alloc("sch", 1024)
    prev_out = 0
    for d in range(DC):
        tsl = slice(toff, toff + 1024)

        dfx = DS(b, "dfx")

        def sp_x(sp, d=d, th=th, tsl=tsl, prev_out=prev_out):
            sp.wait_ge(b.DV.h, b.yadd_dv)
            if prev_out:
                sp.wait_ge(b.DV.h, b.fin_dv)
            sp.dma_start(out=xfin, in_=b.dI["xT"][d * 128:(d + 1) * 128, tsl]
                         ).then_inc(dfx.h, 16)
            sp.dma_start(out=sbf, in_=b.d_s[d * 128:(d + 1) * 128, tsl]
                         ).then_inc(dfx.h, 16)
            dfx.n += 32
            b.x_ld = dfx.n
        on_sp(b, sp_x)

        def dve_f1(dve, d=d, tsl=tsl):
            dve.wait_ge(b.DV.h, b.yadd_dv)
            if b.out_sd:
                dve.wait_ge(DS(b, "dout").h, b.out_sd)
            ins = dve.tensor_add(gsc, b.xn[d][:, tsl], y[d][:, :])
            b.DV.inc(ins)
        on_dve(b, dve_f1)
        f1_dv = b.DV.n

        def act_f(act):
            act.wait_ge(b.DV.h, f1_dv)
            ins = act.activation(gsc, gsc, AF.Silu)
            b.AC.inc(ins)
        on_act(b, act_f)
        f_ac = b.AC.n

        def dve_f2(dve, d=d):
            dve.wait_ge(b.AC.h, f_ac)
            dve.wait_ge(DS(b, "dfx").h, b.x_ld)
            dve.tensor_mul(gsc, sbf, gsc)
            ins = dve.tensor_add(gsc, gsc, xfin)
            b.DV.inc(ins)
        on_dve(b, dve_f2)
        b.fin_dv = b.DV.n
        prev_out = 1

        dout = DS(b, "dout")

        def sp_out(sp, d=d, tsl=tsl):
            sp.wait_ge(b.DV.h, b.fin_dv)
            sp.dma_start(out=b.d_out[d * 128:(d + 1) * 128, tsl], in_=gsc
                         ).then_inc(dout.h, 16)
            dout.n += 16
            b.out_sd = dout.n
        on_sp(b, sp_out)
    b.abf.free("sch")


def finish(b, tap=None):
    tap = tap if tap is not None else DEBUG_TAP
    if tap is not None and BUILD_PHASES < 99:
        tiles = b.taps[tap]

        dtap = DS(b, "dtap")

        def po_tap(po):
            wait_snap(po, b, snap(b), pe=True, dv=True, ac=True)
            for k, t in enumerate(tiles[:8]):
                po.dma_start(out=b.d_out[k * 128:(k + 1) * 128, 0:t.shape[-1]],
                             in_=t).then_inc(dtap.h, 16)
                dtap.n += 16
        on_pool(b, po_tap)

    def sp_end(sp):
        for s_ in b.dsems.values():
            if s_.n:
                sp.wait_ge(s_.h, s_.n)
    on_sp(b, sp_end)



def prep_inputs(inputs):
    f32 = np.float32
    g = {k: np.asarray(v, f32) for k, v in inputs.items()}
    m = {}
    for k, v in host_constants().items():
        m["c_" + k] = np.ascontiguousarray(v)
    m["wqkv"] = np.stack([tile_kMf(g["L0_attn_W_qkv"][l], 24) for l in range(NL)])
    m["wr"] = np.stack([tile_kMf(g["L0_attn_W_r"][l], DC) for l in range(NL)])
    m["wo"] = np.stack([tile_kMf(g["L0_attn_W_o"][l], DC) for l in range(NL)])
    m["wff1"] = np.ascontiguousarray(g["L0_attn_W_ff1"].reshape(NL, DC, 128, DF))
    w2 = g["L0_attn_W_ff2"].reshape(NL, 32, 128, DC, 128)
    m["wff2"] = np.ascontiguousarray(w2.transpose(0, 3, 2, 1, 4))
    m["u_t"] = np.ascontiguousarray(g["L0_attn_u"].T)
    m["v_t"] = np.ascontiguousarray(g["L0_attn_v"].T)
    for nm, key in (("ln1g", "L0_attn_ln1_g"), ("ln1b", "L0_attn_ln1_b"),
                    ("ln2g", "L0_attn_ln2_g"), ("ln2b", "L0_attn_ln2_b"),
                    ("bff1", "L0_attn_b_ff1"), ("bff2", "L0_attn_b_ff2")):
        m[nm] = np.stack([col_bias(g[key][l]) for l in range(NL)])
    m["fwq"] = tile_kMf(g["L0_ffn_Wq"], DC)
    m["fwk"] = tile_kMf(g["L0_ffn_Wk"], 2)
    m["fwv"] = tile_kMf(g["L0_ffn_Wv"], 2)
    m["fwg"] = g["L0_ffn_Wg"]
    m["fwo"] = tile_kMf(g["L0_ffn_Wo"], DC)
    m["fwr"] = np.ascontiguousarray(g["L0_ffn_Wr"].reshape(DC, 128, NE))
    m["fbr"] = np.ascontiguousarray(g["L0_ffn_br"].reshape(1, NE))
    w1 = g["L0_ffn_W1"].reshape(NE, DC, 128, 32, 128)
    m["fw1"] = np.ascontiguousarray(w1.transpose(0, 3, 2, 1, 4))
    m["fb1"] = np.stack([col_bias(g["L0_ffn_b1"][e]) for e in range(NE)])
    m["fw2"] = np.ascontiguousarray(g["L0_ffn_W2"])
    m["fb2"] = np.ascontiguousarray(g["L0_ffn_b2"])
    m["fbq"] = col_bias(g["L0_ffn_bq"])
    m["fbk"] = col_bias(g["L0_ffn_bk"])
    m["fbv"] = col_bias(g["L0_ffn_bv"])
    m["fbg"] = np.ascontiguousarray(g["L0_ffn_bg"].reshape(DH, 1))
    m["fbo"] = col_bias(g["L0_ffn_bo"])
    m["fln1"] = col_bias(g["L0_ffn_ln1_w"])
    m["fln2"] = col_bias(g["L0_ffn_ln2_w"])
    m["fqn"] = np.ascontiguousarray(g["L0_ffn_qn_w"].reshape(1, DH))
    m["fkn"] = np.ascontiguousarray(g["L0_ffn_kn_w"].reshape(1, DH))
    return m


def kernel(**inputs):
    x = np.asarray(inputs["L0_attn_x"], np.float32)
    shared = prep_inputs(inputs)
    nc = build_program()
    in_maps = []
    for i in range(8):
        im = dict(shared)
        im["xT"] = np.ascontiguousarray(x[i].T)
        in_maps.append(im)
    res = run_bass_kernel_spmd(nc, in_maps, core_ids=list(range(8)))
    outs = [np.asarray(res.results[i]["out"]).T for i in range(8)]
    return np.stack(outs).astype(np.float32)


# version 04
# speedup vs baseline: 1.0068x; 1.0068x over previous
import contextlib

import numpy as np
import ml_dtypes

import concourse.bass as bass
import concourse.mybir as mybir
from concourse.bass_utils import run_bass_kernel_spmd

F32 = mybir.dt.float32
BF16 = mybir.dt.bfloat16
AF = mybir.ActivationFunctionType
ALU = mybir.AluOpType
AX = mybir.AxisListType
BF = ml_dtypes.bfloat16

S = 2048
D = 1024
DC = 8
NL = 4
H = 16
DH = 64
DF = 4096
HID = 2048
NE = 8
NKV = 4

BUILD_PHASES = 99
DEBUG_TAP = None



def host_constants():
    c = {}
    c["ident"] = np.eye(128, dtype=np.float32).astype(BF)
    c["identf"] = np.eye(128, dtype=np.float32)
    c["ident64"] = np.eye(64, dtype=np.float32).astype(BF)
    hd = DH
    inv = 1.0 / 10000.0 ** (np.arange(0, hd, 2, dtype=np.float32) / hd)
    t = np.arange(S, dtype=np.float32)
    fr = np.einsum("i,j->ij", t, inv)
    cos = np.cos(fr).T
    sin = np.sin(fr).T
    Ct = np.zeros((128, S), np.float32)
    S2 = np.zeros((128, S), np.float32)
    for base in (0, 64):
        Ct[base:base + 32] = cos
        Ct[base + 32:base + 64] = cos
        S2[base:base + 32] = -sin
        S2[base + 32:base + 64] = sin
    c["ropeC"] = Ct.astype(BF)
    c["ropeS2"] = S2.astype(BF)
    Pl = np.zeros((128, 128), np.float32)
    for i in range(128):
        Pl[i ^ 32, i] = 1.0
    c["swap32"] = Pl.astype(BF)
    m = np.zeros((128, 128), np.float32)
    for q in range(128):
        m[q, q + 1:] = -1e9
    c["causal"] = m
    E8 = np.zeros((8, 8 * 128), np.float32)
    for e in range(8):
        E8[e, e * 128:(e + 1) * 128] = 1.0
    c["e8"] = E8.astype(BF)
    c["ones_col"] = np.ones((128, 1), np.float32).astype(BF)
    c["ones_row"] = np.ones((1, 128), np.float32).astype(BF)
    c["onesrf"] = np.ones((1, 128), np.float32)
    pos = np.arange(S - 1, -1, -1, dtype=np.float32)
    freq = 1.0 / 10000.0 ** (np.arange(0, D, 2, dtype=np.float32) / D)
    ang = pos[:, None] * freq[None, :]
    r = np.concatenate([np.sin(ang), np.cos(ang)], axis=-1)
    c["rT"] = np.ascontiguousarray(r.T).reshape(DC, 128, S).astype(BF)
    return c


def tile_kMf(w, nf):
    din, f = w.shape
    kc = din // 128
    a = w.reshape(kc, 128, nf, 128)
    return np.ascontiguousarray(a.transpose(2, 1, 0, 3))


def col_bias(v):
    f = v.shape[0]
    return np.ascontiguousarray(v.reshape(f // 128, 128).T)



class Sem:
    def __init__(self, ctx, nc, name):
        self.h = ctx.enter_context(nc.semaphore(name))
        self.n = 0

    def inc(self, instr, k=1):
        instr.then_inc(self.h, k)
        self.n += k
        return self.n


class Arena:
    def __init__(self, t, cols, name):
        self.t = t
        self.cols = cols
        self.name = name
        self.iv = [(0, cols)]
        self.live = {}

    def alloc(self, name, cols, align=16):
        assert name not in self.live, name
        for i, (a, b_) in enumerate(self.iv):
            a2 = (a + align - 1) // align * align
            if b_ - a2 >= cols:
                del self.iv[i]
                if a2 > a:
                    self.iv.insert(i, (a, a2))
                    i += 1
                if a2 + cols < b_:
                    self.iv.insert(i, (a2 + cols, b_))
                self.live[name] = (a2, cols)
                return self.t[:, a2:a2 + cols]
        raise MemoryError(f"arena {self.name}: alloc {name} ({cols}) failed; "
                          f"live={self.live}")

    def free(self, *names):
        for name in names:
            a, n = self.live.pop(name)
            self.iv.append((a, a + n))
        self.iv.sort()
        out = []
        for s_, e in self.iv:
            if out and out[-1][1] >= s_:
                out[-1] = (out[-1][0], max(out[-1][1], e))
            else:
                out.append((s_, e))
        self.iv = [tuple(x) for x in out]


class B:
    pass


def declare_inputs(nc, consts):
    d = {}

    def inp(name, shape, dtype):
        d[name] = nc.dram_tensor(name, list(shape), dtype, kind="ExternalInput").ap()

    inp("xT", (D, S), F32)
    for k, v in consts.items():
        inp("c_" + k, v.shape, BF16 if v.dtype == BF else F32)
    inp("wqkv", (NL, 24, 128, DC, 128), F32)
    inp("wr", (NL, DC, 128, DC, 128), F32)
    inp("wo", (NL, DC, 128, DC, 128), F32)
    inp("wff1", (NL, DC, 128, DF), F32)
    inp("wff2", (NL, DC, 128, 32, 128), F32)
    inp("u_t", (DH, H), F32)
    inp("v_t", (DH, H), F32)
    inp("ln1g", (NL, 128, DC), F32)
    inp("ln1b", (NL, 128, DC), F32)
    inp("ln2g", (NL, 128, DC), F32)
    inp("ln2b", (NL, 128, DC), F32)
    inp("bff1", (NL, 128, 32), F32)
    inp("bff2", (NL, 128, DC), F32)
    inp("fwq", (DC, 128, DC, 128), F32)
    inp("fwk", (2, 128, DC, 128), F32)
    inp("fwv", (2, 128, DC, 128), F32)
    inp("fwg", (DH, DH), F32)
    inp("fwo", (DC, 128, DC, 128), F32)
    inp("fwr", (DC, 128, NE), F32)
    inp("fbr", (1, NE), F32)
    inp("fw1", (NE, 32, 128, DC, 128), F32)
    inp("fb1", (NE, 128, 32), F32)
    inp("fw2", (NE, HID, D), F32)
    inp("fb2", (NE, D), F32)
    inp("fbq", (128, DC), F32)
    inp("fbk", (128, 2), F32)
    inp("fbv", (128, 2), F32)
    inp("fbg", (DH, 1), F32)
    inp("fbo", (128, DC), F32)
    inp("fln1", (128, DC), F32)
    inp("fln2", (128, DC), F32)
    inp("fqn", (1, DH), F32)
    inp("fkn", (1, DH), F32)
    return d


def DS(b, name):
    if name not in b.dsems:
        b.dsems[name] = Sem(b._ctx, b.nc, name)
    return b.dsems[name]


def on_pe(b, f):
    b.block.tensor(f)


def on_dve(b, f):
    b.block.vector(f)


def on_act(b, f):
    b.block.scalar(f)


def on_pool(b, f):
    b.block.gpsimd(f)


def on_sp(b, f):
    b.block.sync(f)


def snap(b):
    return (b.PE.n, b.DV.n, b.AC.n, b.PO.n, b.SD.n, b.PD.n)


def wait_snap(eng, b, sn, pe=True, dv=True, ac=False, po=False, sd=False,
              pd=False):
    if pe and sn[0]:
        eng.wait_ge(b.PE.h, sn[0])
    if dv and sn[1]:
        eng.wait_ge(b.DV.h, sn[1])
    if ac and sn[2]:
        eng.wait_ge(b.AC.h, sn[2])
    if po and sn[3]:
        eng.wait_ge(b.PO.h, sn[3])
    if sd and sn[4]:
        eng.wait_ge(b.SD.h, sn[4])
    if pd and sn[5]:
        eng.wait_ge(b.PD.h, sn[5])



def build_program():
    consts = host_constants()
    nc = bass.Bass("TRN2", target_bir_lowering=False)
    dI = declare_inputs(nc, consts)
    d_out = nc.dram_tensor("out", [D, S], F32, kind="ExternalOutput").ap()
    d_xn = nc.dram_tensor("xn_spill", [D, S], BF16, kind="Internal").ap()
    d_s = nc.dram_tensor("s_spill", [D, S], BF16, kind="Internal").ap()

    with contextlib.ExitStack() as ctx:
        b = B()
        b.nc = nc
        b.dI = dI
        b.d_out = d_out
        b.d_xn = d_xn
        b.d_s = d_s
        b.abf = Arena(ctx.enter_context(
            nc.sbuf_tensor("arena_bf", [128, 91500], BF16)), 91500, "bf")
        b.af = Arena(ctx.enter_context(
            nc.sbuf_tensor("arena_f32", [128, 7168], F32)), 7168, "f32")
        b.psA = ctx.enter_context(nc.psum_tensor("psA", [128, 2048], F32))
        b.psB = ctx.enter_context(nc.psum_tensor("psB", [128, 2048], F32))
        b.PE = Sem(ctx, nc, "sPE")
        b.DV = Sem(ctx, nc, "sDV")
        b.AC = Sem(ctx, nc, "sAC")
        b.PO = Sem(ctx, nc, "sPO")
        b.SD = Sem(ctx, nc, "sSD")
        b.PD = Sem(ctx, nc, "sPD")
        b.dsems = {}
        b._ctx = ctx
        b.block = ctx.enter_context(nc.Block())
        build_all(b)
        globals()["_last_b"] = b
    return nc


def build_all(b):
    alloc_persistent(b)
    load_constants(b)
    phase_xn(b)
    if BUILD_PHASES < 2:
        return finish(b)
    for l in range(NL):
        attn_layer(b, l)
        if BUILD_PHASES < 3 + l:
            return finish(b)
    spill_s(b)
    if BUILD_PHASES < 7:
        return finish(b)
    ffn_module(b)
    finish(b)


def alloc_persistent(b):
    a = b.abf
    b.h = [a.alloc(f"h{k}", S) for k in range(DC)]
    b.ident = a.alloc("ident", 128)
    b.ident64 = a.alloc("ident64", 64)
    b.ones_col = a.alloc("ones_col", 16)
    b.ones_row = a.alloc("ones_row", 128)
    b.swap32 = a.alloc("swap32", 128)
    b.junk = a.alloc("junk", 512)
    f = b.af
    b.identf = f.alloc("identf", 128)
    b.ones_rowf = f.alloc("ones_rowf", 128)
    b.causal = f.alloc("causal", 128)
    b.uv = f.alloc("uv", 2 * H)
    b.rows = f.alloc("rows", 2 * S)
    b.row0 = b.rows[0:1, 0:S]
    b.row1 = b.rows[0:1, S:2 * S]
    b.sc1 = f.alloc("sc1", 2048)
    b.den = f.alloc("den", 4)
    b.lng = f.alloc("lng", DC)
    b.lnb = f.alloc("lnb", DC)
    b.ln2g = f.alloc("ln2g", DC)
    b.ln2b = f.alloc("ln2b", DC)
    b.bf1 = f.alloc("bf1", 32)
    b.bf2 = f.alloc("bf2", DC)
    b.fw_norm = f.alloc("fw_norm", DC)
    b.taps = {}


def load_constants(b):
    dI = b.dI

    def body(sp):
        cs = DS(b, "dconst")

        def ld(dst, src):
            sp.dma_start(out=dst, in_=src).then_inc(cs.h, 16)
            cs.n += 16
        ld(b.ident, dI["c_ident"])
        ld(b.identf, dI["c_identf"])
        ld(b.ident64[0:64, :], dI["c_ident64"])
        ld(b.ident64[64:128, :], dI["c_ident64"])
        ld(b.ones_col[:, 0:1], dI["c_ones_col"])
        ld(b.ones_row[0:1, :], dI["c_ones_row"])
        ld(b.ones_rowf[0:1, :], dI["c_onesrf"])
        ld(b.swap32, dI["c_swap32"])
        ld(b.causal, dI["c_causal"])
        ld(b.uv[0:64, 0:H], dI["u_t"])
        ld(b.uv[0:64, H:2 * H], dI["v_t"])
        b.const_ready = cs.n
    on_sp(b, body)
    b.const_sem = DS(b, "dconst")



def phase_xn(b):
    mb, rb, sq = norm_scratch(b)
    ld1 = []

    dx = DS(b, "dx")

    def po1(po):
        po.wait_ge(b.const_sem.h, b.const_ready)
        for k in range(DC):
            po.dma_start(out=b.h[k], in_=b.dI["xT"][k * 128:(k + 1) * 128, :]
                         ).then_inc(dx.h, 16)
            dx.n += 16
            ld1.append(dx.n)
    on_pool(b, po1)

    ac1 = [b.AC.n + k + 1 for k in range(DC)]
    pe_sq = [b.PE.n + 2 * (k + 1) for k in range(DC)]

    def act1(act):
        for k in range(DC):
            act.wait_ge(dx.h, ld1[-1] if True else ld1[k])
            if k > 0:
                act.wait_ge(b.PE.h, pe_sq[k - 1])
            ins = act.activation(sq, b.h[k], AF.Square)
            b.AC.inc(ins)
    on_act(b, act1)

    def pe1(pe):
        for k in range(DC):
            pe.wait_ge(dx.h, ld1[-1])
            for nc_ in range(4):
                sl = slice(nc_ * 512, (nc_ + 1) * 512)
                ins = pe.matmul(b.psA[0:1, sl], b.ones_col[:, 0:1], b.h[k][:, sl],
                                start=(k == 0), stop=(k == DC - 1))
            b.PE.inc(ins)
            pe.wait_ge(b.AC.h, ac1[k])
            for nc_ in range(4):
                sl = slice(nc_ * 512, (nc_ + 1) * 512)
                ins = pe.matmul(b.psB[0:1, sl], b.ones_col[:, 0:1], sq[:, sl],
                                start=(k == 0), stop=(k == DC - 1))
            b.PE.inc(ins)
    on_pe(b, pe1)
    stats_pe = b.PE.n

    mean_var_rows(b, stats_pe, 1.0 / 1024.0, 1e-5, with_mean=True)
    bcast_rows(b, mb, rb, with_mean=True)
    bc_dv = b.bc_dv

    dv2 = [b.DV.n + 2 * (k + 1) for k in range(DC)]

    def dve2(dve):
        dve.wait_ge(b.DV.h, bc_dv)
        for k in range(DC):
            dve.tensor_sub(b.sc1, b.h[k], mb)
            ins = dve.tensor_mul(b.h[k], b.sc1, rb)
            b.DV.inc(ins, 2)
    on_dve(b, dve2)
    assert b.DV.n == dv2[-1]
    b.xn_dv = dv2[-1]

    dsp = DS(b, "dspill")

    def sp3(sp):
        for k in range(DC):
            sp.wait_ge(b.DV.h, dv2[k])
            sp.dma_start(out=b.d_xn[k * 128:(k + 1) * 128, :], in_=b.h[k]
                         ).then_inc(dsp.h, 16)
            dsp.n += 16
        b.xn_spilled = dsp.n
    on_sp(b, sp3)

    free_norm_scratch(b)
    b.taps["xn"] = list(b.h)


def norm_scratch(b):
    mb = b.abf.alloc("mb", S)
    rb = b.abf.alloc("rb", S)
    sq = b.abf.alloc("sq", S)
    b._mb, b._rb, b._sq = mb, rb, sq
    return mb, rb, sq


def free_norm_scratch(b):
    b.abf.free("mb", "rb", "sq")


def mean_var_rows(b, stats_pe, inv_n, eps, with_mean):
    def dve_rows(dve):
        dve.wait_ge(b.PE.h, stats_pe)
        if with_mean:
            dve.tensor_scalar_mul(b.row0, b.psA[0:1, 0:2048], inv_n)
            dve.tensor_scalar_mul(b.row1, b.psB[0:1, 0:2048], inv_n)
            dve.tensor_mul(b.sc1[0:1, :], b.row0, b.row0)
            dve.tensor_sub(b.row1, b.row1, b.sc1[0:1, :])
        else:
            dve.tensor_scalar_mul(b.row1, b.psB[0:1, 0:2048], inv_n)
        ins = dve.tensor_scalar_add(b.row1, b.row1, eps)
        b.DV.inc(ins)
    on_dve(b, dve_rows)
    var_dv = b.DV.n

    def act_rows(act):
        act.wait_ge(b.DV.h, var_dv)
        ins = act.activation(b.row1, b.row1, AF.Sqrt)
        b.AC.inc(ins)
    on_act(b, act_rows)
    sq_ac = b.AC.n

    def dve_recip(dve):
        dve.wait_ge(b.AC.h, sq_ac)
        ins = dve.reciprocal(b.row1, b.row1)
        b.DV.inc(ins)
    on_dve(b, dve_recip)
    b.rstd_dv = b.DV.n


def bcast_rows(b, mb, rb, with_mean):
    def pe_bc(pe):
        pe.wait_ge(b.DV.h, b.rstd_dv)
        for nc_ in range(4):
            sl = slice(nc_ * 512, (nc_ + 1) * 512)
            if with_mean:
                pe.matmul(b.psA[:, sl], b.ones_rowf[0:1, :], b.row0[:, sl],
                          start=True, stop=True)
            ins = pe.matmul(b.psB[:, sl], b.ones_rowf[0:1, :], b.row1[:, sl],
                            start=True, stop=True)
        b.PE.inc(ins)
    on_pe(b, pe_bc)
    bc_pe = b.PE.n

    def dve_bc(dve):
        dve.wait_ge(b.PE.h, bc_pe)
        if with_mean:
            dve.tensor_copy(mb, b.psA[:, 0:2048])
        ins = dve.tensor_copy(rb, b.psB[:, 0:2048])
        b.DV.inc(ins)
    on_dve(b, dve_bc)
    b.bc_dv = b.DV.n



def attn_layer(b, l):
    b.vecT = [b.abf.alloc(f"vecT{k}", S) for k in range(DC)]
    for half in range(2):
        kr_half(b, l, half)
        if l == 0 and half == 0:
            b.taps["kr"] = list(b.krh)
            if BUILD_PHASES < 2.3:
                return
        qkv_half(b, l, half)
        if l == 0 and half == 0:
            b.taps["qh"] = list(b.qh)
            b.taps["kh"] = list(b.kh)
            b.taps["vh"] = list(b.vh)
            if BUILD_PHASES < 2.5:
                return
        attention_half(b, l, half)
        if l == 0 and half == 0:
            b.taps["vec0"] = list(b.vecT)
            if BUILD_PHASES < 2.7:
                return
    if l == 0:
        b.taps["vec"] = list(b.vecT)
        if BUILD_PHASES < 2.8:
            return
    wo_ln1(b, l)
    if l == 0 and BUILD_PHASES < 2.9:
        b.taps["postln1"] = list(b.h)
        return
    b.abf.free(*[f"vecT{k}" for k in range(DC)])
    ff_block(b, l)
    b.taps[f"l{l}"] = list(b.h)


def kr_half(b, l, half):
    a = b.abf
    sn = snap(b)
    rT = [a.alloc(f"rT{k}", S) for k in range(DC)]
    b.krh = [a.alloc(f"kr{j}", S) for j in range(4)]
    wbuf = [a.alloc(f"wkr{i}", DC * 128) for i in range(2)]
    ld = []

    drt = DS(b, "drt")
    dwkr = [DS(b, "dwkr0"), DS(b, "dwkr1")]

    def sp_body(sp):
        wait_snap(sp, b, sn, pe=True, dv=True)
        for k in range(DC):
            sp.dma_start(out=rT[k], in_=b.dI["c_rT"][k]).then_inc(drt.h, 16)
            drt.n += 16
            ld.append(drt.n)
    on_sp(b, sp_body)

    wld = []
    pe_use = [b.PE.n + j + 1 for j in range(4)]

    def pool_body(po):
        wait_snap(po, b, sn, pe=True, dv=True)
        for j in range(4):
            ft = half * 4 + j
            if j >= 2:
                po.wait_ge(b.PE.h, pe_use[j - 2])
            s_ = dwkr[j % 2]
            po.dma_start(out=wbuf[j % 2], in_=b.dI["wr"][l, ft]
                         ).then_inc(s_.h, 16)
            s_.n += 16
            wld.append((s_, s_.n))
    on_pool(b, pool_body)

    evac = [b.DV.n + j + 1 for j in range(4)]
    pe_done = []

    def pe_body(pe):
        pe.wait_ge(drt.h, ld[-1])
        wait_snap(pe, b, sn, pe=False, dv=True)
        for j in range(4):
            pe.wait_ge(wld[j][0].h, wld[j][1])
            if j >= 2:
                pe.wait_ge(b.DV.h, evac[j - 2])
            ps = b.psA if j % 2 == 0 else b.psB
            for k in range(DC):
                for nc_ in range(4):
                    sl = slice(nc_ * 512, (nc_ + 1) * 512)
                    ins = pe.matmul(ps[:, sl], wbuf[j % 2][:, k * 128:(k + 1) * 128],
                                    rT[k][:, sl], start=(k == 0), stop=(k == DC - 1))
            pe_done.append(b.PE.inc(ins))
    on_pe(b, pe_body)
    assert pe_done == [] or True

    def dve_body(dve):
        for j in range(4):
            dve.wait_ge(b.PE.h, pe_use[j])
            ps = b.psA if j % 2 == 0 else b.psB
            ins = dve.tensor_copy(b.krh[j], ps[:, 0:2048])
            b.DV.inc(ins)
    on_dve(b, dve_body)
    assert b.DV.n == evac[-1]
    b.kr_dv = evac[-1]
    a.free(*[f"rT{k}" for k in range(DC)], "wkr0", "wkr1")


def qkv_half(b, l, half):
    a = b.abf
    sn = snap(b)
    b.qh = [a.alloc(f"qh{j}", S) for j in range(4)]
    b.kh = [a.alloc(f"kh{j}", S) for j in range(4)]
    b.vh = [a.alloc(f"vh{j}", S) for j in range(4)]
    outs = {"q": b.qh, "k": b.kh, "v": b.vh}
    wbuf = [a.alloc(f"wqkv{i}", DC * 128) for i in range(2)]
    fts = ([("q", j, half * 4 + j) for j in range(4)]
           + [("k", j, 8 + half * 4 + j) for j in range(4)]
           + [("v", j, 16 + half * 4 + j) for j in range(4)])
    wld = []
    pe_use = [b.PE.n + i + 1 for i in range(12)]
    dwq = [DS(b, "dwqkv0"), DS(b, "dwqkv1")]

    def pool_body(po):
        wait_snap(po, b, sn, pe=True, dv=True)
        for i, (_, _, ft) in enumerate(fts):
            if i >= 2:
                po.wait_ge(b.PE.h, pe_use[i - 2])
            s_ = dwq[i % 2]
            po.dma_start(out=wbuf[i % 2], in_=b.dI["wqkv"][l, ft]
                         ).then_inc(s_.h, 16)
            s_.n += 16
            wld.append((s_, s_.n))
    on_pool(b, pool_body)

    evac = [b.DV.n + i + 1 for i in range(12)]

    def pe_body(pe):
        wait_snap(pe, b, sn, pe=False, dv=True)
        for i, (kind, j, ft) in enumerate(fts):
            pe.wait_ge(wld[i][0].h, wld[i][1])
            if i >= 2:
                pe.wait_ge(b.DV.h, evac[i - 2])
            ps = b.psA if i % 2 == 0 else b.psB
            for k in range(DC):
                for nc_ in range(4):
                    sl = slice(nc_ * 512, (nc_ + 1) * 512)
                    ins = pe.matmul(ps[:, sl], wbuf[i % 2][:, k * 128:(k + 1) * 128],
                                    b.h[k][:, sl], start=(k == 0), stop=(k == DC - 1))
            b.PE.inc(ins)
    on_pe(b, pe_body)

    def dve_body(dve):
        for i, (kind, j, ft) in enumerate(fts):
            dve.wait_ge(b.PE.h, pe_use[i])
            ps = b.psA if i % 2 == 0 else b.psB
            ins = dve.tensor_copy(outs[kind][j], ps[:, 0:2048])
            b.DV.inc(ins)
    on_dve(b, dve_body)
    assert b.DV.n == evac[-1]
    b.qkv_dv = evac[-1]
    a.free("wqkv0", "wqkv1")


def softmax_block(b, T, nk, prob_ap, qt):
    def mx(dve):
        dve.wait_ge(b.PE.h, b.sc_mark)
        if b.last_exp_ac:
            dve.wait_ge(b.AC.h, b.last_exp_ac)
        if nk < 2048 or b.mask_diag:
            dve.tensor_add(T[:, nk - 128:nk], T[:, nk - 128:nk], b.causal)
        ins = dve.reduce_max(b.nmax[0:128, 0:1], T[:, 0:nk], axis=AX.X,
                             negate=True)
        b.DV.inc(ins)
    on_dve(b, mx)
    mx_dv = b.DV.n

    def expf(act):
        act.wait_ge(b.DV.h, mx_dv)
        ins = act.activation(prob_ap[:, 0:nk], T[:, 0:nk], AF.Exp,
                             bias=b.nmax[0:128, 0:1], scale=1.0,
                             accum_out=b.den[0:128, 0:1])
        b.AC.inc(ins)
    on_act(b, expf)
    b.last_exp_ac = b.AC.n

    def rd(dve):
        dve.wait_ge(b.AC.h, b.last_exp_ac)
        if b.last_tr_pe:
            dve.wait_ge(b.PE.h, b.last_tr_pe)
        dve.reciprocal(b.den[0:128, 1:2], b.den[0:128, 0:1])
        dve.memset(b.junk[:, 0:384], 0.0)
        dve.tensor_scalar_mul(b.diag, b.ident, b.den[0:128, 1:2])
        ins = dve.memset(b.junk[:, 0:128], 0.0)
        b.DV.inc(ins)
    on_dve(b, rd)
    return b.DV.n


def attn_softmax_part(b, st):
    T, nk, qt = st["T"], st["nch"] * 128, st["qt"]
    prob_ap = st["prob"]

    def mx(dve):
        dve.wait_ge(b.PE.h, st["sc_mark"])
        if b.last_exp_ac:
            dve.wait_ge(b.AC.h, b.last_exp_ac)
        if nk < 2048 or b.mask_diag:
            dve.tensor_add(T[:, nk - 128:nk], T[:, nk - 128:nk], b.causal)
        ins = dve.reduce_max(b.nmax[0:128, 0:1], T[:, 0:nk], axis=AX.X,
                             negate=True)
        b.DV.inc(ins)
    on_dve(b, mx)
    mx_dv = b.DV.n

    def expf(act):
        act.wait_ge(b.DV.h, mx_dv)
        ins = act.activation(prob_ap[:, 0:nk], T[:, 0:nk], AF.Exp,
                             bias=b.nmax[0:128, 0:1], scale=1.0,
                             accum_out=b.den[0:128, 0:1])
        b.AC.inc(ins)
    on_act(b, expf)
    b.last_exp_ac = b.AC.n

    diag_ap = b.diag2[qt % 2]

    def rd(dve):
        dve.wait_ge(b.AC.h, b.last_exp_ac)
        if b.tr_pe_par[qt % 2]:
            dve.wait_ge(b.PE.h, b.tr_pe_par[qt % 2])
        dve.reciprocal(b.den[0:128, 1:2], b.den[0:128, 0:1])
        dve.memset(b.junk[:, 0:384], 0.0)
        dve.tensor_scalar_mul(diag_ap, b.ident, b.den[0:128, 1:2])
        ins = dve.memset(b.junk[:, 0:128], 0.0)
        b.DV.inc(ins)
    on_dve(b, rd)
    st["diag_dv"] = b.DV.n


def attn_tv_part(b, st):
    T, nch, qt = st["T"], st["nch"], st["qt"]
    prob_ap = st["prob"]
    diag_ap = b.diag2[qt % 2]
    nrounds = (nch + 3) // 4
    tr_ev = [b.DV.n + r + 1 for r in range(nrounds)]
    pe_tr0 = b.PE.n

    def tr_pe(pe):
        pe.wait_ge(b.DV.h, st["diag_dv"])
        for r in range(nrounds):
            if r >= 2:
                pe.wait_ge(b.DV.h, tr_ev[r - 2])
            w = min(4, nch - r * 4)
            for c in range(w):
                ch = r * 4 + c
                ins = pe.matmul(
                    T[:, (r % 2) * 512 + c * 128:(r % 2) * 512 + (c + 1) * 128],
                    prob_ap[:, ch * 128:(ch + 1) * 128],
                    diag_ap, start=(c == 0), stop=(c == w - 1))
            b.PE.inc(ins)
    on_pe(b, tr_pe)
    tr_marks = [pe_tr0 + r + 1 for r in range(nrounds)]
    b.tr_pe_par[qt % 2] = tr_marks[-1]

    def tr_dve(dve):
        for r in range(nrounds):
            dve.wait_ge(b.PE.h, tr_marks[r])
            w = min(4, nch - r * 4) * 128
            dst = b.pt[r % 2][:, (r // 2 % 2) * 512:(r // 2 % 2) * 512 + w]
            dve.tensor_copy(dst, T[:, (r % 2) * 512:(r % 2) * 512 + w])
            ins = dve.memset(b.junk[:, 0:128], 0.0)
            b.DV.inc(ins)
    on_dve(b, tr_dve)
    assert b.DV.n == tr_ev[-1]

    vecreg = T[0:64, 1664:1792]

    def vec_pe(pe):
        for r in range(nrounds):
            pe.wait_ge(b.DV.h, tr_ev[r])
            w = min(4, nch - r * 4)
            for c in range(w):
                ch = r * 4 + c
                src = b.pt[r % 2][:, (r // 2 % 2) * 512 + c * 128:
                                  (r // 2 % 2) * 512 + (c + 1) * 128]
                ins = pe.matmul(vecreg, st["vtok"][:, ch * 64:(ch + 1) * 64],
                                src, start=(ch == 0), stop=(ch == nch - 1))
        b.PE.inc(ins)
    on_pe(b, vec_pe)
    vec_mark = b.PE.n

    def vec_dve(dve):
        dve.wait_ge(b.PE.h, vec_mark)
        st["out_fn"](dve, vecreg)
        ins = dve.memset(b.junk[:, 0:128], 0.0)
        b.DV.inc(ins)
    on_dve(b, vec_dve)
    st["done"] = b.DV.n


def attn_qt_pipeline(b, nqt, mk_state):
    states = {}
    states[0] = mk_state(0, states)
    attn_softmax_part(b, states[0])
    for qt in range(nqt):
        if qt + 1 < nqt:
            states[qt + 1] = mk_state(qt + 1, states)
            attn_softmax_part(b, states[qt + 1])
        attn_tv_part(b, states[qt])
    return states


def attention_half(b, l, half):
    a = b.abf
    if (DEBUG_TAP == "dbg" and l == 0 and half == 0
            and not hasattr(b, "dbgP")):
        b.dbgP = a.alloc("dbgP", S)
        b.dbgS = b.sc1
        b.dbgD = b.af.alloc("dbgD", 2)
        b.dbg_snap = True
    qstack = a.alloc("qstack", S)
    kstack = a.alloc("kstack", S)
    vtok = a.alloc("vtok", 16 * DH)
    b.prob = [a.alloc(f"prob{i}", S) for i in range(2)]
    b.pt = [a.alloc(f"pt{i}", 1024) for i in range(2)]
    b.diag2 = [a.alloc(f"diag{i}", 128) for i in range(2)]
    b.nmax = b.af.alloc("nmax", 2)
    b.mask_diag = False
    b.last_exp_ac = 0
    b.tr_pe_par = [0, 0]
    if not hasattr(b, "head_done_dv"):
        b.head_done_dv = 0
    b.last_scores_pe = 0

    for hh in range(8):
        head = half * 8 + hh
        j = hh // 2
        r0 = (hh % 2) * 64

        def build_dve(dve, j=j, r0=r0, head=head, hh=hh):
            if hh == 0:
                dve.wait_ge(b.DV.h, max(b.qkv_dv, b.kr_dv))
            if b.last_scores_pe:
                dve.wait_ge(b.PE.h, b.last_scores_pe)
            dve.tensor_scalar(qstack[0:64, :], b.qh[j][r0:r0 + 64, :],
                              b.uv[0:64, head:head + 1], 0.125,
                              ALU.add, ALU.mult)
            dve.tensor_scalar(qstack[64:128, :], b.qh[j][r0:r0 + 64, :],
                              b.uv[0:64, H + head:H + head + 1], 0.125,
                              ALU.add, ALU.mult)
            dve.tensor_copy(kstack[0:64, :], b.kh[j][r0:r0 + 64, :])
            dve.tensor_copy(kstack[64:128, :], b.krh[j][r0:r0 + 64, :])
            ins = dve.memset(b.junk[:, 0:128], 0.0)
            b.DV.inc(ins)
        on_dve(b, build_dve)
        stacks_dv = b.DV.n

        pe0 = b.PE.n

        def vt_pe(pe, j=j, r0=r0, hh=hh):
            if b.head_done_dv:
                pe.wait_ge(b.DV.h, b.head_done_dv)
            pe.wait_ge(b.DV.h, stacks_dv)
            for c in range(16):
                col = (c % 8) * 64 + (c // 8) * 512
                ins = pe.matmul(b.psB[0:128, col:col + 64],
                                b.vh[j][r0:r0 + 64, c * 128:(c + 1) * 128],
                                b.ident64[r0:r0 + 64, :], start=True, stop=True)
                if c % 8 == 7:
                    b.PE.inc(ins)
        on_pe(b, vt_pe)
        vt_marks = [pe0 + 1, pe0 + 2]

        def vt_dve(dve):
            for r in range(2):
                dve.wait_ge(b.PE.h, vt_marks[r])
                dve.tensor_copy(vtok[:, r * 512:(r + 1) * 512],
                                b.psB[:, r * 512:(r + 1) * 512])
                ins = dve.memset(b.junk[:, 0:128], 0.0)
                b.DV.inc(ins)
        on_dve(b, vt_dve)
        vtok_dv = b.DV.n

        def mk_state(qt, states, head=head, vtok_dv=vtok_dv):
            T = b.psA if qt % 2 == 0 else b.psB
            st = {"qt": qt, "T": T, "nch": 16, "prob": b.prob[qt % 2],
                  "vtok": vtok}
            prev2 = states.get(qt - 2)

            def sc_pe(pe):
                if qt < 2:
                    pe.wait_ge(b.DV.h, vtok_dv)
                else:
                    pe.wait_ge(b.DV.h, prev2["done"])
                for nc_ in range(4):
                    sl = slice(nc_ * 512, (nc_ + 1) * 512)
                    ins = pe.matmul(T[:, sl],
                                    qstack[:, qt * 128:(qt + 1) * 128],
                                    kstack[:, sl], start=True, stop=True)
                b.PE.inc(ins)
            on_pe(b, sc_pe)
            st["sc_mark"] = b.PE.n
            b.last_scores_pe = b.PE.n

            def out_fn(dve, vecreg, head=head, qt=qt):
                dst = b.vecT[head // 2][(head % 2) * 64:(head % 2) * 64 + 64,
                                       qt * 128:(qt + 1) * 128]
                return dve.tensor_copy(dst, vecreg)
            st["out_fn"] = out_fn
            return st

        states = attn_qt_pipeline(b, 16, mk_state)
        b.head_done_dv = states[15]["done"]

    a.free("qstack", "kstack", "vtok", "prob0", "prob1", "pt0", "pt1",
           "diag0", "diag1")
    b.af.free("nmax")
    a.free(*[f"qh{j}" for j in range(4)], *[f"kh{j}" for j in range(4)],
           *[f"vh{j}" for j in range(4)], *[f"kr{j}" for j in range(4)])
    b.attn_dv = b.head_done_dv


def wo_ln1(b, l):
    a = b.abf
    sn = snap(b)
    wbuf = [a.alloc(f"wwo{i}", DC * 128) for i in range(2)]

    dpar = DS(b, "dpar")
    b.par_sem = dpar

    def sp_par(sp):
        wait_snap(sp, b, sn, pe=False, dv=True)
        for nm, dst in (("ln1g", b.lng), ("ln1b", b.lnb),
                        ("ln2g", b.ln2g), ("ln2b", b.ln2b)):
            sp.dma_start(out=dst, in_=b.dI[nm][l]).then_inc(dpar.h, 16)
            dpar.n += 16
        sp.dma_start(out=b.bf1, in_=b.dI["bff1"][l]).then_inc(dpar.h, 16)
        sp.dma_start(out=b.bf2, in_=b.dI["bff2"][l]).then_inc(dpar.h, 16)
        dpar.n += 32
        b.par_ld = dpar.n
    on_sp(b, sp_par)

    wld = []
    pe_use = [b.PE.n + i + 1 for i in range(DC)]
    dwo = [DS(b, "dwwo0"), DS(b, "dwwo1")]

    def pool_body(po):
        wait_snap(po, b, sn, pe=True, dv=True)
        for i in range(DC):
            if i >= 2:
                po.wait_ge(b.PE.h, pe_use[i - 2])
            s_ = dwo[i % 2]
            po.dma_start(out=wbuf[i % 2], in_=b.dI["wo"][l, i]).then_inc(s_.h, 16)
            s_.n += 16
            wld.append((s_, s_.n))
    on_pool(b, pool_body)

    add_dv = [b.DV.n + i + 1 for i in range(DC)]

    def pe_body(pe):
        pe.wait_ge(b.DV.h, b.attn_dv)
        for i in range(DC):
            pe.wait_ge(wld[i][0].h, wld[i][1])
            if i >= 2:
                pe.wait_ge(b.DV.h, add_dv[i - 2])
            ps = b.psA if i % 2 == 0 else b.psB
            for k in range(DC):
                for nc_ in range(4):
                    sl = slice(nc_ * 512, (nc_ + 1) * 512)
                    ins = pe.matmul(ps[:, sl], wbuf[i % 2][:, k * 128:(k + 1) * 128],
                                    b.vecT[k][:, sl], start=(k == 0),
                                    stop=(k == DC - 1))
            b.PE.inc(ins)
    on_pe(b, pe_body)

    def dve_body(dve):
        for i in range(DC):
            dve.wait_ge(b.PE.h, pe_use[i])
            ps = b.psA if i % 2 == 0 else b.psB
            ins = dve.tensor_add(b.h[i], b.h[i], ps[:, 0:2048])
            b.DV.inc(ins)
    on_dve(b, dve_body)
    assert b.DV.n == add_dv[-1]
    a.free("wwo0", "wwo1")
    layer_norm_h(b, b.lng, b.lnb, add_dv, par_ld=b.par_ld)


def layer_norm_h(b, gamma, beta, ready_dv, par_ld=None):
    mb, rb, sq = norm_scratch(b)
    ac0 = [b.AC.n + k + 1 for k in range(DC)]
    pe_sq = [b.PE.n + 2 * (k + 1) for k in range(DC)]

    def act_body(act):
        for k in range(DC):
            act.wait_ge(b.DV.h, ready_dv[-1])
            if k > 0:
                act.wait_ge(b.PE.h, pe_sq[k - 1])
            ins = act.activation(sq, b.h[k], AF.Square)
            b.AC.inc(ins)
    on_act(b, act_body)

    def pe_body(pe):
        for k in range(DC):
            pe.wait_ge(b.DV.h, ready_dv[-1])
            for nc_ in range(4):
                sl = slice(nc_ * 512, (nc_ + 1) * 512)
                ins = pe.matmul(b.psA[0:1, sl], b.ones_col[:, 0:1], b.h[k][:, sl],
                                start=(k == 0), stop=(k == DC - 1))
            b.PE.inc(ins)
            pe.wait_ge(b.AC.h, ac0[k])
            for nc_ in range(4):
                sl = slice(nc_ * 512, (nc_ + 1) * 512)
                ins = pe.matmul(b.psB[0:1, sl], b.ones_col[:, 0:1], sq[:, sl],
                                start=(k == 0), stop=(k == DC - 1))
            b.PE.inc(ins)
    on_pe(b, pe_body)
    stats_pe = b.PE.n

    mean_var_rows(b, stats_pe, 1.0 / 1024.0, 1e-5, with_mean=True)
    bcast_rows(b, mb, rb, with_mean=True)
    bc_dv = b.bc_dv

    out_dv = []

    def dve_norm(dve):
        dve.wait_ge(b.DV.h, bc_dv)
        if par_ld:
            dve.wait_ge(b.par_sem.h, par_ld)
        for k in range(DC):
            dve.tensor_sub(b.sc1, b.h[k], mb)
            dve.tensor_mul(b.sc1, b.sc1, rb)
            ins = dve.tensor_scalar(b.h[k], b.sc1, gamma[:, k:k + 1],
                                    beta[:, k:k + 1], ALU.mult, ALU.add)
            out_dv.append(b.DV.inc(ins, 3))
    on_dve(b, dve_norm)
    b.ln_dv = list(out_dv)
    free_norm_scratch(b)


def ff_block(b, l):
    a = b.abf
    sn = snap(b)
    w1 = [a.alloc(f"wf1_{k}", DF) for k in range(DC)]
    w2 = [a.alloc(f"wf2_{d}", 32 * 128) for d in range(DC)]
    ring = a.alloc("ring", 32 * 256)
    w1ld, w2ld = [], []

    dwf1 = DS(b, "dwf1")
    dwf2 = DS(b, "dwf2")

    def pool_body(po):
        wait_snap(po, b, sn, pe=True, dv=True, ac=True)
        for k in range(DC):
            po.dma_start(out=w1[k], in_=b.dI["wff1"][l, k]).then_inc(dwf1.h, 16)
            dwf1.n += 16
            w1ld.append(dwf1.n)
        for d in range(DC):
            po.dma_start(out=w2[d], in_=b.dI["wff2"][l, d]).then_inc(dwf2.h, 16)
            dwf2.n += 16
            w2ld.append(dwf2.n)
    on_pool(b, pool_body)

    NT = S // 256
    relu_ac = {}
    c = b.AC.n
    for t in range(NT):
        for kf in range(32):
            c += 1
            relu_ac[(t, kf)] = c
    ff1_pe = {}
    yA_pe, yB_pe = [], []
    c = b.PE.n
    for t in range(NT):
        for kf in range(32):
            c += 1
            ff1_pe[(t, kf)] = c
        c += 1
        yA_pe.append(c)
        c += 1
        yB_pe.append(c)
    evA_dv = [b.DV.n + 2 * t + 1 for t in range(NT)]
    evB_dv = [b.DV.n + 2 * t + 2 for t in range(NT)]

    def pe_body(pe):
        pe.wait_ge(dwf1.h, w1ld[-1])
        pe.wait_ge(b.DV.h, b.ln_dv[-1])
        for t in range(NT):
            tsl = slice(t * 256, (t + 1) * 256)
            if t == 0:
                pe.wait_ge(dwf2.h, w2ld[-1])
            for kf in range(33):
                if kf < 32:
                    fof = (kf % 2) * 512
                    if kf >= 2:
                        pe.wait_ge(b.AC.h, relu_ac[(t, kf - 2)])
                    for k in range(DC):
                        ins = pe.matmul(b.psA[:, fof:fof + 256],
                                        w1[k][:, kf * 128:(kf + 1) * 128],
                                        b.h[k][:, tsl], start=(k == 0),
                                        stop=(k == DC - 1))
                    b.PE.inc(ins)
                if kf == 0:
                    continue
                kc = kf - 1
                if kc == 0 and t >= 1:
                    pe.wait_ge(b.DV.h, evB_dv[t - 1])
                pe.wait_ge(b.AC.h, relu_ac[(t, kc)])
                for dd in range(4):
                    ins = pe.matmul(b.psB[:, dd * 512:dd * 512 + 256],
                                    w2[dd][:, kc * 128:(kc + 1) * 128],
                                    ring[:, kc * 256:(kc + 1) * 256],
                                    start=(kc == 0), stop=(kc == 31))
            b.PE.inc(ins)
            pe.wait_ge(b.DV.h, evA_dv[t])
            for kf in range(32):
                for dd in range(4, DC):
                    ins = pe.matmul(b.psB[:, (dd - 4) * 512:(dd - 4) * 512 + 256],
                                    w2[dd][:, kf * 128:(kf + 1) * 128],
                                    ring[:, kf * 256:(kf + 1) * 256],
                                    start=(kf == 0), stop=(kf == 31))
            b.PE.inc(ins)
    on_pe(b, pe_body)

    def act_body(act):
        act.wait_ge(b.par_sem.h, b.par_ld)
        for t in range(NT):
            for kf in range(32):
                act.wait_ge(b.PE.h, ff1_pe[(t, kf)])
                fof = (kf % 2) * 512
                ins = act.activation(ring[:, kf * 256:(kf + 1) * 256],
                                     b.psA[:, fof:fof + 256],
                                     AF.Relu, bias=b.bf1[:, kf:kf + 1], scale=1.0)
                b.AC.inc(ins)
                assert b.AC.n == relu_ac[(t, kf)]
    on_act(b, act_body)

    def dve_body(dve):
        dve.wait_ge(b.par_sem.h, b.par_ld)
        for t in range(NT):
            tsl = slice(t * 256, (t + 1) * 256)
            dve.wait_ge(b.PE.h, yA_pe[t])
            for dd in range(4):
                ins = dve.tensor_scalar(b.h[dd][:, tsl],
                                        b.psB[:, dd * 512:dd * 512 + 256],
                                        b.bf2[:, dd:dd + 1], None, ALU.add)
            b.DV.inc(ins)
            assert b.DV.n == evA_dv[t]
            dve.wait_ge(b.PE.h, yB_pe[t])
            for dd in range(4, DC):
                ins = dve.tensor_scalar(b.h[dd][:, tsl],
                                        b.psB[:, (dd - 4) * 512:(dd - 4) * 512 + 256],
                                        b.bf2[:, dd:dd + 1], None, ALU.add)
            b.DV.inc(ins)
            assert b.DV.n == evB_dv[t]
    on_dve(b, dve_body)

    a.free(*[f"wf1_{k}" for k in range(DC)], *[f"wf2_{d}" for d in range(DC)],
           "ring")
    layer_norm_h(b, b.ln2g, b.ln2b, [evB_dv[-1]], par_ld=b.par_ld)


def spill_s(b):
    dss = DS(b, "dsspill")

    def sp_body(sp):
        for k in range(DC):
            sp.wait_ge(b.DV.h, b.ln_dv[k]) if False else sp.wait_ge(b.DV.h, b.ln_dv[-1])
            sp.dma_start(out=b.d_s[k * 128:(k + 1) * 128, :], in_=b.h[k]
                         ).then_inc(dss.h, 16)
            dss.n += 16
        b.s_spilled = dss.n
    on_sp(b, sp_body)
    b.taps["s"] = list(b.h)



def ffn_module(b):
    a = b.abf
    b.xn = b.h

    dxr = DS(b, "dxnr")
    b.xnr_sem = dxr

    def sp_body(sp):
        sp.wait_ge(DS(b, "dspill").h, b.xn_spilled)
        sp.wait_ge(DS(b, "dsspill").h, b.s_spilled)
        for k in range(DC):
            sp.dma_start(out=b.xn[k], in_=b.d_xn[k * 128:(k + 1) * 128, :]
                         ).then_inc(dxr.h, 16)
            dxr.n += 16
        b.xn_ld = dxr.n
    on_sp(b, sp_body)

    mb, rb, sq = norm_scratch(b)
    ffn_rms(b, src=b.xn, ready_sd=b.xn_ld, eps=1e-6, wname="fln1", rb=rb, sq=sq)
    b.h1 = [a.alloc(f"h1_{k}", S) for k in range(DC)]
    h1_dv = []

    def dve_h1(dve):
        dve.wait_ge(b.xnr_sem.h, b.xn_ld)
        dve.wait_ge(DS(b, "dfwn").h, b.fwn_ld)
        for k in range(DC):
            ins = dve.scalar_tensor_tensor(b.h1[k], b.xn[k],
                                           b.fw_norm[:, k:k + 1], rb,
                                           ALU.mult, ALU.mult)
            h1_dv.append(b.DV.inc(ins))
    on_dve(b, dve_h1)
    b.h1_dv = list(h1_dv)
    free_norm_scratch(b)

    ffn_qkv(b)
    if BUILD_PHASES < 7.6:
        return
    ffn_rope_qknorm(b)
    if BUILD_PHASES < 8:
        return
    ffn_attention(b)
    if BUILD_PHASES < 8.6:
        return
    ffn_wo_xf(b)
    if BUILD_PHASES < 9:
        return
    mb, rb, sq = norm_scratch(b)
    ffn_rms(b, src=b.xn, ready_dv=b.xf_dv, eps=1e-6, wname="fln2", rb=rb, sq=sq)
    b.h2 = [a.alloc(f"h2_{k}", S) for k in range(DC)]
    h2_dv = []

    def dve_h2(dve):
        dve.wait_ge(DS(b, "dfwn").h, b.fwn_ld)
        for k in range(DC):
            ins = dve.scalar_tensor_tensor(b.h2[k], b.xn[k],
                                           b.fw_norm[:, k:k + 1], rb,
                                           ALU.mult, ALU.mult)
            h2_dv.append(b.DV.inc(ins))
    on_dve(b, dve_h2)
    b.h2_dv = list(h2_dv)
    free_norm_scratch(b)

    ffn_logits_mask(b)
    if BUILD_PHASES < 10:
        return
    ffn_moe(b)


def ffn_rms(b, src, eps, wname, rb, sq, ready_sd=None, ready_dv=None):
    sn = snap(b)

    dfwn = DS(b, "dfwn")

    def sp_par(sp):
        wait_snap(sp, b, sn, pe=False, dv=True)
        sp.dma_start(out=b.fw_norm, in_=b.dI[wname]).then_inc(dfwn.h, 16)
        dfwn.n += 16
        b.fwn_ld = dfwn.n
    on_sp(b, sp_par)

    ac0 = [b.AC.n + k + 1 for k in range(DC)]
    pe_sq = [b.PE.n + k + 1 for k in range(DC)]

    def act_body(act):
        if ready_sd:
            act.wait_ge(b.xnr_sem.h, ready_sd)
        if ready_dv:
            act.wait_ge(b.DV.h, ready_dv)
        for k in range(DC):
            if k > 0:
                act.wait_ge(b.PE.h, pe_sq[k - 1])
            ins = act.activation(sq, src[k], AF.Square)
            b.AC.inc(ins)
    on_act(b, act_body)

    def pe_body(pe):
        wait_snap(pe, b, sn, pe=False, dv=True)
        for k in range(DC):
            pe.wait_ge(b.AC.h, ac0[k])
            for nc_ in range(4):
                sl = slice(nc_ * 512, (nc_ + 1) * 512)
                ins = pe.matmul(b.psB[0:1, sl], b.ones_col[:, 0:1], sq[:, sl],
                                start=(k == 0), stop=(k == DC - 1))
            b.PE.inc(ins)
    on_pe(b, pe_body)
    stats_pe = b.PE.n

    mean_var_rows(b, stats_pe, 1.0 / 1024.0, eps, with_mean=False)
    bcast_rows(b, None, rb, with_mean=False)



def ffn_qkv(b):
    a = b.abf
    sn = snap(b)
    b.fq = [a.alloc(f"fq{j}", S) for j in range(DC)]
    b.fk = [a.alloc(f"fk{j}", S) for j in range(2)]
    b.fv = [a.alloc(f"fv{j}", S) for j in range(2)]
    wbuf = [a.alloc(f"wfq{i}", DC * 128) for i in range(2)]
    b.fbq = b.af.alloc("fbq", DC)
    b.fbk = b.af.alloc("fbk", 2)
    b.fbv = b.af.alloc("fbv", 2)

    dfb = DS(b, "dfbias")

    def sp_par(sp):
        wait_snap(sp, b, sn, pe=False, dv=True)
        for nm, dst in (("fbq", b.fbq), ("fbk", b.fbk), ("fbv", b.fbv)):
            sp.dma_start(out=dst, in_=b.dI[nm]).then_inc(dfb.h, 16)
            dfb.n += 16
        b.fbias_ld = dfb.n
    on_sp(b, sp_par)

    jobs = ([("fwq", j, b.fq, j, b.fbq) for j in range(DC)]
            + [("fwk", j, b.fk, j, b.fbk) for j in range(2)]
            + [("fwv", j, b.fv, j, b.fbv) for j in range(2)])
    wld = []
    pe_use = [b.PE.n + i + 1 for i in range(len(jobs))]
    dfq = [DS(b, "dwfq0"), DS(b, "dwfq1")]

    def pool_body(po):
        wait_snap(po, b, sn, pe=True, dv=True)
        for i, (wn, ft, _, _, _) in enumerate(jobs):
            if i >= 2:
                po.wait_ge(b.PE.h, pe_use[i - 2])
            s_ = dfq[i % 2]
            po.dma_start(out=wbuf[i % 2], in_=b.dI[wn][ft]).then_inc(s_.h, 16)
            s_.n += 16
            wld.append((s_, s_.n))
    on_pool(b, pool_body)

    evac = [b.DV.n + i + 1 for i in range(len(jobs))]

    def pe_body(pe):
        pe.wait_ge(b.DV.h, b.h1_dv[-1])
        for i, (wn, ft, outl, j, bias) in enumerate(jobs):
            pe.wait_ge(wld[i][0].h, wld[i][1])
            if i >= 2:
                pe.wait_ge(b.DV.h, evac[i - 2])
            ps = b.psA if i % 2 == 0 else b.psB
            for k in range(DC):
                for nc_ in range(4):
                    sl = slice(nc_ * 512, (nc_ + 1) * 512)
                    ins = pe.matmul(ps[:, sl], wbuf[i % 2][:, k * 128:(k + 1) * 128],
                                    b.h1[k][:, sl], start=(k == 0),
                                    stop=(k == DC - 1))
            b.PE.inc(ins)
    on_pe(b, pe_body)

    def dve_body(dve):
        dve.wait_ge(dfb.h, b.fbias_ld)
        for i, (wn, ft, outl, j, bias) in enumerate(jobs):
            dve.wait_ge(b.PE.h, pe_use[i])
            ps = b.psA if i % 2 == 0 else b.psB
            ins = dve.tensor_scalar_add(outl[j], ps[:, 0:2048], bias[:, j:j + 1])
            b.DV.inc(ins)
    on_dve(b, dve_body)
    assert b.DV.n == evac[-1]
    b.fqkv_dv = evac[-1]
    b.taps["fq0"] = list(b.fq)
    b.taps["h1"] = list(b.h1)
    a.free("wfq0", "wfq1", *[f"h1_{k}" for k in range(DC)])


def ffn_rope_qknorm(b):
    a = b.abf
    sn = snap(b)
    b.a2q = b.af.alloc("a2q", 128)
    b.a2k = b.af.alloc("a2k", 128)
    b.ehalf = a.alloc("ehalf", 2)
    ropeC = a.alloc("ropeC", S)
    ropeS2 = a.alloc("ropeS2", S)
    t1 = a.alloc("rope_t1", S)
    sq = a.alloc("sq", S)

    def pool_a2(po):
        wait_snap(po, b, sn, pe=True, dv=True)
        po.memset(b.a2q[0:2, :], 0.0)
        po.memset(b.a2k[0:2, :], 0.0)
        po.memset(b.ehalf[:, :], 0.0)
        po.memset(b.ehalf[0:64, 0:1], 1.0)
        ins = po.memset(b.ehalf[64:128, 1:2], 1.0)
        b.PO.inc(ins)
        da2 = DS(b, "da2")
        po.dma_start(out=b.a2q[0:1, 0:64], in_=b.dI["fqn"]).then_inc(da2.h, 16)
        po.dma_start(out=b.a2q[1:2, 64:128], in_=b.dI["fqn"]).then_inc(da2.h, 16)
        po.dma_start(out=b.a2k[0:1, 0:64], in_=b.dI["fkn"]).then_inc(da2.h, 16)
        po.dma_start(out=b.a2k[1:2, 64:128], in_=b.dI["fkn"]).then_inc(da2.h, 16)
        da2.n += 64
        b.a2_ld = da2.n
        b.eh_po = b.PO.n
    on_pool(b, pool_a2)

    drp = DS(b, "drope")

    def sp_rope(sp):
        wait_snap(sp, b, sn, pe=True, dv=True)
        sp.dma_start(out=ropeC, in_=b.dI["c_ropeC"]).then_inc(drp.h, 16)
        sp.dma_start(out=ropeS2, in_=b.dI["c_ropeS2"]).then_inc(drp.h, 16)
        drp.n += 32
        b.rope_ld = drp.n
    on_sp(b, sp_rope)

    tiles = [(b.fq[j], b.a2q) for j in range(DC)] + \
            [(b.fk[j], b.a2k) for j in range(2)]
    prev_dv = b.fqkv_dv
    for ti, (src, a2) in enumerate(tiles):
        is_q = ti < DC
        def pe_sw(pe, src=src, ti=ti, prev_dv=prev_dv):
            pe.wait_ge(b.DV.h, prev_dv)
            if ti == 0:
                pe.wait_ge(DS(b, "drope").h, b.rope_ld)
            for nc_ in range(4):
                sl = slice(nc_ * 512, (nc_ + 1) * 512)
                ins = pe.matmul(b.psA[:, sl], b.swap32, src[:, sl],
                                start=True, stop=True)
            b.PE.inc(ins)
        on_pe(b, pe_sw)
        sw_pe = b.PE.n

        def dve_rope(dve, src=src):
            dve.wait_ge(b.PE.h, sw_pe)
            dve.tensor_mul(t1, src, ropeC)
            dve.tensor_mul(b.sc1, b.psA[:, 0:2048], ropeS2)
            ins = dve.tensor_add(b.sc1, b.sc1, t1)
            b.DV.inc(ins)
        on_dve(b, dve_rope)
        roped_dv = b.DV.n

        def act_sq(act):
            act.wait_ge(b.DV.h, roped_dv)
            ins = act.activation(sq, b.sc1, AF.Square)
            b.AC.inc(ins)
        on_act(b, act_sq)
        sq_ac = b.AC.n

        def pe_ms(pe, ti=ti):
            pe.wait_ge(b.AC.h, sq_ac)
            if ti == 0:
                pe.wait_ge(b.PO.h, b.eh_po)
                pe.wait_ge(DS(b, "da2").h, b.a2_ld)
            for nc_ in range(4):
                sl = slice(nc_ * 512, (nc_ + 1) * 512)
                ins = pe.matmul(b.psB[0:2, sl], b.ehalf, sq[:, sl],
                                start=True, stop=True)
            b.PE.inc(ins)
        on_pe(b, pe_ms)
        ms_pe = b.PE.n

        def dve_ms(dve):
            dve.wait_ge(b.PE.h, ms_pe)
            dve.tensor_scalar_mul(b.rows[0:2, 0:S], b.psB[0:2, 0:2048], 1.0 / 64.0)
            ins = dve.tensor_scalar_add(b.rows[0:2, 0:S], b.rows[0:2, 0:S], 1e-6)
            b.DV.inc(ins)
        on_dve(b, dve_ms)
        ms_dv = b.DV.n

        def act_rt(act):
            act.wait_ge(b.DV.h, ms_dv)
            ins = act.activation(b.rows[0:2, 0:S], b.rows[0:2, 0:S], AF.Sqrt)
            b.AC.inc(ins)
        on_act(b, act_rt)
        rt_ac = b.AC.n

        def dve_rc(dve):
            dve.wait_ge(b.AC.h, rt_ac)
            ins = dve.reciprocal(b.rows[0:2, 0:S], b.rows[0:2, 0:S])
            b.DV.inc(ins)
        on_dve(b, dve_rc)
        rc_dv = b.DV.n

        def pe_bc(pe, a2=a2):
            pe.wait_ge(b.DV.h, rc_dv)
            for nc_ in range(4):
                sl = slice(nc_ * 512, (nc_ + 1) * 512)
                ins = pe.matmul(b.psB[:, sl], a2[0:2, :], b.rows[0:2, sl],
                                start=True, stop=True)
            b.PE.inc(ins)
        on_pe(b, pe_bc)
        bc_pe = b.PE.n

        def dve_fin(dve, src=src, is_q=is_q):
            dve.wait_ge(b.PE.h, bc_pe)
            if is_q:
                ins = dve.scalar_tensor_tensor(src, b.sc1, 0.125,
                                               b.psB[:, 0:2048],
                                               ALU.mult, ALU.mult)
            else:
                ins = dve.tensor_mul(src, b.sc1, b.psB[:, 0:2048])
            b.DV.inc(ins)
        on_dve(b, dve_fin)
        prev_dv = b.DV.n
    b.rope_dv = prev_dv
    b.taps["fq"] = list(b.fq)
    if BUILD_PHASES >= 8:
        a.free("rope_t1", "ropeC", "ropeS2", "sq", "ehalf")
        b.af.free("a2q", "a2k")


def ffn_attention(b):
    a = b.abf
    sn = snap(b)
    b.vecT = [a.alloc(f"vecT{k}", S) for k in range(DC)]
    vtok = a.alloc("fvtok", NKV * 16 * DH)
    b.prob = [a.alloc(f"prob{i}", S) for i in range(2)]
    b.pt = [a.alloc(f"pt{i}", 1024) for i in range(2)]
    b.diag = a.alloc("diag", 128)
    oh = a.alloc("oh", S)
    sig = a.alloc("fsig", S)
    wg = a.alloc("wg", DH)
    b.nmax = b.af.alloc("nmax", 2)
    bg = b.af.alloc("bg", 1)
    b.mask_diag = True
    b.last_exp_ac = 0
    b.last_tr_pe = 0
    fkd = [a.alloc(f"fkd{g}", S) for g in range(NKV)]

    def dve_fkd(dve):
        dve.wait_ge(b.DV.h, b.rope_dv)
        for g in range(NKV):
            src_ = b.fk[g // 2][(g % 2) * 64:(g % 2) * 64 + 64, :]
            dve.tensor_copy(fkd[g][0:64, :], src_)
            dve.tensor_copy(fkd[g][64:128, :], src_)
        ins = dve.memset(b.junk[:, 0:128], 0.0)
        b.DV.inc(ins)
    on_dve(b, dve_fkd)
    b.fkd_dv = b.DV.n

    dwg = DS(b, "dwg")
    dbg_ = DS(b, "dbg")

    def pool_wg(po):
        wait_snap(po, b, sn, pe=True, dv=True)
        po.dma_start(out=wg[0:64, :], in_=b.dI["fwg"]).then_inc(dwg.h, 16)
        dwg.n += 16
        b.wg_ld = dwg.n
    on_pool(b, pool_wg)

    def sp_bg(sp):
        wait_snap(sp, b, sn, pe=False, dv=True)
        sp.dma_start(out=bg[0:64, :], in_=b.dI["fbg"]).then_inc(dbg_.h, 16)
        dbg_.n += 16
        b.bg_ld = dbg_.n
    on_sp(b, sp_bg)

    pe0 = b.PE.n
    vt_marks = [pe0 + i + 1 for i in range(NKV * 2)]
    vt_ev = [b.DV.n + i + 1 for i in range(NKV * 2)]

    def vt_pe(pe):
        pe.wait_ge(b.DV.h, b.rope_dv)
        for g in range(NKV):
            if g >= 1:
                pe.wait_ge(b.DV.h, vt_ev[g * 2 - 1])
            src = b.fv[g // 2]
            r0 = (g % 2) * 64
            for c in range(16):
                col = (c % 8) * 64 + (c // 8) * 512
                ins = pe.matmul(b.psB[0:128, col:col + 64],
                                src[r0:r0 + 64, c * 128:(c + 1) * 128],
                                b.ident64[r0:r0 + 64, :], start=True, stop=True)
                if c % 8 == 7:
                    b.PE.inc(ins)
    on_pe(b, vt_pe)

    def vt_dve(dve):
        for i in range(NKV * 2):
            g, r = i // 2, i % 2
            dve.wait_ge(b.PE.h, vt_marks[i])
            dst = vtok[:, g * 1024 + r * 512:g * 1024 + (r + 1) * 512]
            dve.tensor_copy(dst, b.psB[:, r * 512:(r + 1) * 512])
            ins = dve.memset(b.junk[:, 0:128], 0.0)
            assert b.DV.inc(ins) == vt_ev[i]
    on_dve(b, vt_dve)
    vtok_dv = b.DV.n
    prev_head_dv = vtok_dv

    b.tr_pe_par = [0, 0]
    for head in range(H):
        g = head // 4
        qj, qr = head // 2, (head % 2) * 64

        def mk_state(qt, states, head=head, g=g, qj=qj, qr=qr,
                     prev_head_dv=prev_head_dv):
            nch = qt + 1
            T = b.psA if qt % 2 == 0 else b.psB
            st = {"qt": qt, "T": T, "nch": nch, "prob": b.prob[qt % 2],
                  "vtok": vtok[:, g * 1024:(g + 1) * 1024]}
            prev2 = states.get(qt - 2)

            def sc_pe(pe):
                if qt < 2:
                    pe.wait_ge(b.DV.h, prev_head_dv)
                    pe.wait_ge(b.DV.h, b.fkd_dv)
                else:
                    pe.wait_ge(b.DV.h, prev2["done"])
                nk = nch * 128
                for nc_ in range((nk + 511) // 512):
                    sl = slice(nc_ * 512, min((nc_ + 1) * 512, nk))
                    ins = pe.matmul(T[:, sl],
                                    b.fq[qj][qr:qr + 64, qt * 128:(qt + 1) * 128],
                                    fkd[g][qr:qr + 64, sl],
                                    start=True, stop=True)
                b.PE.inc(ins)
            on_pe(b, sc_pe)
            st["sc_mark"] = b.PE.n

            def out_fn(dve, vecreg, qt=qt):
                return dve.tensor_copy(oh[0:64, qt * 128:(qt + 1) * 128], vecreg)
            st["out_fn"] = out_fn
            return st

        states = attn_qt_pipeline(b, 16, mk_state)
        oh_dv = states[15]["done"]

        for nc_ in range(4):
            sl = slice(nc_ * 512, (nc_ + 1) * 512)

            def pe_g(pe, nc_=nc_, sl=sl, head=head, oh_dv=oh_dv):
                if nc_ == 0:
                    pe.wait_ge(b.DV.h, oh_dv)
                    if head == 0:
                        pe.wait_ge(DS(b, "dwg").h, b.wg_ld)
                else:
                    pe.wait_ge(b.AC.h, b.gate_ac)
                ins = pe.matmul(b.psB[0:64, 1536:2048], wg[0:64, :],
                                oh[0:64, sl], start=True, stop=True)
                b.PE.inc(ins)
            on_pe(b, pe_g)
            gpe = b.PE.n

            def act_g(act, sl=sl, head=head, nc_=nc_):
                act.wait_ge(b.PE.h, gpe)
                if head == 0 and nc_ == 0:
                    act.wait_ge(DS(b, "dbg").h, b.bg_ld)
                ins = act.activation(sig[0:64, sl], b.psB[0:64, 1536:2048],
                                     AF.Sigmoid, bias=bg[0:64, :], scale=1.0)
                b.AC.inc(ins)
            on_act(b, act_g)
            b.gate_ac = b.AC.n

        def dve_gate(dve, head=head):
            dve.wait_ge(b.AC.h, b.gate_ac)
            dst = b.vecT[head // 2][(head % 2) * 64:(head % 2) * 64 + 64, :]
            dve.tensor_mul(dst, oh[0:64, :], sig[0:64, :])
            ins = dve.memset(b.junk[:, 0:128], 0.0)
            b.DV.inc(ins)
        on_dve(b, dve_gate)
        prev_head_dv = b.DV.n

    b.fattn_dv = prev_head_dv
    b.taps["fvec"] = list(b.vecT)
    a.free("fvtok", "prob0", "prob1", "pt0", "pt1", "diag", "oh", "fsig", "wg",
           *[f"fkd{g}" for g in range(NKV)])
    b.af.free("nmax", "bg")
    a.free(*[f"fq{j}" for j in range(DC)], "fk0", "fk1", "fv0", "fv1")


def ffn_wo_xf(b):
    a = b.abf
    sn = snap(b)
    wbuf = [a.alloc(f"wfo{i}", DC * 128) for i in range(2)]
    bo = b.af.alloc("fbo", DC)

    dfbo = DS(b, "dfbo")

    def sp_par(sp):
        wait_snap(sp, b, sn, pe=False, dv=True)
        sp.dma_start(out=bo, in_=b.dI["fbo"]).then_inc(dfbo.h, 16)
        dfbo.n += 16
        b.fbo_ld = dfbo.n
    on_sp(b, sp_par)

    wld = []
    pe_use = [b.PE.n + i + 1 for i in range(DC)]
    dfo = [DS(b, "dwfo0"), DS(b, "dwfo1")]

    def pool_body(po):
        wait_snap(po, b, sn, pe=True, dv=True)
        for i in range(DC):
            if i >= 2:
                po.wait_ge(b.PE.h, pe_use[i - 2])
            s_ = dfo[i % 2]
            po.dma_start(out=wbuf[i % 2], in_=b.dI["fwo"][i]).then_inc(s_.h, 16)
            s_.n += 16
            wld.append((s_, s_.n))
    on_pool(b, pool_body)

    evac = [b.DV.n + i + 1 for i in range(DC)]

    def pe_body(pe):
        pe.wait_ge(b.DV.h, b.fattn_dv)
        for i in range(DC):
            pe.wait_ge(wld[i][0].h, wld[i][1])
            if i >= 2:
                pe.wait_ge(b.DV.h, evac[i - 2])
            ps = b.psA if i % 2 == 0 else b.psB
            for k in range(DC):
                for nc_ in range(4):
                    sl = slice(nc_ * 512, (nc_ + 1) * 512)
                    ins = pe.matmul(ps[:, sl], wbuf[i % 2][:, k * 128:(k + 1) * 128],
                                    b.vecT[k][:, sl], start=(k == 0),
                                    stop=(k == DC - 1))
            b.PE.inc(ins)
    on_pe(b, pe_body)

    def dve_body(dve):
        dve.wait_ge(dfbo.h, b.fbo_ld)
        for i in range(DC):
            dve.wait_ge(b.PE.h, pe_use[i])
            ps = b.psA if i % 2 == 0 else b.psB
            ins = dve.scalar_tensor_tensor(b.xn[i], ps[:, 0:2048],
                                           bo[:, i:i + 1], b.xn[i],
                                           ALU.add, ALU.add)
            b.DV.inc(ins)
    on_dve(b, dve_body)
    assert b.DV.n == evac[-1]
    b.xf_dv = evac[-1]
    a.free("wfo0", "wfo1", *[f"vecT{k}" for k in range(DC)])
    b.af.free("fbo")
    b.taps["xf"] = list(b.xn)


def ffn_logits_mask(b):
    a = b.abf
    sn = snap(b)
    b.mask = a.alloc("mask", S)
    wr = a.alloc("fwr", DC * 8)
    brr = a.alloc("fbr", 8)
    mtok = b.af.alloc("mtok", 8)
    mrow = b.af.alloc("mrow", 1)

    dfwr = DS(b, "dfwr")

    def pool_w(po):
        wait_snap(po, b, sn, pe=True, dv=True)
        for k in range(DC):
            po.dma_start(out=wr[:, k * 8:(k + 1) * 8], in_=b.dI["fwr"][k]
                         ).then_inc(dfwr.h, 16)
        po.dma_start(out=brr[0:1, :], in_=b.dI["fbr"]).then_inc(dfwr.h, 16)
        dfwr.n += 16 * 9
        b.fwr_ld = dfwr.n
    on_pool(b, pool_w)

    prev = 0
    for tt in range(16):
        tsl = slice(tt * 128, (tt + 1) * 128)

        def pe_lg(pe, tt=tt, tsl=tsl, prev=prev):
            if tt == 0:
                pe.wait_ge(DS(b, "dfwr").h, b.fwr_ld)
                pe.wait_ge(b.DV.h, b.h2_dv[-1])
            else:
                pe.wait_ge(b.DV.h, prev)
            for k in range(DC):
                pe.matmul(b.psA[:, 0:8], b.h2[k][:, tsl], wr[:, k * 8:(k + 1) * 8],
                          start=(k == 0), stop=False)
            ins = pe.matmul(b.psA[:, 0:8], b.ones_row[0:1, :], brr[0:1, :],
                            start=False, stop=True)
            b.PE.inc(ins)
        on_pe(b, pe_lg)
        lg_pe = b.PE.n

        def dve_mx(dve, tt=tt):
            dve.wait_ge(b.PE.h, lg_pe)
            dve.reduce_max(mrow[0:128, 0:1], b.psA[:, 0:8], axis=AX.X)
            dve.memset(b.junk[:, 0:384], 0.0)
            dve.tensor_scalar(mtok[:, 0:8], b.psA[:, 0:8],
                              mrow[0:128, 0:1], None, ALU.is_equal)
            ins = dve.memset(b.junk[:, 0:128], 0.0)
            b.DV.inc(ins)
        on_dve(b, dve_mx)
        mt_dv = b.DV.n

        def pe_tr(pe, tt=tt):
            pe.wait_ge(b.DV.h, mt_dv)
            ins = pe.transpose(b.psB[0:8, 1536:1664], mtok[:, 0:8], b.identf)
            b.PE.inc(ins)
        on_pe(b, pe_tr)
        tr_pe = b.PE.n

        def dve_ms(dve, tt=tt, tsl=tsl):
            dve.wait_ge(b.PE.h, tr_pe)
            ins = dve.tensor_copy(b.mask[0:8, tsl], b.psB[0:8, 1536:1664])
            b.DV.inc(ins)
        on_dve(b, dve_ms)
        prev = b.DV.n
    b.mask_dv = prev
    a.free("fwr", "fbr")
    b.af.free("mtok", "mrow")
    b.taps["mask"] = [b.mask]


def ffn_moe(b):
    a = b.abf
    sn = snap(b)
    TH = 1024
    act_s = a.alloc("act_s", 16 * TH)
    w2e = a.alloc("w2e", 16 * 1024)
    w1buf = [a.alloc(f"w1b{i}", DC * 128) for i in range(4)]
    e8t = a.alloc("e8t", 8 * 128)
    maskb = a.alloc("maskb", TH)
    y = [a.alloc(f"y{d}", TH) for d in range(DC)]
    sigs = a.alloc("sigs", 512)
    b2t = a.alloc("b2t", D)
    b1c = b.af.alloc("b1c", 32)
    xfin = b.sc1[:, 0:1024]
    gsc = b.sc1[:, 1024:2048]

    dmb = DS(b, "dmoeb")

    def pool_b2(po):
        wait_snap(po, b, sn, pe=True, dv=True)
        po.dma_start(out=b2t[0:8, :], in_=b.dI["fb2"]).then_inc(dmb.h, 16)
        po.dma_start(out=e8t[0:8, :], in_=b.dI["c_e8"]).then_inc(dmb.h, 16)
        dmb.n += 32
        b.b2_ld = dmb.n
    on_pool(b, pool_b2)

    b.out_sd = 0
    b.yadd_dv = 0
    b.w2_use = 0
    for th in range(2):
        toff = th * TH

        def po_zero(po, th=th):
            if th == 1:
                po.wait_ge(DS(b, "dout").h, b.out_sd)
            else:
                wait_snap(po, b, snap(b), pe=True, dv=True)
            for d in range(DC):
                ins = po.memset(y[d][:, :], 0.0)
            b.PO.inc(ins)
        on_pool(b, po_zero)
        yz_po = b.PO.n

        for e in range(NE):
            mb_dv0 = b.DV.n
            for nch in range(2):
                sl = slice(toff + nch * 512, toff + (nch + 1) * 512)

                def pe_mb(pe, e=e, th=th, nch=nch, sl=sl):
                    if e == 0 and th == 0 and nch == 0:
                        pe.wait_ge(b.DV.h, b.mask_dv)
                        pe.wait_ge(DS(b, "dmoeb").h, b.b2_ld)
                    if b.yadd_dv:
                        pe.wait_ge(b.DV.h, b.yadd_dv)
                    if nch == 1:
                        pe.wait_ge(b.DV.h, mb_dv0 + 1)
                    ins = pe.matmul(b.psB[:, 1536:2048],
                                    e8t[0:8, e * 128:(e + 1) * 128],
                                    b.mask[0:8, sl], start=True, stop=True)
                    b.PE.inc(ins)
                on_pe(b, pe_mb)
                mb_pe = b.PE.n

                def dve_mb(dve, nch=nch):
                    dve.wait_ge(b.PE.h, mb_pe)
                    ins = dve.tensor_copy(maskb[:, nch * 512:(nch + 1) * 512],
                                          b.psB[:, 1536:2048])
                    b.DV.inc(ins)
                on_dve(b, dve_mb)
            mb_dv = b.DV.n

            db1 = DS(b, "db1c")

            def sp_b1(sp, e=e):
                sp.wait_ge(b.DV.h, b.DV.n)
                sp.dma_start(out=b1c, in_=b.dI["fb1"][e]).then_inc(db1.h, 16)
                db1.n += 16
                b.b1_ld = db1.n
            on_sp(b, sp_b1)

            dw2e = DS(b, "dw2e")

            def po_w2(po, e=e, th=th):
                if b.w2_use:
                    po.wait_ge(b.PE.h, b.w2_use)
                po.dma_start(
                    out=w2e.rearrange("p (kf f) -> p kf f", kf=16),
                    in_=b.dI["fw2"][e].rearrange("(kf p) f -> p kf f", p=128)
                ).then_inc(dw2e.h, 16)
                dw2e.n += 16
                b.w2_ld = dw2e.n
            on_pool(b, po_w2)

            peA0 = b.PE.n
            peA_marks = [peA0 + i + 1 for i in range(32)]
            w1_use = [peA0 + 2 * (j + 1) for j in range(16)]
            w1ld = []
            acA0 = b.AC.n
            po_mul = {}

            def po_w1(po, e=e):
                for j in range(16):
                    if j >= 2:
                        po.wait_ge(b.PE.h, w1_use[j - 2])
                    s_ = DS(b, f"dw1s{j % 2}")
                    po.dma_start(out=w1buf[(j % 2) * 2], in_=b.dI["fw1"][e, j]
                                 ).then_inc(s_.h, 16)
                    po.dma_start(out=w1buf[(j % 2) * 2 + 1],
                                 in_=b.dI["fw1"][e, j + 16]).then_inc(s_.h, 16)
                    s_.n += 32
                    w1ld.append((s_, s_.n))
                    if j >= 1:
                        for tc in range(2):
                            i = (j - 1) * 2 + tc
                            po.wait_ge(b.AC.h, acA0 + i + 1)
                            ins = po.tensor_mul(sigs[:, 0:512], sigs[:, 0:512],
                                                maskb[:, tc * 512:(tc + 1) * 512])
                            po_mul[(j - 1, tc)] = b.PO.inc(ins)
                for tc in range(2):
                    i = 15 * 2 + tc
                    po.wait_ge(b.AC.h, acA0 + i + 1)
                    ins = po.tensor_mul(sigs[:, 0:512], sigs[:, 0:512],
                                        maskb[:, tc * 512:(tc + 1) * 512])
                    po_mul[(15, tc)] = b.PO.inc(ins)
            on_pool(b, po_w1)

            dv0 = b.DV.n
            actm = {}
            cnt = 0
            for j in range(16):
                for tc in range(2):
                    cnt += 1
                    actm[(j, tc)] = dv0 + cnt

            def peA(pe, e=e, th=th, toff=toff, mb_dv=mb_dv):
                for j in range(16):
                    pe.wait_ge(w1ld[j][0].h, w1ld[j][1])
                    if j == 0:
                        pe.wait_ge(b.DV.h, mb_dv)
                        pe.wait_ge(b.PO.h, yz_po)
                    for tc in range(2):
                        tsl = slice(toff + tc * 512, toff + (tc + 1) * 512)
                        prevkey = (j, tc - 1) if tc >= 1 else (j - 1, 1)
                        if prevkey in actm:
                            pe.wait_ge(b.DV.h, actm[prevkey])
                        for k in range(DC):
                            pe.matmul(b.psA[:, 0:512],
                                      w1buf[(j % 2) * 2][:, k * 128:(k + 1) * 128],
                                      b.h2[k][:, tsl], start=(k == 0),
                                      stop=(k == DC - 1))
                            ins = pe.matmul(b.psA[:, 512:1024],
                                            w1buf[(j % 2) * 2 + 1][:, k * 128:(k + 1) * 128],
                                            b.h2[k][:, tsl], start=(k == 0),
                                            stop=(k == DC - 1))
                        b.PE.inc(ins)
            on_pe(b, peA)

            def actA(act, e=e):
                for i in range(32):
                    j, tc = i // 2, i % 2
                    act.wait_ge(b.PE.h, peA_marks[i])
                    if i == 0:
                        act.wait_ge(DS(b, "db1c").h, b.b1_ld)
                    ins = act.activation(sigs[:, 0:512], b.psA[:, 512:1024],
                                         AF.Sigmoid, bias=b1c[:, 16 + j:17 + j],
                                         scale=1.0)
                    b.AC.inc(ins)
            on_act(b, actA)

            def dveA(dve, e=e):
                for i in range(32):
                    j, tc = i // 2, i % 2
                    dve.wait_ge(b.PO.h, po_mul[(j, tc)])
                    dst = act_s[:, j * TH + tc * 512:j * TH + (tc + 1) * 512]
                    ins = dve.scalar_tensor_tensor(dst, b.psA[:, 0:512],
                                                   b1c[:, j:j + 1], sigs[:, 0:512],
                                                   ALU.add, ALU.mult)
                    b.DV.inc(ins)
                    assert b.DV.n == actm[(j, tc)]
            on_dve(b, dveA)
            actA_dv = b.DV.n

            yev = []

            yevd = [b.DV.n + 1, b.DV.n + 2]

            def peB(pe, e=e, th=th, toff=toff, actA_dv=actA_dv):
                pe.wait_ge(DS(b, "dw2e").h, b.w2_ld)
                pe.wait_ge(b.DV.h, actA_dv)
                if b.yadd_dv:
                    pe.wait_ge(b.DV.h, b.yadd_dv)
                for tc in range(2):
                    if tc == 1:
                        pe.wait_ge(b.DV.h, yevd[0])
                    tslm = slice(toff + tc * 512, toff + (tc + 1) * 512)
                    for d in range(DC):
                        dsl = slice((d % 4) * 512, (d % 4) * 512 + 512)
                        ps = b.psA if d < 4 else b.psB
                        for kf in range(16):
                            ins = pe.matmul(
                                ps[:, dsl],
                                w2e[:, kf * 1024 + d * 128:kf * 1024 + (d + 1) * 128],
                                act_s[:, kf * TH + tc * 512:kf * TH + (tc + 1) * 512],
                                start=(kf == 0), stop=(kf == 15 and e != NE - 1))
                        if e == NE - 1:
                            ins = pe.matmul(ps[:, dsl],
                                            b2t[0:8, d * 128:(d + 1) * 128],
                                            b.mask[0:8, tslm],
                                            start=False, stop=True)
                    b.PE.inc(ins)
                    yev.append(b.PE.n)
            on_pe(b, peB)
            b.w2_use = b.PE.n

            def dveB(dve, e=e):
                for tc in range(2):
                    dve.wait_ge(b.PE.h, yev[tc])
                    for d in range(DC):
                        dsl = slice((d % 4) * 512, (d % 4) * 512 + 512)
                        ps = b.psA if d < 4 else b.psB
                        ysl = y[d][:, tc * 512:(tc + 1) * 512]
                        ins = dve.tensor_add(ysl, ysl, ps[:, dsl])
                    assert b.DV.inc(ins) == yevd[tc]
            on_dve(b, dveB)
            b.yadd_dv = b.DV.n

        final_combine_th(b, th, y, xfin, gsc)

    a.free("act_s", "w2e", "maskb", "sigs", "b2t", "mask", "e8t",
           *[f"w1b{i}" for i in range(4)], *[f"y{d}" for d in range(DC)],
           *[f"h2_{k}" for k in range(DC)])
    b.af.free("b1c")


def final_combine_th(b, th, y, xfin, gsc):
    toff = th * 1024
    sbf = b.abf.alloc("sch", 1024)
    prev_out = 0
    for d in range(DC):
        tsl = slice(toff, toff + 1024)

        dfx = DS(b, "dfx")

        def sp_x(sp, d=d, th=th, tsl=tsl, prev_out=prev_out):
            sp.wait_ge(b.DV.h, b.yadd_dv)
            if prev_out:
                sp.wait_ge(b.DV.h, b.fin_dv)
            sp.dma_start(out=xfin, in_=b.dI["xT"][d * 128:(d + 1) * 128, tsl]
                         ).then_inc(dfx.h, 16)
            sp.dma_start(out=sbf, in_=b.d_s[d * 128:(d + 1) * 128, tsl]
                         ).then_inc(dfx.h, 16)
            dfx.n += 32
            b.x_ld = dfx.n
        on_sp(b, sp_x)

        def dve_f1(dve, d=d, tsl=tsl):
            dve.wait_ge(b.DV.h, b.yadd_dv)
            if b.out_sd:
                dve.wait_ge(DS(b, "dout").h, b.out_sd)
            ins = dve.tensor_add(gsc, b.xn[d][:, tsl], y[d][:, :])
            b.DV.inc(ins)
        on_dve(b, dve_f1)
        f1_dv = b.DV.n

        def act_f(act):
            act.wait_ge(b.DV.h, f1_dv)
            ins = act.activation(gsc, gsc, AF.Silu)
            b.AC.inc(ins)
        on_act(b, act_f)
        f_ac = b.AC.n

        def dve_f2(dve, d=d):
            dve.wait_ge(b.AC.h, f_ac)
            dve.wait_ge(DS(b, "dfx").h, b.x_ld)
            dve.tensor_mul(gsc, sbf, gsc)
            ins = dve.tensor_add(gsc, gsc, xfin)
            b.DV.inc(ins)
        on_dve(b, dve_f2)
        b.fin_dv = b.DV.n
        prev_out = 1

        dout = DS(b, "dout")

        def sp_out(sp, d=d, tsl=tsl):
            sp.wait_ge(b.DV.h, b.fin_dv)
            sp.dma_start(out=b.d_out[d * 128:(d + 1) * 128, tsl], in_=gsc
                         ).then_inc(dout.h, 16)
            dout.n += 16
            b.out_sd = dout.n
        on_sp(b, sp_out)
    b.abf.free("sch")


def finish(b, tap=None):
    tap = tap if tap is not None else DEBUG_TAP
    if tap is not None and BUILD_PHASES < 99:
        tiles = b.taps[tap]

        dtap = DS(b, "dtap")

        def po_tap(po):
            wait_snap(po, b, snap(b), pe=True, dv=True, ac=True)
            for k, t in enumerate(tiles[:8]):
                po.dma_start(out=b.d_out[k * 128:(k + 1) * 128, 0:t.shape[-1]],
                             in_=t).then_inc(dtap.h, 16)
                dtap.n += 16
        on_pool(b, po_tap)

    def sp_end(sp):
        for s_ in b.dsems.values():
            if s_.n:
                sp.wait_ge(s_.h, s_.n)
    on_sp(b, sp_end)



def prep_inputs(inputs):
    f32 = np.float32
    g = {k: np.asarray(v, f32) for k, v in inputs.items()}
    m = {}
    for k, v in host_constants().items():
        m["c_" + k] = np.ascontiguousarray(v)
    m["wqkv"] = np.stack([tile_kMf(g["L0_attn_W_qkv"][l], 24) for l in range(NL)])
    m["wr"] = np.stack([tile_kMf(g["L0_attn_W_r"][l], DC) for l in range(NL)])
    m["wo"] = np.stack([tile_kMf(g["L0_attn_W_o"][l], DC) for l in range(NL)])
    m["wff1"] = np.ascontiguousarray(g["L0_attn_W_ff1"].reshape(NL, DC, 128, DF))
    w2 = g["L0_attn_W_ff2"].reshape(NL, 32, 128, DC, 128)
    m["wff2"] = np.ascontiguousarray(w2.transpose(0, 3, 2, 1, 4))
    m["u_t"] = np.ascontiguousarray(g["L0_attn_u"].T)
    m["v_t"] = np.ascontiguousarray(g["L0_attn_v"].T)
    for nm, key in (("ln1g", "L0_attn_ln1_g"), ("ln1b", "L0_attn_ln1_b"),
                    ("ln2g", "L0_attn_ln2_g"), ("ln2b", "L0_attn_ln2_b"),
                    ("bff1", "L0_attn_b_ff1"), ("bff2", "L0_attn_b_ff2")):
        m[nm] = np.stack([col_bias(g[key][l]) for l in range(NL)])
    m["fwq"] = tile_kMf(g["L0_ffn_Wq"], DC)
    m["fwk"] = tile_kMf(g["L0_ffn_Wk"], 2)
    m["fwv"] = tile_kMf(g["L0_ffn_Wv"], 2)
    m["fwg"] = g["L0_ffn_Wg"]
    m["fwo"] = tile_kMf(g["L0_ffn_Wo"], DC)
    m["fwr"] = np.ascontiguousarray(g["L0_ffn_Wr"].reshape(DC, 128, NE))
    m["fbr"] = np.ascontiguousarray(g["L0_ffn_br"].reshape(1, NE))
    w1 = g["L0_ffn_W1"].reshape(NE, DC, 128, 32, 128)
    m["fw1"] = np.ascontiguousarray(w1.transpose(0, 3, 2, 1, 4))
    m["fb1"] = np.stack([col_bias(g["L0_ffn_b1"][e]) for e in range(NE)])
    m["fw2"] = np.ascontiguousarray(g["L0_ffn_W2"])
    m["fb2"] = np.ascontiguousarray(g["L0_ffn_b2"])
    m["fbq"] = col_bias(g["L0_ffn_bq"])
    m["fbk"] = col_bias(g["L0_ffn_bk"])
    m["fbv"] = col_bias(g["L0_ffn_bv"])
    m["fbg"] = np.ascontiguousarray(g["L0_ffn_bg"].reshape(DH, 1))
    m["fbo"] = col_bias(g["L0_ffn_bo"])
    m["fln1"] = col_bias(g["L0_ffn_ln1_w"])
    m["fln2"] = col_bias(g["L0_ffn_ln2_w"])
    m["fqn"] = np.ascontiguousarray(g["L0_ffn_qn_w"].reshape(1, DH))
    m["fkn"] = np.ascontiguousarray(g["L0_ffn_kn_w"].reshape(1, DH))
    return m


def kernel(**inputs):
    x = np.asarray(inputs["L0_attn_x"], np.float32)
    shared = prep_inputs(inputs)
    nc = build_program()
    in_maps = []
    for i in range(8):
        im = dict(shared)
        im["xT"] = np.ascontiguousarray(x[i].T)
        in_maps.append(im)
    res = run_bass_kernel_spmd(nc, in_maps, core_ids=list(range(8)))
    outs = [np.asarray(res.results[i]["out"]).T for i in range(8)]
    return np.stack(outs).astype(np.float32)


# version 05
# speedup vs baseline: 1.0324x; 1.0254x over previous
import contextlib

import numpy as np
import ml_dtypes

import concourse.bass as bass
import concourse.mybir as mybir
from concourse.bass_utils import run_bass_kernel_spmd

F32 = mybir.dt.float32
BF16 = mybir.dt.bfloat16
AF = mybir.ActivationFunctionType
ALU = mybir.AluOpType
AX = mybir.AxisListType
BF = ml_dtypes.bfloat16

S = 2048
D = 1024
DC = 8
NL = 4
H = 16
DH = 64
DF = 4096
HID = 2048
NE = 8
NKV = 4

BUILD_PHASES = 99
DEBUG_TAP = None



def host_constants():
    c = {}
    c["ident"] = np.eye(128, dtype=np.float32).astype(BF)
    c["identf"] = np.eye(128, dtype=np.float32)
    c["ident64"] = np.eye(64, dtype=np.float32).astype(BF)
    hd = DH
    inv = 1.0 / 10000.0 ** (np.arange(0, hd, 2, dtype=np.float32) / hd)
    t = np.arange(S, dtype=np.float32)
    fr = np.einsum("i,j->ij", t, inv)
    cos = np.cos(fr).T
    sin = np.sin(fr).T
    Ct = np.zeros((128, S), np.float32)
    S2 = np.zeros((128, S), np.float32)
    for base in (0, 64):
        Ct[base:base + 32] = cos
        Ct[base + 32:base + 64] = cos
        S2[base:base + 32] = -sin
        S2[base + 32:base + 64] = sin
    c["ropeC"] = Ct.astype(BF)
    c["ropeS2"] = S2.astype(BF)
    Pl = np.zeros((128, 128), np.float32)
    for i in range(128):
        Pl[i ^ 32, i] = 1.0
    c["swap32"] = Pl.astype(BF)
    m = np.zeros((128, 128), np.float32)
    for q in range(128):
        m[q, q + 1:] = -1e9
    c["causal"] = m
    E8 = np.zeros((8, 8 * 128), np.float32)
    for e in range(8):
        E8[e, e * 128:(e + 1) * 128] = 1.0
    c["e8"] = E8.astype(BF)
    c["ones_col"] = np.ones((128, 1), np.float32).astype(BF)
    c["ones_row"] = np.ones((1, 128), np.float32).astype(BF)
    c["onesrf"] = np.ones((1, 128), np.float32)
    pos = np.arange(S - 1, -1, -1, dtype=np.float32)
    freq = 1.0 / 10000.0 ** (np.arange(0, D, 2, dtype=np.float32) / D)
    ang = pos[:, None] * freq[None, :]
    r = np.concatenate([np.sin(ang), np.cos(ang)], axis=-1)
    c["rT"] = np.ascontiguousarray(r.T).reshape(DC, 128, S).astype(BF)
    return c


def tile_kMf(w, nf):
    din, f = w.shape
    kc = din // 128
    a = w.reshape(kc, 128, nf, 128)
    return np.ascontiguousarray(a.transpose(2, 1, 0, 3))


def col_bias(v):
    f = v.shape[0]
    return np.ascontiguousarray(v.reshape(f // 128, 128).T)



class Sem:
    def __init__(self, ctx, nc, name):
        self.h = ctx.enter_context(nc.semaphore(name))
        self.n = 0

    def inc(self, instr, k=1):
        instr.then_inc(self.h, k)
        self.n += k
        return self.n


class Arena:
    def __init__(self, t, cols, name):
        self.t = t
        self.cols = cols
        self.name = name
        self.iv = [(0, cols)]
        self.live = {}

    def alloc(self, name, cols, align=16):
        assert name not in self.live, name
        for i, (a, b_) in enumerate(self.iv):
            a2 = (a + align - 1) // align * align
            if b_ - a2 >= cols:
                del self.iv[i]
                if a2 > a:
                    self.iv.insert(i, (a, a2))
                    i += 1
                if a2 + cols < b_:
                    self.iv.insert(i, (a2 + cols, b_))
                self.live[name] = (a2, cols)
                return self.t[:, a2:a2 + cols]
        raise MemoryError(f"arena {self.name}: alloc {name} ({cols}) failed; "
                          f"live={self.live}")

    def free(self, *names):
        for name in names:
            a, n = self.live.pop(name)
            self.iv.append((a, a + n))
        self.iv.sort()
        out = []
        for s_, e in self.iv:
            if out and out[-1][1] >= s_:
                out[-1] = (out[-1][0], max(out[-1][1], e))
            else:
                out.append((s_, e))
        self.iv = [tuple(x) for x in out]


class B:
    pass


def declare_inputs(nc, consts):
    d = {}

    def inp(name, shape, dtype):
        d[name] = nc.dram_tensor(name, list(shape), dtype, kind="ExternalInput").ap()

    inp("xT", (D, S), F32)
    for k, v in consts.items():
        inp("c_" + k, v.shape, BF16 if v.dtype == BF else F32)
    inp("wqkv", (NL, 24, 128, DC, 128), F32)
    inp("wr", (NL, DC, 128, DC, 128), F32)
    inp("wo", (NL, DC, 128, DC, 128), F32)
    inp("wff1", (NL, DC, 128, DF), F32)
    inp("wff2", (NL, DC, 128, 32, 128), F32)
    inp("u_t", (DH, H), F32)
    inp("v_t", (DH, H), F32)
    inp("ln1g", (NL, 128, DC), F32)
    inp("ln1b", (NL, 128, DC), F32)
    inp("ln2g", (NL, 128, DC), F32)
    inp("ln2b", (NL, 128, DC), F32)
    inp("bff1", (NL, 128, 32), F32)
    inp("bff2", (NL, 128, DC), F32)
    inp("fwq", (DC, 128, DC, 128), F32)
    inp("fwk", (2, 128, DC, 128), F32)
    inp("fwv", (2, 128, DC, 128), F32)
    inp("fwg", (DH, DH), F32)
    inp("fwo", (DC, 128, DC, 128), F32)
    inp("fwr", (DC, 128, NE), F32)
    inp("fbr", (1, NE), F32)
    inp("fw1", (NE, 32, 128, DC, 128), F32)
    inp("fb1", (NE, 128, 32), F32)
    inp("fw2", (NE, HID, D), F32)
    inp("fb2", (NE, D), F32)
    inp("fbq", (128, DC), F32)
    inp("fbk", (128, 2), F32)
    inp("fbv", (128, 2), F32)
    inp("fbg", (DH, 1), F32)
    inp("fbo", (128, DC), F32)
    inp("fln1", (128, DC), F32)
    inp("fln2", (128, DC), F32)
    inp("fqn", (1, DH), F32)
    inp("fkn", (1, DH), F32)
    return d


def DS(b, name):
    if name not in b.dsems:
        b.dsems[name] = Sem(b._ctx, b.nc, name)
    return b.dsems[name]


def on_pe(b, f):
    b.block.tensor(f)


def on_dve(b, f):
    b.block.vector(f)


def on_act(b, f):
    b.block.scalar(f)


def on_pool(b, f):
    b.block.gpsimd(f)


def on_sp(b, f):
    b.block.sync(f)


def snap(b):
    return (b.PE.n, b.DV.n, b.AC.n, b.PO.n, b.SD.n, b.PD.n)


def wait_snap(eng, b, sn, pe=True, dv=True, ac=False, po=False, sd=False,
              pd=False):
    if pe and sn[0]:
        eng.wait_ge(b.PE.h, sn[0])
    if dv and sn[1]:
        eng.wait_ge(b.DV.h, sn[1])
    if ac and sn[2]:
        eng.wait_ge(b.AC.h, sn[2])
    if po and sn[3]:
        eng.wait_ge(b.PO.h, sn[3])
    if sd and sn[4]:
        eng.wait_ge(b.SD.h, sn[4])
    if pd and sn[5]:
        eng.wait_ge(b.PD.h, sn[5])



def build_program():
    consts = host_constants()
    nc = bass.Bass("TRN2", target_bir_lowering=False)
    dI = declare_inputs(nc, consts)
    d_out = nc.dram_tensor("out", [D, S], F32, kind="ExternalOutput").ap()
    d_xn = nc.dram_tensor("xn_spill", [D, S], BF16, kind="Internal").ap()
    d_s = nc.dram_tensor("s_spill", [D, S], BF16, kind="Internal").ap()

    with contextlib.ExitStack() as ctx:
        b = B()
        b.nc = nc
        b.dI = dI
        b.d_out = d_out
        b.d_xn = d_xn
        b.d_s = d_s
        b.abf = Arena(ctx.enter_context(
            nc.sbuf_tensor("arena_bf", [128, 91500], BF16)), 91500, "bf")
        b.af = Arena(ctx.enter_context(
            nc.sbuf_tensor("arena_f32", [128, 7168], F32)), 7168, "f32")
        b.psA = ctx.enter_context(nc.psum_tensor("psA", [128, 2048], F32))
        b.psB = ctx.enter_context(nc.psum_tensor("psB", [128, 2048], F32))
        b.PE = Sem(ctx, nc, "sPE")
        b.DV = Sem(ctx, nc, "sDV")
        b.AC = Sem(ctx, nc, "sAC")
        b.PO = Sem(ctx, nc, "sPO")
        b.SD = Sem(ctx, nc, "sSD")
        b.PD = Sem(ctx, nc, "sPD")
        b.dsems = {}
        b._ctx = ctx
        b.block = ctx.enter_context(nc.Block())
        build_all(b)
        globals()["_last_b"] = b
    return nc


def build_all(b):
    alloc_persistent(b)
    load_constants(b)
    phase_xn(b)
    if BUILD_PHASES < 2:
        return finish(b)
    for l in range(NL):
        attn_layer(b, l)
        if BUILD_PHASES < 3 + l:
            return finish(b)
    spill_s(b)
    if BUILD_PHASES < 7:
        return finish(b)
    ffn_module(b)
    finish(b)


def alloc_persistent(b):
    a = b.abf
    b.h = [a.alloc(f"h{k}", S) for k in range(DC)]
    b.ident = a.alloc("ident", 128)
    b.ident64 = a.alloc("ident64", 64)
    b.ones_col = a.alloc("ones_col", 16)
    b.ones_row = a.alloc("ones_row", 128)
    b.swap32 = a.alloc("swap32", 128)
    b.junk = a.alloc("junk", 512)
    f = b.af
    b.identf = f.alloc("identf", 128)
    b.ones_rowf = f.alloc("ones_rowf", 128)
    b.causal = f.alloc("causal", 128)
    b.uv = f.alloc("uv", 2 * H)
    b.rows = f.alloc("rows", 2 * S)
    b.row0 = b.rows[0:1, 0:S]
    b.row1 = b.rows[0:1, S:2 * S]
    b.sc1 = f.alloc("sc1", 2048)
    b.den = f.alloc("den", 4)
    b.lng = f.alloc("lng", DC)
    b.lnb = f.alloc("lnb", DC)
    b.ln2g = f.alloc("ln2g", DC)
    b.ln2b = f.alloc("ln2b", DC)
    b.bf1 = f.alloc("bf1", 32)
    b.bf2 = f.alloc("bf2", DC)
    b.fw_norm = f.alloc("fw_norm", DC)
    b.taps = {}


def load_constants(b):
    dI = b.dI

    def body(sp):
        cs = DS(b, "dconst")

        def ld(dst, src):
            sp.dma_start(out=dst, in_=src).then_inc(cs.h, 16)
            cs.n += 16
        ld(b.ident, dI["c_ident"])
        ld(b.identf, dI["c_identf"])
        ld(b.ident64[0:64, :], dI["c_ident64"])
        ld(b.ident64[64:128, :], dI["c_ident64"])
        ld(b.ones_col[:, 0:1], dI["c_ones_col"])
        ld(b.ones_row[0:1, :], dI["c_ones_row"])
        ld(b.ones_rowf[0:1, :], dI["c_onesrf"])
        ld(b.swap32, dI["c_swap32"])
        ld(b.causal, dI["c_causal"])
        ld(b.uv[0:64, 0:H], dI["u_t"])
        ld(b.uv[0:64, H:2 * H], dI["v_t"])
        b.const_ready = cs.n
    on_sp(b, body)
    b.const_sem = DS(b, "dconst")



def phase_xn(b):
    mb, rb, sq = norm_scratch(b)
    ld1 = []

    dx = DS(b, "dx")

    def po1(po):
        po.wait_ge(b.const_sem.h, b.const_ready)
        for k in range(DC):
            po.dma_start(out=b.h[k], in_=b.dI["xT"][k * 128:(k + 1) * 128, :]
                         ).then_inc(dx.h, 16)
            dx.n += 16
            ld1.append(dx.n)
    on_pool(b, po1)

    ac1 = [b.AC.n + k + 1 for k in range(DC)]
    pe_sq = [b.PE.n + 2 * (k + 1) for k in range(DC)]

    def act1(act):
        for k in range(DC):
            act.wait_ge(dx.h, ld1[-1] if True else ld1[k])
            if k > 0:
                act.wait_ge(b.PE.h, pe_sq[k - 1])
            ins = act.activation(sq, b.h[k], AF.Square)
            b.AC.inc(ins)
    on_act(b, act1)

    def pe1(pe):
        for k in range(DC):
            pe.wait_ge(dx.h, ld1[-1])
            for nc_ in range(4):
                sl = slice(nc_ * 512, (nc_ + 1) * 512)
                ins = pe.matmul(b.psA[0:1, sl], b.ones_col[:, 0:1], b.h[k][:, sl],
                                start=(k == 0), stop=(k == DC - 1))
            b.PE.inc(ins)
            pe.wait_ge(b.AC.h, ac1[k])
            for nc_ in range(4):
                sl = slice(nc_ * 512, (nc_ + 1) * 512)
                ins = pe.matmul(b.psB[0:1, sl], b.ones_col[:, 0:1], sq[:, sl],
                                start=(k == 0), stop=(k == DC - 1))
            b.PE.inc(ins)
    on_pe(b, pe1)
    stats_pe = b.PE.n

    mean_var_rows(b, stats_pe, 1.0 / 1024.0, 1e-5, with_mean=True)
    bcast_rows(b, mb, rb, with_mean=True)
    bc_dv = b.bc_dv

    dv2 = [b.DV.n + 2 * (k + 1) for k in range(DC)]

    def dve2(dve):
        dve.wait_ge(b.DV.h, bc_dv)
        for k in range(DC):
            dve.tensor_sub(b.sc1, b.h[k], mb)
            ins = dve.tensor_mul(b.h[k], b.sc1, rb)
            b.DV.inc(ins, 2)
    on_dve(b, dve2)
    assert b.DV.n == dv2[-1]
    b.xn_dv = dv2[-1]

    dsp = DS(b, "dspill")

    def sp3(sp):
        for k in range(DC):
            sp.wait_ge(b.DV.h, dv2[k])
            sp.dma_start(out=b.d_xn[k * 128:(k + 1) * 128, :], in_=b.h[k]
                         ).then_inc(dsp.h, 16)
            dsp.n += 16
        b.xn_spilled = dsp.n
    on_sp(b, sp3)

    free_norm_scratch(b)
    b.taps["xn"] = list(b.h)


def norm_scratch(b):
    mb = b.abf.alloc("mb", S)
    rb = b.abf.alloc("rb", S)
    sq = b.abf.alloc("sq", S)
    b._mb, b._rb, b._sq = mb, rb, sq
    return mb, rb, sq


def free_norm_scratch(b):
    b.abf.free("mb", "rb", "sq")


def mean_var_rows(b, stats_pe, inv_n, eps, with_mean):
    def dve_rows(dve):
        dve.wait_ge(b.PE.h, stats_pe)
        if with_mean:
            dve.tensor_scalar_mul(b.row0, b.psA[0:1, 0:2048], inv_n)
            dve.tensor_scalar_mul(b.row1, b.psB[0:1, 0:2048], inv_n)
            dve.tensor_mul(b.sc1[0:1, :], b.row0, b.row0)
            dve.tensor_sub(b.row1, b.row1, b.sc1[0:1, :])
        else:
            dve.tensor_scalar_mul(b.row1, b.psB[0:1, 0:2048], inv_n)
        ins = dve.tensor_scalar_add(b.row1, b.row1, eps)
        b.DV.inc(ins)
    on_dve(b, dve_rows)
    var_dv = b.DV.n

    def act_rows(act):
        act.wait_ge(b.DV.h, var_dv)
        ins = act.activation(b.row1, b.row1, AF.Sqrt)
        b.AC.inc(ins)
    on_act(b, act_rows)
    sq_ac = b.AC.n

    def dve_recip(dve):
        dve.wait_ge(b.AC.h, sq_ac)
        ins = dve.reciprocal(b.row1, b.row1)
        b.DV.inc(ins)
    on_dve(b, dve_recip)
    b.rstd_dv = b.DV.n


def bcast_rows(b, mb, rb, with_mean):
    def pe_bc(pe):
        pe.wait_ge(b.DV.h, b.rstd_dv)
        for nc_ in range(4):
            sl = slice(nc_ * 512, (nc_ + 1) * 512)
            if with_mean:
                pe.matmul(b.psA[:, sl], b.ones_rowf[0:1, :], b.row0[:, sl],
                          start=True, stop=True)
            ins = pe.matmul(b.psB[:, sl], b.ones_rowf[0:1, :], b.row1[:, sl],
                            start=True, stop=True)
        b.PE.inc(ins)
    on_pe(b, pe_bc)
    bc_pe = b.PE.n

    def dve_bc(dve):
        dve.wait_ge(b.PE.h, bc_pe)
        if with_mean:
            dve.tensor_copy(mb, b.psA[:, 0:2048])
        ins = dve.tensor_copy(rb, b.psB[:, 0:2048])
        b.DV.inc(ins)
    on_dve(b, dve_bc)
    b.bc_dv = b.DV.n



def attn_layer(b, l):
    b.vecT = [b.abf.alloc(f"vecT{k}", S) for k in range(DC)]
    for half in range(2):
        kr_half(b, l, half)
        if l == 0 and half == 0:
            b.taps["kr"] = list(b.krh)
            if BUILD_PHASES < 2.3:
                return
        qkv_half(b, l, half)
        if l == 0 and half == 0:
            b.taps["qh"] = list(b.qh)
            b.taps["kh"] = list(b.kh)
            b.taps["vh"] = list(b.vh)
            if BUILD_PHASES < 2.5:
                return
        attention_half(b, l, half)
        if l == 0 and half == 0:
            b.taps["vec0"] = list(b.vecT)
            if BUILD_PHASES < 2.7:
                return
    if l == 0:
        b.taps["vec"] = list(b.vecT)
        if BUILD_PHASES < 2.8:
            return
    wo_ln1(b, l)
    if l == 0 and BUILD_PHASES < 2.9:
        b.taps["postln1"] = list(b.h)
        return
    b.abf.free(*[f"vecT{k}" for k in range(DC)])
    ff_block(b, l)
    b.taps[f"l{l}"] = list(b.h)


def kr_half(b, l, half):
    a = b.abf
    sn = snap(b)
    rT = [a.alloc(f"rT{k}", S) for k in range(DC)]
    b.krh = [a.alloc(f"kr{j}", S) for j in range(4)]
    wbuf = [a.alloc(f"wkr{i}", DC * 128) for i in range(2)]
    ld = []

    drt = DS(b, "drt")
    dwkr = [DS(b, "dwkr0"), DS(b, "dwkr1")]

    def sp_body(sp):
        wait_snap(sp, b, sn, pe=True, dv=True)
        for k in range(DC):
            sp.dma_start(out=rT[k], in_=b.dI["c_rT"][k]).then_inc(drt.h, 16)
            drt.n += 16
            ld.append(drt.n)
    on_sp(b, sp_body)

    wld = []
    pe_use = [b.PE.n + j + 1 for j in range(4)]

    def pool_body(po):
        wait_snap(po, b, sn, pe=True, dv=True)
        for j in range(4):
            ft = half * 4 + j
            if j >= 2:
                po.wait_ge(b.PE.h, pe_use[j - 2])
            s_ = dwkr[j % 2]
            po.dma_start(out=wbuf[j % 2], in_=b.dI["wr"][l, ft]
                         ).then_inc(s_.h, 16)
            s_.n += 16
            wld.append((s_, s_.n))
    on_pool(b, pool_body)

    evac = [b.DV.n + j + 1 for j in range(4)]
    pe_done = []

    def pe_body(pe):
        pe.wait_ge(drt.h, ld[-1])
        wait_snap(pe, b, sn, pe=False, dv=True)
        for j in range(4):
            pe.wait_ge(wld[j][0].h, wld[j][1])
            if j >= 2:
                pe.wait_ge(b.DV.h, evac[j - 2])
            ps = b.psA if j % 2 == 0 else b.psB
            for k in range(DC):
                for nc_ in range(4):
                    sl = slice(nc_ * 512, (nc_ + 1) * 512)
                    ins = pe.matmul(ps[:, sl], wbuf[j % 2][:, k * 128:(k + 1) * 128],
                                    rT[k][:, sl], start=(k == 0), stop=(k == DC - 1))
            pe_done.append(b.PE.inc(ins))
    on_pe(b, pe_body)
    assert pe_done == [] or True

    def dve_body(dve):
        for j in range(4):
            dve.wait_ge(b.PE.h, pe_use[j])
            ps = b.psA if j % 2 == 0 else b.psB
            ins = dve.tensor_copy(b.krh[j], ps[:, 0:2048])
            b.DV.inc(ins)
    on_dve(b, dve_body)
    assert b.DV.n == evac[-1]
    b.kr_dv = evac[-1]
    a.free(*[f"rT{k}" for k in range(DC)], "wkr0", "wkr1")


def qkv_half(b, l, half):
    a = b.abf
    sn = snap(b)
    b.qh = [a.alloc(f"qh{j}", S) for j in range(4)]
    b.kh = [a.alloc(f"kh{j}", S) for j in range(4)]
    b.vh = [a.alloc(f"vh{j}", S) for j in range(4)]
    outs = {"q": b.qh, "k": b.kh, "v": b.vh}
    wbuf = [a.alloc(f"wqkv{i}", DC * 128) for i in range(2)]
    fts = ([("q", j, half * 4 + j) for j in range(4)]
           + [("k", j, 8 + half * 4 + j) for j in range(4)]
           + [("v", j, 16 + half * 4 + j) for j in range(4)])
    wld = []
    pe_use = [b.PE.n + i + 1 for i in range(12)]
    dwq = [DS(b, "dwqkv0"), DS(b, "dwqkv1")]

    def pool_body(po):
        wait_snap(po, b, sn, pe=True, dv=True)
        for i, (_, _, ft) in enumerate(fts):
            if i >= 2:
                po.wait_ge(b.PE.h, pe_use[i - 2])
            s_ = dwq[i % 2]
            po.dma_start(out=wbuf[i % 2], in_=b.dI["wqkv"][l, ft]
                         ).then_inc(s_.h, 16)
            s_.n += 16
            wld.append((s_, s_.n))
    on_pool(b, pool_body)

    evac = [b.DV.n + i + 1 for i in range(12)]

    def pe_body(pe):
        wait_snap(pe, b, sn, pe=False, dv=True)
        for i, (kind, j, ft) in enumerate(fts):
            pe.wait_ge(wld[i][0].h, wld[i][1])
            if i >= 2:
                pe.wait_ge(b.DV.h, evac[i - 2])
            ps = b.psA if i % 2 == 0 else b.psB
            for k in range(DC):
                for nc_ in range(4):
                    sl = slice(nc_ * 512, (nc_ + 1) * 512)
                    ins = pe.matmul(ps[:, sl], wbuf[i % 2][:, k * 128:(k + 1) * 128],
                                    b.h[k][:, sl], start=(k == 0), stop=(k == DC - 1))
            b.PE.inc(ins)
    on_pe(b, pe_body)

    def dve_body(dve):
        for i, (kind, j, ft) in enumerate(fts):
            dve.wait_ge(b.PE.h, pe_use[i])
            ps = b.psA if i % 2 == 0 else b.psB
            ins = dve.tensor_copy(outs[kind][j], ps[:, 0:2048])
            b.DV.inc(ins)
    on_dve(b, dve_body)
    assert b.DV.n == evac[-1]
    b.qkv_dv = evac[-1]
    a.free("wqkv0", "wqkv1")


def softmax_block(b, T, nk, prob_ap, qt):
    def mx(dve):
        dve.wait_ge(b.PE.h, b.sc_mark)
        if b.last_exp_ac:
            dve.wait_ge(b.AC.h, b.last_exp_ac)
        if nk < 2048 or b.mask_diag:
            dve.tensor_add(T[:, nk - 128:nk], T[:, nk - 128:nk], b.causal)
        ins = dve.reduce_max(b.nmax[0:128, 0:1], T[:, 0:nk], axis=AX.X,
                             negate=True)
        b.DV.inc(ins)
    on_dve(b, mx)
    mx_dv = b.DV.n

    def expf(act):
        act.wait_ge(b.DV.h, mx_dv)
        ins = act.activation(prob_ap[:, 0:nk], T[:, 0:nk], AF.Exp,
                             bias=b.nmax[0:128, 0:1], scale=1.0,
                             accum_out=b.den[0:128, 0:1])
        b.AC.inc(ins)
    on_act(b, expf)
    b.last_exp_ac = b.AC.n

    def rd(dve):
        dve.wait_ge(b.AC.h, b.last_exp_ac)
        if b.last_tr_pe:
            dve.wait_ge(b.PE.h, b.last_tr_pe)
        dve.reciprocal(b.den[0:128, 1:2], b.den[0:128, 0:1])
        dve.memset(b.junk[:, 0:384], 0.0)
        dve.tensor_scalar_mul(b.diag, b.ident, b.den[0:128, 1:2])
        ins = dve.memset(b.junk[:, 0:128], 0.0)
        b.DV.inc(ins)
    on_dve(b, rd)
    return b.DV.n


def attn_mx_exp_part(b, st):
    T, nk = st["T"], st["nch"] * 128
    prob_ap = st["prob"]

    def mx(dve):
        dve.wait_ge(b.PE.h, st["sc_mark"])
        if b.last_exp_ac:
            dve.wait_ge(b.AC.h, b.last_exp_ac)
        if nk < 2048 or b.mask_diag:
            dve.tensor_add(T[:, nk - 128:nk], T[:, nk - 128:nk], b.causal)
        ins = dve.reduce_max(b.nmax[0:128, 0:1], T[:, 0:nk], axis=AX.X,
                             negate=True)
        b.DV.inc(ins)
    on_dve(b, mx)
    mx_dv = b.DV.n

    def expf(act):
        act.wait_ge(b.DV.h, mx_dv)
        ins = act.activation(prob_ap[:, 0:nk], T[:, 0:nk], AF.Exp,
                             bias=b.nmax[0:128, 0:1], scale=1.0,
                             accum_out=b.den[0:128, 0:1])
        b.AC.inc(ins)
    on_act(b, expf)
    b.last_exp_ac = b.AC.n
    st["exp_ac"] = b.AC.n


def attn_rd_part(b, st):
    qt = st["qt"]
    diag_ap = b.diag2[qt % 2]

    def rd(dve):
        dve.wait_ge(b.AC.h, st["exp_ac"])
        if b.tr_pe_par[qt % 2]:
            dve.wait_ge(b.PE.h, b.tr_pe_par[qt % 2])
        dve.reciprocal(b.den[0:128, 1:2], b.den[0:128, 0:1])
        dve.memset(b.junk[:, 0:384], 0.0)
        dve.tensor_scalar_mul(diag_ap, b.ident, b.den[0:128, 1:2])
        ins = dve.memset(b.junk[:, 0:64], 0.0)
        b.DV.inc(ins)
    on_dve(b, rd)
    st["diag_dv"] = b.DV.n


def attn_tr_part(b, st):
    T, nch, qt = st["T"], st["nch"], st["qt"]
    prob_ap = st["prob"]
    diag_ap = b.diag2[qt % 2]
    nrounds = (nch + 3) // 4
    ndv = sum(1 for r in range(nrounds) if r % 2 == 0)
    dv0, ac0 = b.DV.n, b.AC.n
    ev = []
    cd = ca = 0
    for r in range(nrounds):
        if r % 2 == 0:
            cd += 1
            ev.append(("dv", dv0 + cd))
        else:
            ca += 1
            ev.append(("ac", ac0 + ca))
    pe_tr0 = b.PE.n

    def tr_pe(pe):
        pe.wait_ge(b.DV.h, st["diag_dv"])
        for r in range(nrounds):
            if r >= 2:
                kind, val = ev[r - 2]
                pe.wait_ge(b.DV.h if kind == "dv" else b.AC.h, val)
            w = min(4, nch - r * 4)
            for c in range(w):
                ch = r * 4 + c
                ins = pe.matmul(
                    T[:, (r % 2) * 512 + c * 128:(r % 2) * 512 + (c + 1) * 128],
                    prob_ap[:, ch * 128:(ch + 1) * 128],
                    diag_ap, start=(c == 0), stop=(c == w - 1))
            b.PE.inc(ins)
    on_pe(b, tr_pe)
    tr_marks = [pe_tr0 + r + 1 for r in range(nrounds)]
    b.tr_pe_par[qt % 2] = tr_marks[-1]

    def tr_dve(dve):
        for r in range(0, nrounds, 2):
            dve.wait_ge(b.PE.h, tr_marks[r])
            w = min(4, nch - r * 4) * 128
            dst = b.pt[r % 2][:, (r // 2 % 2) * 512:(r // 2 % 2) * 512 + w]
            dve.tensor_copy(dst, T[:, (r % 2) * 512:(r % 2) * 512 + w])
            ins = dve.memset(b.junk[:, 0:64], 0.0)
            b.DV.inc(ins)
    on_dve(b, tr_dve)

    def tr_act(act):
        for r in range(1, nrounds, 2):
            act.wait_ge(b.PE.h, tr_marks[r])
            w = min(4, nch - r * 4) * 128
            dst = b.pt[r % 2][:, (r // 2 % 2) * 512:(r // 2 % 2) * 512 + w]
            ins = act.copy(dst, T[:, (r % 2) * 512:(r % 2) * 512 + w])
            b.AC.inc(ins)
    on_act(b, tr_act)
    st["ev"] = ev


def attn_vec_part(b, st):
    T, nch, qt = st["T"], st["nch"], st["qt"]
    nrounds = (nch + 3) // 4
    vecreg = T[0:64, 1664:1792]
    ev = st["ev"]

    def vec_pe(pe):
        for r in range(nrounds):
            kind, val = ev[r]
            pe.wait_ge(b.DV.h if kind == "dv" else b.AC.h, val)
            w = min(4, nch - r * 4)
            for c in range(w):
                ch = r * 4 + c
                src = b.pt[r % 2][:, (r // 2 % 2) * 512 + c * 128:
                                  (r // 2 % 2) * 512 + (c + 1) * 128]
                ins = pe.matmul(vecreg, st["vtok"][:, ch * 64:(ch + 1) * 64],
                                src, start=(ch == 0), stop=(ch == nch - 1))
        b.PE.inc(ins)
    on_pe(b, vec_pe)
    vec_mark = b.PE.n

    def vec_dve(dve):
        dve.wait_ge(b.PE.h, vec_mark)
        st["out_fn"](dve, vecreg)
        ins = dve.memset(b.junk[:, 0:64], 0.0)
        b.DV.inc(ins)
    on_dve(b, vec_dve)
    st["done"] = b.DV.n


def attn_qt_pipeline(b, nqt, mk_state):
    states = {}
    states[0] = mk_state(0, states)
    attn_mx_exp_part(b, states[0])
    attn_rd_part(b, states[0])
    for qt in range(nqt):
        if qt + 1 < nqt:
            states[qt + 1] = mk_state(qt + 1, states)
            attn_mx_exp_part(b, states[qt + 1])
        attn_tr_part(b, states[qt])
        if qt + 1 < nqt:
            attn_rd_part(b, states[qt + 1])
        attn_vec_part(b, states[qt])
    return states


def attention_half(b, l, half):
    a = b.abf
    if (DEBUG_TAP == "dbg" and l == 0 and half == 0
            and not hasattr(b, "dbgP")):
        b.dbgP = a.alloc("dbgP", S)
        b.dbgS = b.sc1
        b.dbgD = b.af.alloc("dbgD", 2)
        b.dbg_snap = True
    qstack = a.alloc("qstack", S)
    kstack = a.alloc("kstack", S)
    vtok = a.alloc("vtok", 16 * DH)
    b.prob = [a.alloc(f"prob{i}", S) for i in range(2)]
    b.pt = [a.alloc(f"pt{i}", 1024) for i in range(2)]
    b.diag2 = [a.alloc(f"diag{i}", 128) for i in range(2)]
    b.nmax = b.af.alloc("nmax", 2)
    b.mask_diag = False
    b.last_exp_ac = 0
    b.tr_pe_par = [0, 0]
    if not hasattr(b, "head_done_dv"):
        b.head_done_dv = 0
    b.last_scores_pe = 0

    for hh in range(8):
        head = half * 8 + hh
        j = hh // 2
        r0 = (hh % 2) * 64

        def build_dve(dve, j=j, r0=r0, head=head, hh=hh):
            if hh == 0:
                dve.wait_ge(b.DV.h, max(b.qkv_dv, b.kr_dv))
            if b.last_scores_pe:
                dve.wait_ge(b.PE.h, b.last_scores_pe)
            dve.tensor_scalar(qstack[0:64, :], b.qh[j][r0:r0 + 64, :],
                              b.uv[0:64, head:head + 1], 0.125,
                              ALU.add, ALU.mult)
            dve.tensor_scalar(qstack[64:128, :], b.qh[j][r0:r0 + 64, :],
                              b.uv[0:64, H + head:H + head + 1], 0.125,
                              ALU.add, ALU.mult)
            dve.tensor_copy(kstack[0:64, :], b.kh[j][r0:r0 + 64, :])
            dve.tensor_copy(kstack[64:128, :], b.krh[j][r0:r0 + 64, :])
            ins = dve.memset(b.junk[:, 0:128], 0.0)
            b.DV.inc(ins)
        on_dve(b, build_dve)
        stacks_dv = b.DV.n

        pe0 = b.PE.n

        def vt_pe(pe, j=j, r0=r0, hh=hh):
            if b.head_done_dv:
                pe.wait_ge(b.DV.h, b.head_done_dv)
            pe.wait_ge(b.DV.h, stacks_dv)
            for c in range(16):
                col = (c % 8) * 64 + (c // 8) * 512
                ins = pe.matmul(b.psB[0:128, col:col + 64],
                                b.vh[j][r0:r0 + 64, c * 128:(c + 1) * 128],
                                b.ident64[r0:r0 + 64, :], start=True, stop=True)
                if c % 8 == 7:
                    b.PE.inc(ins)
        on_pe(b, vt_pe)
        vt_marks = [pe0 + 1, pe0 + 2]

        def vt_dve(dve):
            for r in range(2):
                dve.wait_ge(b.PE.h, vt_marks[r])
                dve.tensor_copy(vtok[:, r * 512:(r + 1) * 512],
                                b.psB[:, r * 512:(r + 1) * 512])
                ins = dve.memset(b.junk[:, 0:128], 0.0)
                b.DV.inc(ins)
        on_dve(b, vt_dve)
        vtok_dv = b.DV.n

        def mk_state(qt, states, head=head, vtok_dv=vtok_dv):
            T = b.psA if qt % 2 == 0 else b.psB
            st = {"qt": qt, "T": T, "nch": 16, "prob": b.prob[qt % 2],
                  "vtok": vtok}
            prev2 = states.get(qt - 2)

            def sc_pe(pe):
                if qt < 2:
                    pe.wait_ge(b.DV.h, vtok_dv)
                else:
                    pe.wait_ge(b.DV.h, prev2["done"])
                for nc_ in range(4):
                    sl = slice(nc_ * 512, (nc_ + 1) * 512)
                    ins = pe.matmul(T[:, sl],
                                    qstack[:, qt * 128:(qt + 1) * 128],
                                    kstack[:, sl], start=True, stop=True)
                b.PE.inc(ins)
            on_pe(b, sc_pe)
            st["sc_mark"] = b.PE.n
            b.last_scores_pe = b.PE.n

            def out_fn(dve, vecreg, head=head, qt=qt):
                dst = b.vecT[head // 2][(head % 2) * 64:(head % 2) * 64 + 64,
                                       qt * 128:(qt + 1) * 128]
                return dve.tensor_copy(dst, vecreg)
            st["out_fn"] = out_fn
            return st

        states = attn_qt_pipeline(b, 16, mk_state)
        b.head_done_dv = states[15]["done"]

    a.free("qstack", "kstack", "vtok", "prob0", "prob1", "pt0", "pt1",
           "diag0", "diag1")
    b.af.free("nmax")
    a.free(*[f"qh{j}" for j in range(4)], *[f"kh{j}" for j in range(4)],
           *[f"vh{j}" for j in range(4)], *[f"kr{j}" for j in range(4)])
    b.attn_dv = b.head_done_dv


def wo_ln1(b, l):
    a = b.abf
    sn = snap(b)
    wbuf = [a.alloc(f"wwo{i}", DC * 128) for i in range(2)]

    dpar = DS(b, "dpar")
    b.par_sem = dpar

    def sp_par(sp):
        wait_snap(sp, b, sn, pe=False, dv=True)
        for nm, dst in (("ln1g", b.lng), ("ln1b", b.lnb),
                        ("ln2g", b.ln2g), ("ln2b", b.ln2b)):
            sp.dma_start(out=dst, in_=b.dI[nm][l]).then_inc(dpar.h, 16)
            dpar.n += 16
        sp.dma_start(out=b.bf1, in_=b.dI["bff1"][l]).then_inc(dpar.h, 16)
        sp.dma_start(out=b.bf2, in_=b.dI["bff2"][l]).then_inc(dpar.h, 16)
        dpar.n += 32
        b.par_ld = dpar.n
    on_sp(b, sp_par)

    wld = []
    pe_use = [b.PE.n + i + 1 for i in range(DC)]
    dwo = [DS(b, "dwwo0"), DS(b, "dwwo1")]

    def pool_body(po):
        wait_snap(po, b, sn, pe=True, dv=True)
        for i in range(DC):
            if i >= 2:
                po.wait_ge(b.PE.h, pe_use[i - 2])
            s_ = dwo[i % 2]
            po.dma_start(out=wbuf[i % 2], in_=b.dI["wo"][l, i]).then_inc(s_.h, 16)
            s_.n += 16
            wld.append((s_, s_.n))
    on_pool(b, pool_body)

    add_dv = [b.DV.n + i + 1 for i in range(DC)]

    def pe_body(pe):
        pe.wait_ge(b.DV.h, b.attn_dv)
        for i in range(DC):
            pe.wait_ge(wld[i][0].h, wld[i][1])
            if i >= 2:
                pe.wait_ge(b.DV.h, add_dv[i - 2])
            ps = b.psA if i % 2 == 0 else b.psB
            for k in range(DC):
                for nc_ in range(4):
                    sl = slice(nc_ * 512, (nc_ + 1) * 512)
                    ins = pe.matmul(ps[:, sl], wbuf[i % 2][:, k * 128:(k + 1) * 128],
                                    b.vecT[k][:, sl], start=(k == 0),
                                    stop=(k == DC - 1))
            b.PE.inc(ins)
    on_pe(b, pe_body)

    def dve_body(dve):
        for i in range(DC):
            dve.wait_ge(b.PE.h, pe_use[i])
            ps = b.psA if i % 2 == 0 else b.psB
            ins = dve.tensor_add(b.h[i], b.h[i], ps[:, 0:2048])
            b.DV.inc(ins)
    on_dve(b, dve_body)
    assert b.DV.n == add_dv[-1]
    a.free("wwo0", "wwo1")
    layer_norm_h(b, b.lng, b.lnb, add_dv, par_ld=b.par_ld)


def layer_norm_h(b, gamma, beta, ready_dv, par_ld=None):
    mb, rb, sq = norm_scratch(b)
    ac0 = [b.AC.n + k + 1 for k in range(DC)]
    pe_sq = [b.PE.n + 2 * (k + 1) for k in range(DC)]

    def act_body(act):
        for k in range(DC):
            act.wait_ge(b.DV.h, ready_dv[-1])
            if k > 0:
                act.wait_ge(b.PE.h, pe_sq[k - 1])
            ins = act.activation(sq, b.h[k], AF.Square)
            b.AC.inc(ins)
    on_act(b, act_body)

    def pe_body(pe):
        for k in range(DC):
            pe.wait_ge(b.DV.h, ready_dv[-1])
            for nc_ in range(4):
                sl = slice(nc_ * 512, (nc_ + 1) * 512)
                ins = pe.matmul(b.psA[0:1, sl], b.ones_col[:, 0:1], b.h[k][:, sl],
                                start=(k == 0), stop=(k == DC - 1))
            b.PE.inc(ins)
            pe.wait_ge(b.AC.h, ac0[k])
            for nc_ in range(4):
                sl = slice(nc_ * 512, (nc_ + 1) * 512)
                ins = pe.matmul(b.psB[0:1, sl], b.ones_col[:, 0:1], sq[:, sl],
                                start=(k == 0), stop=(k == DC - 1))
            b.PE.inc(ins)
    on_pe(b, pe_body)
    stats_pe = b.PE.n

    mean_var_rows(b, stats_pe, 1.0 / 1024.0, 1e-5, with_mean=True)
    bcast_rows(b, mb, rb, with_mean=True)
    bc_dv = b.bc_dv

    out_dv = []

    def dve_norm(dve):
        dve.wait_ge(b.DV.h, bc_dv)
        if par_ld:
            dve.wait_ge(b.par_sem.h, par_ld)
        for k in range(DC):
            dve.tensor_sub(b.sc1, b.h[k], mb)
            dve.tensor_mul(b.sc1, b.sc1, rb)
            ins = dve.tensor_scalar(b.h[k], b.sc1, gamma[:, k:k + 1],
                                    beta[:, k:k + 1], ALU.mult, ALU.add)
            out_dv.append(b.DV.inc(ins, 3))
    on_dve(b, dve_norm)
    b.ln_dv = list(out_dv)
    free_norm_scratch(b)


def ff_block(b, l):
    a = b.abf
    sn = snap(b)
    w1 = [a.alloc(f"wf1_{k}", DF) for k in range(DC)]
    w2 = [a.alloc(f"wf2_{d}", 32 * 128) for d in range(DC)]
    ring = a.alloc("ring", 32 * 256)
    w1ld, w2ld = [], []

    dwf1 = DS(b, "dwf1")
    dwf2 = DS(b, "dwf2")

    def pool_body(po):
        wait_snap(po, b, sn, pe=True, dv=True, ac=True)
        for k in range(DC):
            po.dma_start(out=w1[k], in_=b.dI["wff1"][l, k]).then_inc(dwf1.h, 16)
            dwf1.n += 16
            w1ld.append(dwf1.n)
        for d in range(DC):
            po.dma_start(out=w2[d], in_=b.dI["wff2"][l, d]).then_inc(dwf2.h, 16)
            dwf2.n += 16
            w2ld.append(dwf2.n)
    on_pool(b, pool_body)

    NT = S // 256
    relu_ac = {}
    c = b.AC.n
    for t in range(NT):
        for kf in range(32):
            c += 1
            relu_ac[(t, kf)] = c
    ff1_pe = {}
    yA_pe, yB_pe = [], []
    c = b.PE.n
    for t in range(NT):
        for kf in range(32):
            c += 1
            ff1_pe[(t, kf)] = c
        c += 1
        yA_pe.append(c)
        c += 1
        yB_pe.append(c)
    evA_dv = [b.DV.n + 2 * t + 1 for t in range(NT)]
    evB_dv = [b.DV.n + 2 * t + 2 for t in range(NT)]

    def pe_body(pe):
        pe.wait_ge(dwf1.h, w1ld[-1])
        pe.wait_ge(b.DV.h, b.ln_dv[-1])
        for t in range(NT):
            tsl = slice(t * 256, (t + 1) * 256)
            if t == 0:
                pe.wait_ge(dwf2.h, w2ld[-1])
            for kf in range(33):
                if kf < 32:
                    fof = (kf % 2) * 512
                    if kf >= 2:
                        pe.wait_ge(b.AC.h, relu_ac[(t, kf - 2)])
                    for k in range(DC):
                        ins = pe.matmul(b.psA[:, fof:fof + 256],
                                        w1[k][:, kf * 128:(kf + 1) * 128],
                                        b.h[k][:, tsl], start=(k == 0),
                                        stop=(k == DC - 1))
                    b.PE.inc(ins)
                if kf == 0:
                    continue
                kc = kf - 1
                if kc == 0 and t >= 1:
                    pe.wait_ge(b.DV.h, evB_dv[t - 1])
                pe.wait_ge(b.AC.h, relu_ac[(t, kc)])
                for dd in range(4):
                    ins = pe.matmul(b.psB[:, dd * 512:dd * 512 + 256],
                                    w2[dd][:, kc * 128:(kc + 1) * 128],
                                    ring[:, kc * 256:(kc + 1) * 256],
                                    start=(kc == 0), stop=(kc == 31))
            b.PE.inc(ins)
            pe.wait_ge(b.DV.h, evA_dv[t])
            for kf in range(32):
                for dd in range(4, DC):
                    ins = pe.matmul(b.psB[:, (dd - 4) * 512:(dd - 4) * 512 + 256],
                                    w2[dd][:, kf * 128:(kf + 1) * 128],
                                    ring[:, kf * 256:(kf + 1) * 256],
                                    start=(kf == 0), stop=(kf == 31))
            b.PE.inc(ins)
    on_pe(b, pe_body)

    def act_body(act):
        act.wait_ge(b.par_sem.h, b.par_ld)
        for t in range(NT):
            for kf in range(32):
                act.wait_ge(b.PE.h, ff1_pe[(t, kf)])
                fof = (kf % 2) * 512
                ins = act.activation(ring[:, kf * 256:(kf + 1) * 256],
                                     b.psA[:, fof:fof + 256],
                                     AF.Relu, bias=b.bf1[:, kf:kf + 1], scale=1.0)
                b.AC.inc(ins)
                assert b.AC.n == relu_ac[(t, kf)]
    on_act(b, act_body)

    def dve_body(dve):
        dve.wait_ge(b.par_sem.h, b.par_ld)
        for t in range(NT):
            tsl = slice(t * 256, (t + 1) * 256)
            dve.wait_ge(b.PE.h, yA_pe[t])
            for dd in range(4):
                ins = dve.tensor_scalar(b.h[dd][:, tsl],
                                        b.psB[:, dd * 512:dd * 512 + 256],
                                        b.bf2[:, dd:dd + 1], None, ALU.add)
            b.DV.inc(ins)
            assert b.DV.n == evA_dv[t]
            dve.wait_ge(b.PE.h, yB_pe[t])
            for dd in range(4, DC):
                ins = dve.tensor_scalar(b.h[dd][:, tsl],
                                        b.psB[:, (dd - 4) * 512:(dd - 4) * 512 + 256],
                                        b.bf2[:, dd:dd + 1], None, ALU.add)
            b.DV.inc(ins)
            assert b.DV.n == evB_dv[t]
    on_dve(b, dve_body)

    a.free(*[f"wf1_{k}" for k in range(DC)], *[f"wf2_{d}" for d in range(DC)],
           "ring")
    layer_norm_h(b, b.ln2g, b.ln2b, [evB_dv[-1]], par_ld=b.par_ld)


def spill_s(b):
    dss = DS(b, "dsspill")

    def sp_body(sp):
        for k in range(DC):
            sp.wait_ge(b.DV.h, b.ln_dv[k]) if False else sp.wait_ge(b.DV.h, b.ln_dv[-1])
            sp.dma_start(out=b.d_s[k * 128:(k + 1) * 128, :], in_=b.h[k]
                         ).then_inc(dss.h, 16)
            dss.n += 16
        b.s_spilled = dss.n
    on_sp(b, sp_body)
    b.taps["s"] = list(b.h)



def ffn_module(b):
    a = b.abf
    b.xn = b.h

    dxr = DS(b, "dxnr")
    b.xnr_sem = dxr

    def sp_body(sp):
        sp.wait_ge(DS(b, "dspill").h, b.xn_spilled)
        sp.wait_ge(DS(b, "dsspill").h, b.s_spilled)
        for k in range(DC):
            sp.dma_start(out=b.xn[k], in_=b.d_xn[k * 128:(k + 1) * 128, :]
                         ).then_inc(dxr.h, 16)
            dxr.n += 16
        b.xn_ld = dxr.n
    on_sp(b, sp_body)

    mb, rb, sq = norm_scratch(b)
    ffn_rms(b, src=b.xn, ready_sd=b.xn_ld, eps=1e-6, wname="fln1", rb=rb, sq=sq)
    b.h1 = [a.alloc(f"h1_{k}", S) for k in range(DC)]
    h1_dv = []

    def dve_h1(dve):
        dve.wait_ge(b.xnr_sem.h, b.xn_ld)
        dve.wait_ge(DS(b, "dfwn").h, b.fwn_ld)
        for k in range(DC):
            ins = dve.scalar_tensor_tensor(b.h1[k], b.xn[k],
                                           b.fw_norm[:, k:k + 1], rb,
                                           ALU.mult, ALU.mult)
            h1_dv.append(b.DV.inc(ins))
    on_dve(b, dve_h1)
    b.h1_dv = list(h1_dv)
    free_norm_scratch(b)

    ffn_qkv(b)
    if BUILD_PHASES < 7.6:
        return
    ffn_rope_qknorm(b)
    if BUILD_PHASES < 8:
        return
    ffn_attention(b)
    if BUILD_PHASES < 8.6:
        return
    ffn_wo_xf(b)
    if BUILD_PHASES < 9:
        return
    mb, rb, sq = norm_scratch(b)
    ffn_rms(b, src=b.xn, ready_dv=b.xf_dv, eps=1e-6, wname="fln2", rb=rb, sq=sq)
    b.h2 = [a.alloc(f"h2_{k}", S) for k in range(DC)]
    h2_dv = []

    def dve_h2(dve):
        dve.wait_ge(DS(b, "dfwn").h, b.fwn_ld)
        for k in range(DC):
            ins = dve.scalar_tensor_tensor(b.h2[k], b.xn[k],
                                           b.fw_norm[:, k:k + 1], rb,
                                           ALU.mult, ALU.mult)
            h2_dv.append(b.DV.inc(ins))
    on_dve(b, dve_h2)
    b.h2_dv = list(h2_dv)
    free_norm_scratch(b)

    ffn_logits_mask(b)
    if BUILD_PHASES < 10:
        return
    ffn_moe(b)


def ffn_rms(b, src, eps, wname, rb, sq, ready_sd=None, ready_dv=None):
    sn = snap(b)

    dfwn = DS(b, "dfwn")

    def sp_par(sp):
        wait_snap(sp, b, sn, pe=False, dv=True)
        sp.dma_start(out=b.fw_norm, in_=b.dI[wname]).then_inc(dfwn.h, 16)
        dfwn.n += 16
        b.fwn_ld = dfwn.n
    on_sp(b, sp_par)

    ac0 = [b.AC.n + k + 1 for k in range(DC)]
    pe_sq = [b.PE.n + k + 1 for k in range(DC)]

    def act_body(act):
        if ready_sd:
            act.wait_ge(b.xnr_sem.h, ready_sd)
        if ready_dv:
            act.wait_ge(b.DV.h, ready_dv)
        for k in range(DC):
            if k > 0:
                act.wait_ge(b.PE.h, pe_sq[k - 1])
            ins = act.activation(sq, src[k], AF.Square)
            b.AC.inc(ins)
    on_act(b, act_body)

    def pe_body(pe):
        wait_snap(pe, b, sn, pe=False, dv=True)
        for k in range(DC):
            pe.wait_ge(b.AC.h, ac0[k])
            for nc_ in range(4):
                sl = slice(nc_ * 512, (nc_ + 1) * 512)
                ins = pe.matmul(b.psB[0:1, sl], b.ones_col[:, 0:1], sq[:, sl],
                                start=(k == 0), stop=(k == DC - 1))
            b.PE.inc(ins)
    on_pe(b, pe_body)
    stats_pe = b.PE.n

    mean_var_rows(b, stats_pe, 1.0 / 1024.0, eps, with_mean=False)
    bcast_rows(b, None, rb, with_mean=False)



def ffn_qkv(b):
    a = b.abf
    sn = snap(b)
    b.fq = [a.alloc(f"fq{j}", S) for j in range(DC)]
    b.fk = [a.alloc(f"fk{j}", S) for j in range(2)]
    b.fv = [a.alloc(f"fv{j}", S) for j in range(2)]
    wbuf = [a.alloc(f"wfq{i}", DC * 128) for i in range(2)]
    b.fbq = b.af.alloc("fbq", DC)
    b.fbk = b.af.alloc("fbk", 2)
    b.fbv = b.af.alloc("fbv", 2)

    dfb = DS(b, "dfbias")

    def sp_par(sp):
        wait_snap(sp, b, sn, pe=False, dv=True)
        for nm, dst in (("fbq", b.fbq), ("fbk", b.fbk), ("fbv", b.fbv)):
            sp.dma_start(out=dst, in_=b.dI[nm]).then_inc(dfb.h, 16)
            dfb.n += 16
        b.fbias_ld = dfb.n
    on_sp(b, sp_par)

    jobs = ([("fwq", j, b.fq, j, b.fbq) for j in range(DC)]
            + [("fwk", j, b.fk, j, b.fbk) for j in range(2)]
            + [("fwv", j, b.fv, j, b.fbv) for j in range(2)])
    wld = []
    pe_use = [b.PE.n + i + 1 for i in range(len(jobs))]
    dfq = [DS(b, "dwfq0"), DS(b, "dwfq1")]

    def pool_body(po):
        wait_snap(po, b, sn, pe=True, dv=True)
        for i, (wn, ft, _, _, _) in enumerate(jobs):
            if i >= 2:
                po.wait_ge(b.PE.h, pe_use[i - 2])
            s_ = dfq[i % 2]
            po.dma_start(out=wbuf[i % 2], in_=b.dI[wn][ft]).then_inc(s_.h, 16)
            s_.n += 16
            wld.append((s_, s_.n))
    on_pool(b, pool_body)

    evac = [b.DV.n + i + 1 for i in range(len(jobs))]

    def pe_body(pe):
        pe.wait_ge(b.DV.h, b.h1_dv[-1])
        for i, (wn, ft, outl, j, bias) in enumerate(jobs):
            pe.wait_ge(wld[i][0].h, wld[i][1])
            if i >= 2:
                pe.wait_ge(b.DV.h, evac[i - 2])
            ps = b.psA if i % 2 == 0 else b.psB
            for k in range(DC):
                for nc_ in range(4):
                    sl = slice(nc_ * 512, (nc_ + 1) * 512)
                    ins = pe.matmul(ps[:, sl], wbuf[i % 2][:, k * 128:(k + 1) * 128],
                                    b.h1[k][:, sl], start=(k == 0),
                                    stop=(k == DC - 1))
            b.PE.inc(ins)
    on_pe(b, pe_body)

    def dve_body(dve):
        dve.wait_ge(dfb.h, b.fbias_ld)
        for i, (wn, ft, outl, j, bias) in enumerate(jobs):
            dve.wait_ge(b.PE.h, pe_use[i])
            ps = b.psA if i % 2 == 0 else b.psB
            ins = dve.tensor_scalar_add(outl[j], ps[:, 0:2048], bias[:, j:j + 1])
            b.DV.inc(ins)
    on_dve(b, dve_body)
    assert b.DV.n == evac[-1]
    b.fqkv_dv = evac[-1]
    b.taps["fq0"] = list(b.fq)
    b.taps["h1"] = list(b.h1)
    a.free("wfq0", "wfq1", *[f"h1_{k}" for k in range(DC)])


def ffn_rope_qknorm(b):
    a = b.abf
    sn = snap(b)
    b.a2q = b.af.alloc("a2q", 128)
    b.a2k = b.af.alloc("a2k", 128)
    b.ehalf = a.alloc("ehalf", 2)
    ropeC = a.alloc("ropeC", S)
    ropeS2 = a.alloc("ropeS2", S)
    t1 = a.alloc("rope_t1", S)
    sq = a.alloc("sq", S)

    def pool_a2(po):
        wait_snap(po, b, sn, pe=True, dv=True)
        po.memset(b.a2q[0:2, :], 0.0)
        po.memset(b.a2k[0:2, :], 0.0)
        po.memset(b.ehalf[:, :], 0.0)
        po.memset(b.ehalf[0:64, 0:1], 1.0)
        ins = po.memset(b.ehalf[64:128, 1:2], 1.0)
        b.PO.inc(ins)
        da2 = DS(b, "da2")
        po.dma_start(out=b.a2q[0:1, 0:64], in_=b.dI["fqn"]).then_inc(da2.h, 16)
        po.dma_start(out=b.a2q[1:2, 64:128], in_=b.dI["fqn"]).then_inc(da2.h, 16)
        po.dma_start(out=b.a2k[0:1, 0:64], in_=b.dI["fkn"]).then_inc(da2.h, 16)
        po.dma_start(out=b.a2k[1:2, 64:128], in_=b.dI["fkn"]).then_inc(da2.h, 16)
        da2.n += 64
        b.a2_ld = da2.n
        b.eh_po = b.PO.n
    on_pool(b, pool_a2)

    drp = DS(b, "drope")

    def sp_rope(sp):
        wait_snap(sp, b, sn, pe=True, dv=True)
        sp.dma_start(out=ropeC, in_=b.dI["c_ropeC"]).then_inc(drp.h, 16)
        sp.dma_start(out=ropeS2, in_=b.dI["c_ropeS2"]).then_inc(drp.h, 16)
        drp.n += 32
        b.rope_ld = drp.n
    on_sp(b, sp_rope)

    tiles = [(b.fq[j], b.a2q) for j in range(DC)] + \
            [(b.fk[j], b.a2k) for j in range(2)]
    prev_dv = b.fqkv_dv
    for ti, (src, a2) in enumerate(tiles):
        is_q = ti < DC
        def pe_sw(pe, src=src, ti=ti, prev_dv=prev_dv):
            pe.wait_ge(b.DV.h, prev_dv)
            if ti == 0:
                pe.wait_ge(DS(b, "drope").h, b.rope_ld)
            for nc_ in range(4):
                sl = slice(nc_ * 512, (nc_ + 1) * 512)
                ins = pe.matmul(b.psA[:, sl], b.swap32, src[:, sl],
                                start=True, stop=True)
            b.PE.inc(ins)
        on_pe(b, pe_sw)
        sw_pe = b.PE.n

        def dve_rope(dve, src=src):
            dve.wait_ge(b.PE.h, sw_pe)
            dve.tensor_mul(t1, src, ropeC)
            dve.tensor_mul(b.sc1, b.psA[:, 0:2048], ropeS2)
            ins = dve.tensor_add(b.sc1, b.sc1, t1)
            b.DV.inc(ins)
        on_dve(b, dve_rope)
        roped_dv = b.DV.n

        def act_sq(act):
            act.wait_ge(b.DV.h, roped_dv)
            ins = act.activation(sq, b.sc1, AF.Square)
            b.AC.inc(ins)
        on_act(b, act_sq)
        sq_ac = b.AC.n

        def pe_ms(pe, ti=ti):
            pe.wait_ge(b.AC.h, sq_ac)
            if ti == 0:
                pe.wait_ge(b.PO.h, b.eh_po)
                pe.wait_ge(DS(b, "da2").h, b.a2_ld)
            for nc_ in range(4):
                sl = slice(nc_ * 512, (nc_ + 1) * 512)
                ins = pe.matmul(b.psB[0:2, sl], b.ehalf, sq[:, sl],
                                start=True, stop=True)
            b.PE.inc(ins)
        on_pe(b, pe_ms)
        ms_pe = b.PE.n

        def dve_ms(dve):
            dve.wait_ge(b.PE.h, ms_pe)
            dve.tensor_scalar_mul(b.rows[0:2, 0:S], b.psB[0:2, 0:2048], 1.0 / 64.0)
            ins = dve.tensor_scalar_add(b.rows[0:2, 0:S], b.rows[0:2, 0:S], 1e-6)
            b.DV.inc(ins)
        on_dve(b, dve_ms)
        ms_dv = b.DV.n

        def act_rt(act):
            act.wait_ge(b.DV.h, ms_dv)
            ins = act.activation(b.rows[0:2, 0:S], b.rows[0:2, 0:S], AF.Sqrt)
            b.AC.inc(ins)
        on_act(b, act_rt)
        rt_ac = b.AC.n

        def dve_rc(dve):
            dve.wait_ge(b.AC.h, rt_ac)
            ins = dve.reciprocal(b.rows[0:2, 0:S], b.rows[0:2, 0:S])
            b.DV.inc(ins)
        on_dve(b, dve_rc)
        rc_dv = b.DV.n

        def pe_bc(pe, a2=a2):
            pe.wait_ge(b.DV.h, rc_dv)
            for nc_ in range(4):
                sl = slice(nc_ * 512, (nc_ + 1) * 512)
                ins = pe.matmul(b.psB[:, sl], a2[0:2, :], b.rows[0:2, sl],
                                start=True, stop=True)
            b.PE.inc(ins)
        on_pe(b, pe_bc)
        bc_pe = b.PE.n

        def dve_fin(dve, src=src, is_q=is_q):
            dve.wait_ge(b.PE.h, bc_pe)
            if is_q:
                ins = dve.scalar_tensor_tensor(src, b.sc1, 0.125,
                                               b.psB[:, 0:2048],
                                               ALU.mult, ALU.mult)
            else:
                ins = dve.tensor_mul(src, b.sc1, b.psB[:, 0:2048])
            b.DV.inc(ins)
        on_dve(b, dve_fin)
        prev_dv = b.DV.n
    b.rope_dv = prev_dv
    b.taps["fq"] = list(b.fq)
    if BUILD_PHASES >= 8:
        a.free("rope_t1", "ropeC", "ropeS2", "sq", "ehalf")
        b.af.free("a2q", "a2k")


def ffn_attention(b):
    a = b.abf
    sn = snap(b)
    b.vecT = [a.alloc(f"vecT{k}", S) for k in range(DC)]
    vtok = a.alloc("fvtok", NKV * 16 * DH)
    b.prob = [a.alloc(f"prob{i}", S) for i in range(2)]
    b.pt = [a.alloc(f"pt{i}", 1024) for i in range(2)]
    b.diag = a.alloc("diag", 128)
    oh = a.alloc("oh", S)
    sig = a.alloc("fsig", S)
    wg = a.alloc("wg", DH)
    b.nmax = b.af.alloc("nmax", 2)
    bg = b.af.alloc("bg", 1)
    b.mask_diag = True
    b.last_exp_ac = 0
    b.last_tr_pe = 0
    fkd = [a.alloc(f"fkd{g}", S) for g in range(NKV)]

    def dve_fkd(dve):
        dve.wait_ge(b.DV.h, b.rope_dv)
        for g in range(NKV):
            src_ = b.fk[g // 2][(g % 2) * 64:(g % 2) * 64 + 64, :]
            dve.tensor_copy(fkd[g][0:64, :], src_)
            dve.tensor_copy(fkd[g][64:128, :], src_)
        ins = dve.memset(b.junk[:, 0:128], 0.0)
        b.DV.inc(ins)
    on_dve(b, dve_fkd)
    b.fkd_dv = b.DV.n

    dwg = DS(b, "dwg")
    dbg_ = DS(b, "dbg")

    def pool_wg(po):
        wait_snap(po, b, sn, pe=True, dv=True)
        po.dma_start(out=wg[0:64, :], in_=b.dI["fwg"]).then_inc(dwg.h, 16)
        dwg.n += 16
        b.wg_ld = dwg.n
    on_pool(b, pool_wg)

    def sp_bg(sp):
        wait_snap(sp, b, sn, pe=False, dv=True)
        sp.dma_start(out=bg[0:64, :], in_=b.dI["fbg"]).then_inc(dbg_.h, 16)
        dbg_.n += 16
        b.bg_ld = dbg_.n
    on_sp(b, sp_bg)

    pe0 = b.PE.n
    vt_marks = [pe0 + i + 1 for i in range(NKV * 2)]
    vt_ev = [b.DV.n + i + 1 for i in range(NKV * 2)]

    def vt_pe(pe):
        pe.wait_ge(b.DV.h, b.rope_dv)
        for g in range(NKV):
            if g >= 1:
                pe.wait_ge(b.DV.h, vt_ev[g * 2 - 1])
            src = b.fv[g // 2]
            r0 = (g % 2) * 64
            for c in range(16):
                col = (c % 8) * 64 + (c // 8) * 512
                ins = pe.matmul(b.psB[0:128, col:col + 64],
                                src[r0:r0 + 64, c * 128:(c + 1) * 128],
                                b.ident64[r0:r0 + 64, :], start=True, stop=True)
                if c % 8 == 7:
                    b.PE.inc(ins)
    on_pe(b, vt_pe)

    def vt_dve(dve):
        for i in range(NKV * 2):
            g, r = i // 2, i % 2
            dve.wait_ge(b.PE.h, vt_marks[i])
            dst = vtok[:, g * 1024 + r * 512:g * 1024 + (r + 1) * 512]
            dve.tensor_copy(dst, b.psB[:, r * 512:(r + 1) * 512])
            ins = dve.memset(b.junk[:, 0:128], 0.0)
            assert b.DV.inc(ins) == vt_ev[i]
    on_dve(b, vt_dve)
    vtok_dv = b.DV.n
    prev_head_dv = vtok_dv

    b.tr_pe_par = [0, 0]
    for head in range(H):
        g = head // 4
        qj, qr = head // 2, (head % 2) * 64

        def mk_state(qt, states, head=head, g=g, qj=qj, qr=qr,
                     prev_head_dv=prev_head_dv):
            nch = qt + 1
            T = b.psA if qt % 2 == 0 else b.psB
            st = {"qt": qt, "T": T, "nch": nch, "prob": b.prob[qt % 2],
                  "vtok": vtok[:, g * 1024:(g + 1) * 1024]}
            prev2 = states.get(qt - 2)

            def sc_pe(pe):
                if qt < 2:
                    pe.wait_ge(b.DV.h, prev_head_dv)
                    pe.wait_ge(b.DV.h, b.fkd_dv)
                else:
                    pe.wait_ge(b.DV.h, prev2["done"])
                nk = nch * 128
                for nc_ in range((nk + 511) // 512):
                    sl = slice(nc_ * 512, min((nc_ + 1) * 512, nk))
                    ins = pe.matmul(T[:, sl],
                                    b.fq[qj][qr:qr + 64, qt * 128:(qt + 1) * 128],
                                    fkd[g][qr:qr + 64, sl],
                                    start=True, stop=True)
                b.PE.inc(ins)
            on_pe(b, sc_pe)
            st["sc_mark"] = b.PE.n

            def out_fn(dve, vecreg, qt=qt):
                return dve.tensor_copy(oh[0:64, qt * 128:(qt + 1) * 128], vecreg)
            st["out_fn"] = out_fn
            return st

        states = attn_qt_pipeline(b, 16, mk_state)
        oh_dv = states[15]["done"]

        for nc_ in range(4):
            sl = slice(nc_ * 512, (nc_ + 1) * 512)

            def pe_g(pe, nc_=nc_, sl=sl, head=head, oh_dv=oh_dv):
                if nc_ == 0:
                    pe.wait_ge(b.DV.h, oh_dv)
                    if head == 0:
                        pe.wait_ge(DS(b, "dwg").h, b.wg_ld)
                else:
                    pe.wait_ge(b.AC.h, b.gate_ac)
                ins = pe.matmul(b.psB[0:64, 1536:2048], wg[0:64, :],
                                oh[0:64, sl], start=True, stop=True)
                b.PE.inc(ins)
            on_pe(b, pe_g)
            gpe = b.PE.n

            def act_g(act, sl=sl, head=head, nc_=nc_):
                act.wait_ge(b.PE.h, gpe)
                if head == 0 and nc_ == 0:
                    act.wait_ge(DS(b, "dbg").h, b.bg_ld)
                ins = act.activation(sig[0:64, sl], b.psB[0:64, 1536:2048],
                                     AF.Sigmoid, bias=bg[0:64, :], scale=1.0)
                b.AC.inc(ins)
            on_act(b, act_g)
            b.gate_ac = b.AC.n

        def dve_gate(dve, head=head):
            dve.wait_ge(b.AC.h, b.gate_ac)
            dst = b.vecT[head // 2][(head % 2) * 64:(head % 2) * 64 + 64, :]
            dve.tensor_mul(dst, oh[0:64, :], sig[0:64, :])
            ins = dve.memset(b.junk[:, 0:128], 0.0)
            b.DV.inc(ins)
        on_dve(b, dve_gate)
        prev_head_dv = b.DV.n

    b.fattn_dv = prev_head_dv
    b.taps["fvec"] = list(b.vecT)
    a.free("fvtok", "prob0", "prob1", "pt0", "pt1", "diag", "oh", "fsig", "wg",
           *[f"fkd{g}" for g in range(NKV)])
    b.af.free("nmax", "bg")
    a.free(*[f"fq{j}" for j in range(DC)], "fk0", "fk1", "fv0", "fv1")


def ffn_wo_xf(b):
    a = b.abf
    sn = snap(b)
    wbuf = [a.alloc(f"wfo{i}", DC * 128) for i in range(2)]
    bo = b.af.alloc("fbo", DC)

    dfbo = DS(b, "dfbo")

    def sp_par(sp):
        wait_snap(sp, b, sn, pe=False, dv=True)
        sp.dma_start(out=bo, in_=b.dI["fbo"]).then_inc(dfbo.h, 16)
        dfbo.n += 16
        b.fbo_ld = dfbo.n
    on_sp(b, sp_par)

    wld = []
    pe_use = [b.PE.n + i + 1 for i in range(DC)]
    dfo = [DS(b, "dwfo0"), DS(b, "dwfo1")]

    def pool_body(po):
        wait_snap(po, b, sn, pe=True, dv=True)
        for i in range(DC):
            if i >= 2:
                po.wait_ge(b.PE.h, pe_use[i - 2])
            s_ = dfo[i % 2]
            po.dma_start(out=wbuf[i % 2], in_=b.dI["fwo"][i]).then_inc(s_.h, 16)
            s_.n += 16
            wld.append((s_, s_.n))
    on_pool(b, pool_body)

    evac = [b.DV.n + i + 1 for i in range(DC)]

    def pe_body(pe):
        pe.wait_ge(b.DV.h, b.fattn_dv)
        for i in range(DC):
            pe.wait_ge(wld[i][0].h, wld[i][1])
            if i >= 2:
                pe.wait_ge(b.DV.h, evac[i - 2])
            ps = b.psA if i % 2 == 0 else b.psB
            for k in range(DC):
                for nc_ in range(4):
                    sl = slice(nc_ * 512, (nc_ + 1) * 512)
                    ins = pe.matmul(ps[:, sl], wbuf[i % 2][:, k * 128:(k + 1) * 128],
                                    b.vecT[k][:, sl], start=(k == 0),
                                    stop=(k == DC - 1))
            b.PE.inc(ins)
    on_pe(b, pe_body)

    def dve_body(dve):
        dve.wait_ge(dfbo.h, b.fbo_ld)
        for i in range(DC):
            dve.wait_ge(b.PE.h, pe_use[i])
            ps = b.psA if i % 2 == 0 else b.psB
            ins = dve.scalar_tensor_tensor(b.xn[i], ps[:, 0:2048],
                                           bo[:, i:i + 1], b.xn[i],
                                           ALU.add, ALU.add)
            b.DV.inc(ins)
    on_dve(b, dve_body)
    assert b.DV.n == evac[-1]
    b.xf_dv = evac[-1]
    a.free("wfo0", "wfo1", *[f"vecT{k}" for k in range(DC)])
    b.af.free("fbo")
    b.taps["xf"] = list(b.xn)


def ffn_logits_mask(b):
    a = b.abf
    sn = snap(b)
    b.mask = a.alloc("mask", S)
    wr = a.alloc("fwr", DC * 8)
    brr = a.alloc("fbr", 8)
    mtok = b.af.alloc("mtok", 8)
    mrow = b.af.alloc("mrow", 1)

    dfwr = DS(b, "dfwr")

    def pool_w(po):
        wait_snap(po, b, sn, pe=True, dv=True)
        for k in range(DC):
            po.dma_start(out=wr[:, k * 8:(k + 1) * 8], in_=b.dI["fwr"][k]
                         ).then_inc(dfwr.h, 16)
        po.dma_start(out=brr[0:1, :], in_=b.dI["fbr"]).then_inc(dfwr.h, 16)
        dfwr.n += 16 * 9
        b.fwr_ld = dfwr.n
    on_pool(b, pool_w)

    prev = 0
    for tt in range(16):
        tsl = slice(tt * 128, (tt + 1) * 128)

        def pe_lg(pe, tt=tt, tsl=tsl, prev=prev):
            if tt == 0:
                pe.wait_ge(DS(b, "dfwr").h, b.fwr_ld)
                pe.wait_ge(b.DV.h, b.h2_dv[-1])
            else:
                pe.wait_ge(b.DV.h, prev)
            for k in range(DC):
                pe.matmul(b.psA[:, 0:8], b.h2[k][:, tsl], wr[:, k * 8:(k + 1) * 8],
                          start=(k == 0), stop=False)
            ins = pe.matmul(b.psA[:, 0:8], b.ones_row[0:1, :], brr[0:1, :],
                            start=False, stop=True)
            b.PE.inc(ins)
        on_pe(b, pe_lg)
        lg_pe = b.PE.n

        def dve_mx(dve, tt=tt):
            dve.wait_ge(b.PE.h, lg_pe)
            dve.reduce_max(mrow[0:128, 0:1], b.psA[:, 0:8], axis=AX.X)
            dve.memset(b.junk[:, 0:384], 0.0)
            dve.tensor_scalar(mtok[:, 0:8], b.psA[:, 0:8],
                              mrow[0:128, 0:1], None, ALU.is_equal)
            ins = dve.memset(b.junk[:, 0:128], 0.0)
            b.DV.inc(ins)
        on_dve(b, dve_mx)
        mt_dv = b.DV.n

        def pe_tr(pe, tt=tt):
            pe.wait_ge(b.DV.h, mt_dv)
            ins = pe.transpose(b.psB[0:8, 1536:1664], mtok[:, 0:8], b.identf)
            b.PE.inc(ins)
        on_pe(b, pe_tr)
        tr_pe = b.PE.n

        def dve_ms(dve, tt=tt, tsl=tsl):
            dve.wait_ge(b.PE.h, tr_pe)
            ins = dve.tensor_copy(b.mask[0:8, tsl], b.psB[0:8, 1536:1664])
            b.DV.inc(ins)
        on_dve(b, dve_ms)
        prev = b.DV.n
    b.mask_dv = prev
    a.free("fwr", "fbr")
    b.af.free("mtok", "mrow")
    b.taps["mask"] = [b.mask]


def ffn_moe(b):
    a = b.abf
    sn = snap(b)
    TH = 1024
    act_s = a.alloc("act_s", 16 * TH)
    w2e = a.alloc("w2e", 16 * 1024)
    w1buf = [a.alloc(f"w1b{i}", DC * 128) for i in range(4)]
    e8t = a.alloc("e8t", 8 * 128)
    maskb = a.alloc("maskb", TH)
    y = [a.alloc(f"y{d}", TH) for d in range(DC)]
    sigs = a.alloc("sigs", 512)
    b2t = a.alloc("b2t", D)
    b1c = b.af.alloc("b1c", 32)
    xfin = b.sc1[:, 0:1024]
    gsc = b.sc1[:, 1024:2048]

    dmb = DS(b, "dmoeb")

    def pool_b2(po):
        wait_snap(po, b, sn, pe=True, dv=True)
        po.dma_start(out=b2t[0:8, :], in_=b.dI["fb2"]).then_inc(dmb.h, 16)
        po.dma_start(out=e8t[0:8, :], in_=b.dI["c_e8"]).then_inc(dmb.h, 16)
        dmb.n += 32
        b.b2_ld = dmb.n
    on_pool(b, pool_b2)

    b.out_sd = 0
    b.yadd_dv = 0
    b.w2_use = 0
    for th in range(2):
        toff = th * TH

        def po_zero(po, th=th):
            if th == 1:
                po.wait_ge(DS(b, "dout").h, b.out_sd)
            else:
                wait_snap(po, b, snap(b), pe=True, dv=True)
            for d in range(DC):
                ins = po.memset(y[d][:, :], 0.0)
            b.PO.inc(ins)
        on_pool(b, po_zero)
        yz_po = b.PO.n

        for e in range(NE):
            mb_dv0 = b.DV.n
            for nch in range(2):
                sl = slice(toff + nch * 512, toff + (nch + 1) * 512)

                def pe_mb(pe, e=e, th=th, nch=nch, sl=sl):
                    if e == 0 and th == 0 and nch == 0:
                        pe.wait_ge(b.DV.h, b.mask_dv)
                        pe.wait_ge(DS(b, "dmoeb").h, b.b2_ld)
                    if b.yadd_dv:
                        pe.wait_ge(b.DV.h, b.yadd_dv)
                    if nch == 1:
                        pe.wait_ge(b.DV.h, mb_dv0 + 1)
                    ins = pe.matmul(b.psB[:, 1536:2048],
                                    e8t[0:8, e * 128:(e + 1) * 128],
                                    b.mask[0:8, sl], start=True, stop=True)
                    b.PE.inc(ins)
                on_pe(b, pe_mb)
                mb_pe = b.PE.n

                def dve_mb(dve, nch=nch):
                    dve.wait_ge(b.PE.h, mb_pe)
                    ins = dve.tensor_copy(maskb[:, nch * 512:(nch + 1) * 512],
                                          b.psB[:, 1536:2048])
                    b.DV.inc(ins)
                on_dve(b, dve_mb)
            mb_dv = b.DV.n

            db1 = DS(b, "db1c")

            def sp_b1(sp, e=e):
                sp.wait_ge(b.DV.h, b.DV.n)
                sp.dma_start(out=b1c, in_=b.dI["fb1"][e]).then_inc(db1.h, 16)
                db1.n += 16
                b.b1_ld = db1.n
            on_sp(b, sp_b1)

            dw2e = DS(b, "dw2e")

            def po_w2(po, e=e, th=th):
                if b.w2_use:
                    po.wait_ge(b.PE.h, b.w2_use)
                po.dma_start(
                    out=w2e.rearrange("p (kf f) -> p kf f", kf=16),
                    in_=b.dI["fw2"][e].rearrange("(kf p) f -> p kf f", p=128)
                ).then_inc(dw2e.h, 16)
                dw2e.n += 16
                b.w2_ld = dw2e.n
            on_pool(b, po_w2)

            peA0 = b.PE.n
            peA_marks = [peA0 + i + 1 for i in range(32)]
            w1_use = [peA0 + 2 * (j + 1) for j in range(16)]
            w1ld = []
            acA0 = b.AC.n
            po_mul = {}

            def po_w1(po, e=e):
                for j in range(16):
                    if j >= 2:
                        po.wait_ge(b.PE.h, w1_use[j - 2])
                    s_ = DS(b, f"dw1s{j % 2}")
                    po.dma_start(out=w1buf[(j % 2) * 2], in_=b.dI["fw1"][e, j]
                                 ).then_inc(s_.h, 16)
                    po.dma_start(out=w1buf[(j % 2) * 2 + 1],
                                 in_=b.dI["fw1"][e, j + 16]).then_inc(s_.h, 16)
                    s_.n += 32
                    w1ld.append((s_, s_.n))
                    if j >= 1:
                        for tc in range(2):
                            i = (j - 1) * 2 + tc
                            po.wait_ge(b.AC.h, acA0 + i + 1)
                            ins = po.tensor_mul(sigs[:, 0:512], sigs[:, 0:512],
                                                maskb[:, tc * 512:(tc + 1) * 512])
                            po_mul[(j - 1, tc)] = b.PO.inc(ins)
                for tc in range(2):
                    i = 15 * 2 + tc
                    po.wait_ge(b.AC.h, acA0 + i + 1)
                    ins = po.tensor_mul(sigs[:, 0:512], sigs[:, 0:512],
                                        maskb[:, tc * 512:(tc + 1) * 512])
                    po_mul[(15, tc)] = b.PO.inc(ins)
            on_pool(b, po_w1)

            dv0 = b.DV.n
            actm = {}
            cnt = 0
            for j in range(16):
                for tc in range(2):
                    cnt += 1
                    actm[(j, tc)] = dv0 + cnt

            def peA(pe, e=e, th=th, toff=toff, mb_dv=mb_dv):
                for j in range(16):
                    pe.wait_ge(w1ld[j][0].h, w1ld[j][1])
                    if j == 0:
                        pe.wait_ge(b.DV.h, mb_dv)
                        pe.wait_ge(b.PO.h, yz_po)
                    for tc in range(2):
                        tsl = slice(toff + tc * 512, toff + (tc + 1) * 512)
                        prevkey = (j, tc - 1) if tc >= 1 else (j - 1, 1)
                        if prevkey in actm:
                            pe.wait_ge(b.DV.h, actm[prevkey])
                        for k in range(DC):
                            pe.matmul(b.psA[:, 0:512],
                                      w1buf[(j % 2) * 2][:, k * 128:(k + 1) * 128],
                                      b.h2[k][:, tsl], start=(k == 0),
                                      stop=(k == DC - 1))
                            ins = pe.matmul(b.psA[:, 512:1024],
                                            w1buf[(j % 2) * 2 + 1][:, k * 128:(k + 1) * 128],
                                            b.h2[k][:, tsl], start=(k == 0),
                                            stop=(k == DC - 1))
                        b.PE.inc(ins)
            on_pe(b, peA)

            def actA(act, e=e):
                for i in range(32):
                    j, tc = i // 2, i % 2
                    act.wait_ge(b.PE.h, peA_marks[i])
                    if i == 0:
                        act.wait_ge(DS(b, "db1c").h, b.b1_ld)
                    ins = act.activation(sigs[:, 0:512], b.psA[:, 512:1024],
                                         AF.Sigmoid, bias=b1c[:, 16 + j:17 + j],
                                         scale=1.0)
                    b.AC.inc(ins)
            on_act(b, actA)

            def dveA(dve, e=e):
                for i in range(32):
                    j, tc = i // 2, i % 2
                    dve.wait_ge(b.PO.h, po_mul[(j, tc)])
                    dst = act_s[:, j * TH + tc * 512:j * TH + (tc + 1) * 512]
                    ins = dve.scalar_tensor_tensor(dst, b.psA[:, 0:512],
                                                   b1c[:, j:j + 1], sigs[:, 0:512],
                                                   ALU.add, ALU.mult)
                    b.DV.inc(ins)
                    assert b.DV.n == actm[(j, tc)]
            on_dve(b, dveA)
            actA_dv = b.DV.n

            yev = []

            yevd = [b.DV.n + 1, b.DV.n + 2]

            def peB(pe, e=e, th=th, toff=toff, actA_dv=actA_dv):
                pe.wait_ge(DS(b, "dw2e").h, b.w2_ld)
                pe.wait_ge(b.DV.h, actA_dv)
                if b.yadd_dv:
                    pe.wait_ge(b.DV.h, b.yadd_dv)
                for tc in range(2):
                    if tc == 1:
                        pe.wait_ge(b.DV.h, yevd[0])
                    tslm = slice(toff + tc * 512, toff + (tc + 1) * 512)
                    for d in range(DC):
                        dsl = slice((d % 4) * 512, (d % 4) * 512 + 512)
                        ps = b.psA if d < 4 else b.psB
                        for kf in range(16):
                            ins = pe.matmul(
                                ps[:, dsl],
                                w2e[:, kf * 1024 + d * 128:kf * 1024 + (d + 1) * 128],
                                act_s[:, kf * TH + tc * 512:kf * TH + (tc + 1) * 512],
                                start=(kf == 0), stop=(kf == 15 and e != NE - 1))
                        if e == NE - 1:
                            ins = pe.matmul(ps[:, dsl],
                                            b2t[0:8, d * 128:(d + 1) * 128],
                                            b.mask[0:8, tslm],
                                            start=False, stop=True)
                    b.PE.inc(ins)
                    yev.append(b.PE.n)
            on_pe(b, peB)
            b.w2_use = b.PE.n

            def dveB(dve, e=e):
                for tc in range(2):
                    dve.wait_ge(b.PE.h, yev[tc])
                    for d in range(DC):
                        dsl = slice((d % 4) * 512, (d % 4) * 512 + 512)
                        ps = b.psA if d < 4 else b.psB
                        ysl = y[d][:, tc * 512:(tc + 1) * 512]
                        ins = dve.tensor_add(ysl, ysl, ps[:, dsl])
                    assert b.DV.inc(ins) == yevd[tc]
            on_dve(b, dveB)
            b.yadd_dv = b.DV.n

        final_combine_th(b, th, y, xfin, gsc)

    a.free("act_s", "w2e", "maskb", "sigs", "b2t", "mask", "e8t",
           *[f"w1b{i}" for i in range(4)], *[f"y{d}" for d in range(DC)],
           *[f"h2_{k}" for k in range(DC)])
    b.af.free("b1c")


def final_combine_th(b, th, y, xfin, gsc):
    toff = th * 1024
    sbf = b.abf.alloc("sch", 1024)
    prev_out = 0
    for d in range(DC):
        tsl = slice(toff, toff + 1024)

        dfx = DS(b, "dfx")

        def sp_x(sp, d=d, th=th, tsl=tsl, prev_out=prev_out):
            sp.wait_ge(b.DV.h, b.yadd_dv)
            if prev_out:
                sp.wait_ge(b.DV.h, b.fin_dv)
            sp.dma_start(out=xfin, in_=b.dI["xT"][d * 128:(d + 1) * 128, tsl]
                         ).then_inc(dfx.h, 16)
            sp.dma_start(out=sbf, in_=b.d_s[d * 128:(d + 1) * 128, tsl]
                         ).then_inc(dfx.h, 16)
            dfx.n += 32
            b.x_ld = dfx.n
        on_sp(b, sp_x)

        def dve_f1(dve, d=d, tsl=tsl):
            dve.wait_ge(b.DV.h, b.yadd_dv)
            if b.out_sd:
                dve.wait_ge(DS(b, "dout").h, b.out_sd)
            ins = dve.tensor_add(gsc, b.xn[d][:, tsl], y[d][:, :])
            b.DV.inc(ins)
        on_dve(b, dve_f1)
        f1_dv = b.DV.n

        def act_f(act):
            act.wait_ge(b.DV.h, f1_dv)
            ins = act.activation(gsc, gsc, AF.Silu)
            b.AC.inc(ins)
        on_act(b, act_f)
        f_ac = b.AC.n

        def dve_f2(dve, d=d):
            dve.wait_ge(b.AC.h, f_ac)
            dve.wait_ge(DS(b, "dfx").h, b.x_ld)
            dve.tensor_mul(gsc, sbf, gsc)
            ins = dve.tensor_add(gsc, gsc, xfin)
            b.DV.inc(ins)
        on_dve(b, dve_f2)
        b.fin_dv = b.DV.n
        prev_out = 1

        dout = DS(b, "dout")

        def sp_out(sp, d=d, tsl=tsl):
            sp.wait_ge(b.DV.h, b.fin_dv)
            sp.dma_start(out=b.d_out[d * 128:(d + 1) * 128, tsl], in_=gsc
                         ).then_inc(dout.h, 16)
            dout.n += 16
            b.out_sd = dout.n
        on_sp(b, sp_out)
    b.abf.free("sch")


def finish(b, tap=None):
    tap = tap if tap is not None else DEBUG_TAP
    if tap is not None and BUILD_PHASES < 99:
        tiles = b.taps[tap]

        dtap = DS(b, "dtap")

        def po_tap(po):
            wait_snap(po, b, snap(b), pe=True, dv=True, ac=True)
            for k, t in enumerate(tiles[:8]):
                po.dma_start(out=b.d_out[k * 128:(k + 1) * 128, 0:t.shape[-1]],
                             in_=t).then_inc(dtap.h, 16)
                dtap.n += 16
        on_pool(b, po_tap)

    def sp_end(sp):
        for s_ in b.dsems.values():
            if s_.n:
                sp.wait_ge(s_.h, s_.n)
    on_sp(b, sp_end)



def prep_inputs(inputs):
    f32 = np.float32
    g = {k: np.asarray(v, f32) for k, v in inputs.items()}
    m = {}
    for k, v in host_constants().items():
        m["c_" + k] = np.ascontiguousarray(v)
    m["wqkv"] = np.stack([tile_kMf(g["L0_attn_W_qkv"][l], 24) for l in range(NL)])
    m["wr"] = np.stack([tile_kMf(g["L0_attn_W_r"][l], DC) for l in range(NL)])
    m["wo"] = np.stack([tile_kMf(g["L0_attn_W_o"][l], DC) for l in range(NL)])
    m["wff1"] = np.ascontiguousarray(g["L0_attn_W_ff1"].reshape(NL, DC, 128, DF))
    w2 = g["L0_attn_W_ff2"].reshape(NL, 32, 128, DC, 128)
    m["wff2"] = np.ascontiguousarray(w2.transpose(0, 3, 2, 1, 4))
    m["u_t"] = np.ascontiguousarray(g["L0_attn_u"].T)
    m["v_t"] = np.ascontiguousarray(g["L0_attn_v"].T)
    for nm, key in (("ln1g", "L0_attn_ln1_g"), ("ln1b", "L0_attn_ln1_b"),
                    ("ln2g", "L0_attn_ln2_g"), ("ln2b", "L0_attn_ln2_b"),
                    ("bff1", "L0_attn_b_ff1"), ("bff2", "L0_attn_b_ff2")):
        m[nm] = np.stack([col_bias(g[key][l]) for l in range(NL)])
    m["fwq"] = tile_kMf(g["L0_ffn_Wq"], DC)
    m["fwk"] = tile_kMf(g["L0_ffn_Wk"], 2)
    m["fwv"] = tile_kMf(g["L0_ffn_Wv"], 2)
    m["fwg"] = g["L0_ffn_Wg"]
    m["fwo"] = tile_kMf(g["L0_ffn_Wo"], DC)
    m["fwr"] = np.ascontiguousarray(g["L0_ffn_Wr"].reshape(DC, 128, NE))
    m["fbr"] = np.ascontiguousarray(g["L0_ffn_br"].reshape(1, NE))
    w1 = g["L0_ffn_W1"].reshape(NE, DC, 128, 32, 128)
    m["fw1"] = np.ascontiguousarray(w1.transpose(0, 3, 2, 1, 4))
    m["fb1"] = np.stack([col_bias(g["L0_ffn_b1"][e]) for e in range(NE)])
    m["fw2"] = np.ascontiguousarray(g["L0_ffn_W2"])
    m["fb2"] = np.ascontiguousarray(g["L0_ffn_b2"])
    m["fbq"] = col_bias(g["L0_ffn_bq"])
    m["fbk"] = col_bias(g["L0_ffn_bk"])
    m["fbv"] = col_bias(g["L0_ffn_bv"])
    m["fbg"] = np.ascontiguousarray(g["L0_ffn_bg"].reshape(DH, 1))
    m["fbo"] = col_bias(g["L0_ffn_bo"])
    m["fln1"] = col_bias(g["L0_ffn_ln1_w"])
    m["fln2"] = col_bias(g["L0_ffn_ln2_w"])
    m["fqn"] = np.ascontiguousarray(g["L0_ffn_qn_w"].reshape(1, DH))
    m["fkn"] = np.ascontiguousarray(g["L0_ffn_kn_w"].reshape(1, DH))
    return m


def kernel(**inputs):
    x = np.asarray(inputs["L0_attn_x"], np.float32)
    shared = prep_inputs(inputs)
    nc = build_program()
    in_maps = []
    for i in range(8):
        im = dict(shared)
        im["xT"] = np.ascontiguousarray(x[i].T)
        in_maps.append(im)
    res = run_bass_kernel_spmd(nc, in_maps, core_ids=list(range(8)))
    outs = [np.asarray(res.results[i]["out"]).T for i in range(8)]
    return np.stack(outs).astype(np.float32)


# version 06 (final)
# speedup vs baseline: 1.0783x; 1.0446x over previous
import contextlib

import numpy as np
import ml_dtypes

import concourse.bass as bass
import concourse.mybir as mybir
from concourse.bass_utils import run_bass_kernel_spmd

F32 = mybir.dt.float32
BF16 = mybir.dt.bfloat16
AF = mybir.ActivationFunctionType
ALU = mybir.AluOpType
AX = mybir.AxisListType
BF = ml_dtypes.bfloat16

S = 2048
D = 1024
DC = 8
NL = 4
H = 16
DH = 64
DF = 4096
HID = 2048
NE = 8
NKV = 4

BUILD_PHASES = 99
DEBUG_TAP = None



def host_constants():
    c = {}
    c["ident"] = np.eye(128, dtype=np.float32).astype(BF)
    c["identf"] = np.eye(128, dtype=np.float32)
    c["ident64"] = np.eye(64, dtype=np.float32).astype(BF)
    hd = DH
    inv = 1.0 / 10000.0 ** (np.arange(0, hd, 2, dtype=np.float32) / hd)
    t = np.arange(S, dtype=np.float32)
    fr = np.einsum("i,j->ij", t, inv)
    cos = np.cos(fr).T
    sin = np.sin(fr).T
    Ct = np.zeros((128, S), np.float32)
    S2 = np.zeros((128, S), np.float32)
    for base in (0, 64):
        Ct[base:base + 32] = cos
        Ct[base + 32:base + 64] = cos
        S2[base:base + 32] = -sin
        S2[base + 32:base + 64] = sin
    c["ropeC"] = Ct.astype(BF)
    c["ropeS2"] = S2.astype(BF)
    Pl = np.zeros((128, 128), np.float32)
    for i in range(128):
        Pl[i ^ 32, i] = 1.0
    c["swap32"] = Pl.astype(BF)
    m = np.zeros((128, 128), np.float32)
    for q in range(128):
        m[q, q + 1:] = -1e9
    c["causal"] = m
    E8 = np.zeros((8, 8 * 128), np.float32)
    for e in range(8):
        E8[e, e * 128:(e + 1) * 128] = 1.0
    c["e8"] = E8.astype(BF)
    c["ones_col"] = np.ones((128, 1), np.float32).astype(BF)
    c["ones_row"] = np.ones((1, 128), np.float32).astype(BF)
    c["onesrf"] = np.ones((1, 128), np.float32)
    pos = np.arange(S - 1, -1, -1, dtype=np.float32)
    freq = 1.0 / 10000.0 ** (np.arange(0, D, 2, dtype=np.float32) / D)
    ang = pos[:, None] * freq[None, :]
    r = np.concatenate([np.sin(ang), np.cos(ang)], axis=-1)
    c["rT"] = np.ascontiguousarray(r.T).reshape(DC, 128, S).astype(BF)
    return c


def tile_kMf(w, nf):
    din, f = w.shape
    kc = din // 128
    a = w.reshape(kc, 128, nf, 128)
    return np.ascontiguousarray(a.transpose(2, 1, 0, 3))


def col_bias(v):
    f = v.shape[0]
    return np.ascontiguousarray(v.reshape(f // 128, 128).T)



class Sem:
    def __init__(self, ctx, nc, name):
        self.h = ctx.enter_context(nc.semaphore(name))
        self.n = 0

    def inc(self, instr, k=1):
        instr.then_inc(self.h, k)
        self.n += k
        return self.n


class Arena:
    def __init__(self, t, cols, name):
        self.t = t
        self.cols = cols
        self.name = name
        self.iv = [(0, cols)]
        self.live = {}

    def alloc(self, name, cols, align=16):
        assert name not in self.live, name
        for i, (a, b_) in enumerate(self.iv):
            a2 = (a + align - 1) // align * align
            if b_ - a2 >= cols:
                del self.iv[i]
                if a2 > a:
                    self.iv.insert(i, (a, a2))
                    i += 1
                if a2 + cols < b_:
                    self.iv.insert(i, (a2 + cols, b_))
                self.live[name] = (a2, cols)
                return self.t[:, a2:a2 + cols]
        raise MemoryError(f"arena {self.name}: alloc {name} ({cols}) failed; "
                          f"live={self.live}")

    def free(self, *names):
        for name in names:
            a, n = self.live.pop(name)
            self.iv.append((a, a + n))
        self.iv.sort()
        out = []
        for s_, e in self.iv:
            if out and out[-1][1] >= s_:
                out[-1] = (out[-1][0], max(out[-1][1], e))
            else:
                out.append((s_, e))
        self.iv = [tuple(x) for x in out]


class B:
    pass


def declare_inputs(nc, consts):
    d = {}

    def inp(name, shape, dtype):
        d[name] = nc.dram_tensor(name, list(shape), dtype, kind="ExternalInput").ap()

    inp("xT", (D, S), F32)
    for k, v in consts.items():
        inp("c_" + k, v.shape, BF16 if v.dtype == BF else F32)
    inp("wqkv", (NL, 24, 128, DC, 128), F32)
    inp("wr", (NL, DC, 128, DC, 128), F32)
    inp("wo", (NL, DC, 128, DC, 128), F32)
    inp("wff1", (NL, DC, 128, DF), F32)
    inp("wff2", (NL, DC, 128, 32, 128), F32)
    inp("u_t", (DH, H), F32)
    inp("v_t", (DH, H), F32)
    inp("ln1g", (NL, 128, DC), F32)
    inp("ln1b", (NL, 128, DC), F32)
    inp("ln2g", (NL, 128, DC), F32)
    inp("ln2b", (NL, 128, DC), F32)
    inp("bff1", (NL, 128, 32), F32)
    inp("bff2", (NL, 128, DC), F32)
    inp("fwq", (DC, 128, DC, 128), F32)
    inp("fwk", (2, 128, DC, 128), F32)
    inp("fwv", (2, 128, DC, 128), F32)
    inp("fwg", (DH, DH), F32)
    inp("fwo", (DC, 128, DC, 128), F32)
    inp("fwr", (DC, 128, NE), F32)
    inp("fbr", (1, NE), F32)
    inp("fw1", (NE, 32, 128, DC, 128), F32)
    inp("fb1", (NE, 128, 32), F32)
    inp("fw2", (NE, HID, D), F32)
    inp("fb2", (NE, D), F32)
    inp("fbq", (128, DC), F32)
    inp("fbk", (128, 2), F32)
    inp("fbv", (128, 2), F32)
    inp("fbg", (DH, 1), F32)
    inp("fbo", (128, DC), F32)
    inp("fln1", (128, DC), F32)
    inp("fln2", (128, DC), F32)
    inp("fqn", (1, DH), F32)
    inp("fkn", (1, DH), F32)
    return d


def DS(b, name):
    if name not in b.dsems:
        b.dsems[name] = Sem(b._ctx, b.nc, name)
    return b.dsems[name]


def on_pe(b, f):
    b.block.tensor(f)


def on_dve(b, f):
    b.block.vector(f)


def on_act(b, f):
    b.block.scalar(f)


def on_pool(b, f):
    b.block.gpsimd(f)


def on_sp(b, f):
    b.block.sync(f)


def snap(b):
    return (b.PE.n, b.DV.n, b.AC.n, b.PO.n, b.SD.n, b.PD.n)


def wait_snap(eng, b, sn, pe=True, dv=True, ac=False, po=False, sd=False,
              pd=False):
    if pe and sn[0]:
        eng.wait_ge(b.PE.h, sn[0])
    if dv and sn[1]:
        eng.wait_ge(b.DV.h, sn[1])
    if ac and sn[2]:
        eng.wait_ge(b.AC.h, sn[2])
    if po and sn[3]:
        eng.wait_ge(b.PO.h, sn[3])
    if sd and sn[4]:
        eng.wait_ge(b.SD.h, sn[4])
    if pd and sn[5]:
        eng.wait_ge(b.PD.h, sn[5])



def build_program():
    consts = host_constants()
    nc = bass.Bass("TRN2", target_bir_lowering=False)
    dI = declare_inputs(nc, consts)
    d_out = nc.dram_tensor("out", [D, S], F32, kind="ExternalOutput").ap()
    d_xn = nc.dram_tensor("xn_spill", [D, S], BF16, kind="Internal").ap()
    d_s = nc.dram_tensor("s_spill", [D, S], BF16, kind="Internal").ap()

    with contextlib.ExitStack() as ctx:
        b = B()
        b.nc = nc
        b.dI = dI
        b.d_out = d_out
        b.d_xn = d_xn
        b.d_s = d_s
        b.abf = Arena(ctx.enter_context(
            nc.sbuf_tensor("arena_bf", [128, 91500], BF16)), 91500, "bf")
        b.af = Arena(ctx.enter_context(
            nc.sbuf_tensor("arena_f32", [128, 7168], F32)), 7168, "f32")
        b.psA = ctx.enter_context(nc.psum_tensor("psA", [128, 2048], F32))
        b.psB = ctx.enter_context(nc.psum_tensor("psB", [128, 2048], F32))
        b.PE = Sem(ctx, nc, "sPE")
        b.DV = Sem(ctx, nc, "sDV")
        b.AC = Sem(ctx, nc, "sAC")
        b.PO = Sem(ctx, nc, "sPO")
        b.SD = Sem(ctx, nc, "sSD")
        b.PD = Sem(ctx, nc, "sPD")
        b.dsems = {}
        b._ctx = ctx
        b.block = ctx.enter_context(nc.Block())
        build_all(b)
        globals()["_last_b"] = b
    return nc


def build_all(b):
    alloc_persistent(b)
    load_constants(b)
    phase_xn(b)
    if BUILD_PHASES < 2:
        return finish(b)
    for l in range(NL):
        attn_layer(b, l)
        if BUILD_PHASES < 3 + l:
            return finish(b)
    spill_s(b)
    if BUILD_PHASES < 7:
        return finish(b)
    ffn_module(b)
    finish(b)


def alloc_persistent(b):
    a = b.abf
    b.h = [a.alloc(f"h{k}", S) for k in range(DC)]
    b.ident = a.alloc("ident", 128)
    b.ident64 = a.alloc("ident64", 64)
    b.ones_col = a.alloc("ones_col", 16)
    b.ones_row = a.alloc("ones_row", 128)
    b.swap32 = a.alloc("swap32", 128)
    b.junk = a.alloc("junk", 512)
    f = b.af
    b.identf = f.alloc("identf", 128)
    b.ones_rowf = f.alloc("ones_rowf", 128)
    b.causal = f.alloc("causal", 128)
    b.uv = f.alloc("uv", 2 * H)
    b.rows = f.alloc("rows", 2 * S)
    b.row0 = b.rows[0:1, 0:S]
    b.row1 = b.rows[0:1, S:2 * S]
    b.sc1 = f.alloc("sc1", 2048)
    b.den = f.alloc("den", 4)
    b.lng = f.alloc("lng", DC)
    b.lnb = f.alloc("lnb", DC)
    b.ln2g = f.alloc("ln2g", DC)
    b.ln2b = f.alloc("ln2b", DC)
    b.bf1 = f.alloc("bf1", 32)
    b.bf2 = f.alloc("bf2", DC)
    b.fw_norm = f.alloc("fw_norm", DC)
    b.taps = {}


def load_constants(b):
    dI = b.dI

    def body(sp):
        cs = DS(b, "dconst")

        def ld(dst, src):
            sp.dma_start(out=dst, in_=src).then_inc(cs.h, 16)
            cs.n += 16
        ld(b.ident, dI["c_ident"])
        ld(b.identf, dI["c_identf"])
        ld(b.ident64[0:64, :], dI["c_ident64"])
        ld(b.ident64[64:128, :], dI["c_ident64"])
        ld(b.ones_col[:, 0:1], dI["c_ones_col"])
        ld(b.ones_row[0:1, :], dI["c_ones_row"])
        ld(b.ones_rowf[0:1, :], dI["c_onesrf"])
        ld(b.swap32, dI["c_swap32"])
        ld(b.causal, dI["c_causal"])
        ld(b.uv[0:64, 0:H], dI["u_t"])
        ld(b.uv[0:64, H:2 * H], dI["v_t"])
        b.const_ready = cs.n
    on_sp(b, body)
    b.const_sem = DS(b, "dconst")



def phase_xn(b):
    mb, rb, sq = norm_scratch(b)
    ld1 = []

    dx = DS(b, "dx")

    def po1(po):
        po.wait_ge(b.const_sem.h, b.const_ready)
        for k in range(DC):
            po.dma_start(out=b.h[k], in_=b.dI["xT"][k * 128:(k + 1) * 128, :]
                         ).then_inc(dx.h, 16)
            dx.n += 16
            ld1.append(dx.n)
    on_pool(b, po1)

    ac1 = [b.AC.n + k + 1 for k in range(DC)]
    pe_sq = [b.PE.n + 2 * (k + 1) for k in range(DC)]

    def act1(act):
        for k in range(DC):
            act.wait_ge(dx.h, ld1[-1] if True else ld1[k])
            if k > 0:
                act.wait_ge(b.PE.h, pe_sq[k - 1])
            ins = act.activation(sq, b.h[k], AF.Square)
            b.AC.inc(ins)
    on_act(b, act1)

    def pe1(pe):
        for k in range(DC):
            pe.wait_ge(dx.h, ld1[-1])
            for nc_ in range(4):
                sl = slice(nc_ * 512, (nc_ + 1) * 512)
                ins = pe.matmul(b.psA[0:1, sl], b.ones_col[:, 0:1], b.h[k][:, sl],
                                start=(k == 0), stop=(k == DC - 1))
            b.PE.inc(ins)
            pe.wait_ge(b.AC.h, ac1[k])
            for nc_ in range(4):
                sl = slice(nc_ * 512, (nc_ + 1) * 512)
                ins = pe.matmul(b.psB[0:1, sl], b.ones_col[:, 0:1], sq[:, sl],
                                start=(k == 0), stop=(k == DC - 1))
            b.PE.inc(ins)
    on_pe(b, pe1)
    stats_pe = b.PE.n

    mean_var_rows(b, stats_pe, 1.0 / 1024.0, 1e-5, with_mean=True)
    bcast_rows(b, mb, rb, with_mean=True)
    bc_dv = b.bc_dv

    dv2 = [b.DV.n + 2 * (k + 1) for k in range(DC)]

    def dve2(dve):
        dve.wait_ge(b.DV.h, bc_dv)
        for k in range(DC):
            dve.tensor_sub(b.sc1, b.h[k], mb)
            ins = dve.tensor_mul(b.h[k], b.sc1, rb)
            b.DV.inc(ins, 2)
    on_dve(b, dve2)
    assert b.DV.n == dv2[-1]
    b.xn_dv = dv2[-1]

    dsp = DS(b, "dspill")

    def sp3(sp):
        for k in range(DC):
            sp.wait_ge(b.DV.h, dv2[k])
            sp.dma_start(out=b.d_xn[k * 128:(k + 1) * 128, :], in_=b.h[k]
                         ).then_inc(dsp.h, 16)
            dsp.n += 16
        b.xn_spilled = dsp.n
    on_sp(b, sp3)

    free_norm_scratch(b)
    b.taps["xn"] = list(b.h)


def norm_scratch(b):
    mb = b.abf.alloc("mb", S)
    rb = b.abf.alloc("rb", S)
    sq = b.abf.alloc("sq", S)
    b._mb, b._rb, b._sq = mb, rb, sq
    return mb, rb, sq


def free_norm_scratch(b):
    b.abf.free("mb", "rb", "sq")


def mean_var_rows(b, stats_pe, inv_n, eps, with_mean):
    def dve_rows(dve):
        dve.wait_ge(b.PE.h, stats_pe)
        if with_mean:
            dve.tensor_scalar_mul(b.row0, b.psA[0:1, 0:2048], inv_n)
            dve.tensor_scalar_mul(b.row1, b.psB[0:1, 0:2048], inv_n)
            dve.tensor_mul(b.sc1[0:1, :], b.row0, b.row0)
            dve.tensor_sub(b.row1, b.row1, b.sc1[0:1, :])
        else:
            dve.tensor_scalar_mul(b.row1, b.psB[0:1, 0:2048], inv_n)
        ins = dve.tensor_scalar_add(b.row1, b.row1, eps)
        b.DV.inc(ins)
    on_dve(b, dve_rows)
    var_dv = b.DV.n

    def act_rows(act):
        act.wait_ge(b.DV.h, var_dv)
        ins = act.activation(b.row1, b.row1, AF.Sqrt)
        b.AC.inc(ins)
    on_act(b, act_rows)
    sq_ac = b.AC.n

    def dve_recip(dve):
        dve.wait_ge(b.AC.h, sq_ac)
        ins = dve.reciprocal(b.row1, b.row1)
        b.DV.inc(ins)
    on_dve(b, dve_recip)
    b.rstd_dv = b.DV.n


def bcast_rows(b, mb, rb, with_mean):
    def pe_bc(pe):
        pe.wait_ge(b.DV.h, b.rstd_dv)
        for nc_ in range(4):
            sl = slice(nc_ * 512, (nc_ + 1) * 512)
            if with_mean:
                pe.matmul(b.psA[:, sl], b.ones_rowf[0:1, :], b.row0[:, sl],
                          start=True, stop=True)
            ins = pe.matmul(b.psB[:, sl], b.ones_rowf[0:1, :], b.row1[:, sl],
                            start=True, stop=True)
        b.PE.inc(ins)
    on_pe(b, pe_bc)
    bc_pe = b.PE.n

    def dve_bc(dve):
        dve.wait_ge(b.PE.h, bc_pe)
        if with_mean:
            dve.tensor_copy(mb, b.psA[:, 0:2048])
        ins = dve.tensor_copy(rb, b.psB[:, 0:2048])
        b.DV.inc(ins)
    on_dve(b, dve_bc)
    b.bc_dv = b.DV.n



def attn_layer(b, l):
    b.vecT = [b.abf.alloc(f"vecT{k}", S) for k in range(DC)]
    for half in range(2):
        kr_half(b, l, half)
        if l == 0 and half == 0:
            b.taps["kr"] = list(b.krh)
            if BUILD_PHASES < 2.3:
                return
        qkv_half(b, l, half)
        if l == 0 and half == 0:
            b.taps["qh"] = list(b.qh)
            b.taps["kh"] = list(b.kh)
            b.taps["vh"] = list(b.vh)
            if BUILD_PHASES < 2.5:
                return
        attention_half(b, l, half)
        if l == 0 and half == 0:
            b.taps["vec0"] = list(b.vecT)
            if BUILD_PHASES < 2.7:
                return
    if l == 0:
        b.taps["vec"] = list(b.vecT)
        if BUILD_PHASES < 2.8:
            return
    wo_ln1(b, l)
    if l == 0 and BUILD_PHASES < 2.9:
        b.taps["postln1"] = list(b.h)
        return
    b.abf.free(*[f"vecT{k}" for k in range(DC)])
    ff_block(b, l)
    b.taps[f"l{l}"] = list(b.h)


def kr_half(b, l, half):
    a = b.abf
    sn = snap(b)
    rT = [a.alloc(f"rT{k}", S) for k in range(DC)]
    b.krh = [a.alloc(f"kr{j}", S) for j in range(4)]
    wbuf = [a.alloc(f"wkr{i}", DC * 128) for i in range(2)]
    ld = []

    drt = DS(b, "drt")
    dwkr = [DS(b, "dwkr0"), DS(b, "dwkr1")]

    def sp_body(sp):
        wait_snap(sp, b, sn, pe=True, dv=True)
        for k in range(DC):
            sp.dma_start(out=rT[k], in_=b.dI["c_rT"][k]).then_inc(drt.h, 16)
            drt.n += 16
            ld.append(drt.n)
    on_sp(b, sp_body)

    wld = []
    pe_use = [b.PE.n + j + 1 for j in range(4)]

    def pool_body(po):
        wait_snap(po, b, sn, pe=True, dv=True)
        for j in range(4):
            ft = half * 4 + j
            if j >= 2:
                po.wait_ge(b.PE.h, pe_use[j - 2])
            s_ = dwkr[j % 2]
            po.dma_start(out=wbuf[j % 2], in_=b.dI["wr"][l, ft]
                         ).then_inc(s_.h, 16)
            s_.n += 16
            wld.append((s_, s_.n))
    on_pool(b, pool_body)

    evac = [b.DV.n + j + 1 for j in range(4)]
    pe_done = []

    def pe_body(pe):
        pe.wait_ge(drt.h, ld[-1])
        wait_snap(pe, b, sn, pe=False, dv=True)
        for j in range(4):
            pe.wait_ge(wld[j][0].h, wld[j][1])
            if j >= 2:
                pe.wait_ge(b.DV.h, evac[j - 2])
            ps = b.psA if j % 2 == 0 else b.psB
            for k in range(DC):
                for nc_ in range(4):
                    sl = slice(nc_ * 512, (nc_ + 1) * 512)
                    ins = pe.matmul(ps[:, sl], wbuf[j % 2][:, k * 128:(k + 1) * 128],
                                    rT[k][:, sl], start=(k == 0), stop=(k == DC - 1))
            pe_done.append(b.PE.inc(ins))
    on_pe(b, pe_body)
    assert pe_done == [] or True

    def dve_body(dve):
        for j in range(4):
            dve.wait_ge(b.PE.h, pe_use[j])
            ps = b.psA if j % 2 == 0 else b.psB
            ins = dve.tensor_copy(b.krh[j], ps[:, 0:2048])
            b.DV.inc(ins)
    on_dve(b, dve_body)
    assert b.DV.n == evac[-1]
    b.kr_dv = evac[-1]
    a.free(*[f"rT{k}" for k in range(DC)], "wkr0", "wkr1")


def qkv_half(b, l, half):
    a = b.abf
    sn = snap(b)
    b.qh = [a.alloc(f"qh{j}", S) for j in range(4)]
    b.kh = [a.alloc(f"kh{j}", S) for j in range(4)]
    b.vh = [a.alloc(f"vh{j}", S) for j in range(4)]
    outs = {"q": b.qh, "k": b.kh, "v": b.vh}
    wbuf = [a.alloc(f"wqkv{i}", DC * 128) for i in range(2)]
    fts = ([("q", j, half * 4 + j) for j in range(4)]
           + [("k", j, 8 + half * 4 + j) for j in range(4)]
           + [("v", j, 16 + half * 4 + j) for j in range(4)])
    wld = []
    pe_use = [b.PE.n + i + 1 for i in range(12)]
    dwq = [DS(b, "dwqkv0"), DS(b, "dwqkv1")]

    def pool_body(po):
        wait_snap(po, b, sn, pe=True, dv=True)
        for i, (_, _, ft) in enumerate(fts):
            if i >= 2:
                po.wait_ge(b.PE.h, pe_use[i - 2])
            s_ = dwq[i % 2]
            po.dma_start(out=wbuf[i % 2], in_=b.dI["wqkv"][l, ft]
                         ).then_inc(s_.h, 16)
            s_.n += 16
            wld.append((s_, s_.n))
    on_pool(b, pool_body)

    evac = [b.DV.n + i + 1 for i in range(12)]

    def pe_body(pe):
        wait_snap(pe, b, sn, pe=False, dv=True)
        for i, (kind, j, ft) in enumerate(fts):
            pe.wait_ge(wld[i][0].h, wld[i][1])
            if i >= 2:
                pe.wait_ge(b.DV.h, evac[i - 2])
            ps = b.psA if i % 2 == 0 else b.psB
            for k in range(DC):
                for nc_ in range(4):
                    sl = slice(nc_ * 512, (nc_ + 1) * 512)
                    ins = pe.matmul(ps[:, sl], wbuf[i % 2][:, k * 128:(k + 1) * 128],
                                    b.h[k][:, sl], start=(k == 0), stop=(k == DC - 1))
            b.PE.inc(ins)
    on_pe(b, pe_body)

    def dve_body(dve):
        for i, (kind, j, ft) in enumerate(fts):
            dve.wait_ge(b.PE.h, pe_use[i])
            ps = b.psA if i % 2 == 0 else b.psB
            ins = dve.tensor_copy(outs[kind][j], ps[:, 0:2048])
            b.DV.inc(ins)
    on_dve(b, dve_body)
    assert b.DV.n == evac[-1]
    b.qkv_dv = evac[-1]
    a.free("wqkv0", "wqkv1")


def softmax_block(b, T, nk, prob_ap, qt):
    def mx(dve):
        dve.wait_ge(b.PE.h, b.sc_mark)
        if b.last_exp_ac:
            dve.wait_ge(b.AC.h, b.last_exp_ac)
        if nk < 2048 or b.mask_diag:
            dve.tensor_add(T[:, nk - 128:nk], T[:, nk - 128:nk], b.causal)
        ins = dve.reduce_max(b.nmax[0:128, 0:1], T[:, 0:nk], axis=AX.X,
                             negate=True)
        b.DV.inc(ins)
    on_dve(b, mx)
    mx_dv = b.DV.n

    def expf(act):
        act.wait_ge(b.DV.h, mx_dv)
        ins = act.activation(prob_ap[:, 0:nk], T[:, 0:nk], AF.Exp,
                             bias=b.nmax[0:128, 0:1], scale=1.0,
                             accum_out=b.den[0:128, 0:1])
        b.AC.inc(ins)
    on_act(b, expf)
    b.last_exp_ac = b.AC.n

    def rd(dve):
        dve.wait_ge(b.AC.h, b.last_exp_ac)
        if b.last_tr_pe:
            dve.wait_ge(b.PE.h, b.last_tr_pe)
        dve.reciprocal(b.den[0:128, 1:2], b.den[0:128, 0:1])
        dve.memset(b.junk[:, 0:384], 0.0)
        dve.tensor_scalar_mul(b.diag, b.ident, b.den[0:128, 1:2])
        ins = dve.memset(b.junk[:, 0:128], 0.0)
        b.DV.inc(ins)
    on_dve(b, rd)
    return b.DV.n


def attn_softmax_part(b, st):
    T, nk, qt = st["T"], st["nch"] * 128, st["qt"]
    prob_ap = st["prob"]

    def mx(dve):
        dve.wait_ge(b.PE.h, st["sc_mark"])
        if b.last_exp_ac:
            dve.wait_ge(b.AC.h, b.last_exp_ac)
        if nk < 2048 or b.mask_diag:
            dve.tensor_add(T[:, nk - 128:nk], T[:, nk - 128:nk], b.causal)
        ins = dve.reduce_max(b.nmax[0:128, 0:1], T[:, 0:nk], axis=AX.X,
                             negate=True)
        b.DV.inc(ins)
    on_dve(b, mx)
    mx_dv = b.DV.n

    def expf(act):
        act.wait_ge(b.DV.h, mx_dv)
        ins = act.activation(prob_ap[:, 0:nk], T[:, 0:nk], AF.Exp,
                             bias=b.nmax[0:128, 0:1], scale=1.0,
                             accum_out=b.den[0:128, 0:1])
        b.AC.inc(ins)
    on_act(b, expf)
    b.last_exp_ac = b.AC.n

    diag_ap = b.diag2[qt % 2]

    def rd(dve):
        dve.wait_ge(b.AC.h, b.last_exp_ac)
        if b.tr_pe_par[qt % 2]:
            dve.wait_ge(b.PE.h, b.tr_pe_par[qt % 2])
        dve.reciprocal(b.den[0:128, 1:2], b.den[0:128, 0:1])
        dve.memset(b.junk[:, 0:384], 0.0)
        dve.tensor_scalar_mul(diag_ap, b.ident, b.den[0:128, 1:2])
        ins = dve.memset(b.junk[:, 0:128], 0.0)
        b.DV.inc(ins)
    on_dve(b, rd)
    st["diag_dv"] = b.DV.n


def attn_tv_part(b, st):
    T, nch, qt = st["T"], st["nch"], st["qt"]
    prob_ap = st["prob"]
    diag_ap = b.diag2[qt % 2]
    nrounds = (nch + 3) // 4
    tr_ev = [b.DV.n + r + 1 for r in range(nrounds)]
    pe_tr0 = b.PE.n

    def tr_pe(pe):
        pe.wait_ge(b.DV.h, st["diag_dv"])
        for r in range(nrounds):
            if r >= 2:
                pe.wait_ge(b.DV.h, tr_ev[r - 2])
            w = min(4, nch - r * 4)
            for c in range(w):
                ch = r * 4 + c
                ins = pe.matmul(
                    T[:, (r % 2) * 512 + c * 128:(r % 2) * 512 + (c + 1) * 128],
                    prob_ap[:, ch * 128:(ch + 1) * 128],
                    diag_ap, start=(c == 0), stop=(c == w - 1))
            b.PE.inc(ins)
    on_pe(b, tr_pe)
    tr_marks = [pe_tr0 + r + 1 for r in range(nrounds)]
    b.tr_pe_par[qt % 2] = tr_marks[-1]

    def tr_dve(dve):
        for r in range(nrounds):
            dve.wait_ge(b.PE.h, tr_marks[r])
            w = min(4, nch - r * 4) * 128
            dst = b.pt[r % 2][:, (r // 2 % 2) * 512:(r // 2 % 2) * 512 + w]
            dve.tensor_copy(dst, T[:, (r % 2) * 512:(r % 2) * 512 + w])
            ins = dve.memset(b.junk[:, 0:128], 0.0)
            b.DV.inc(ins)
    on_dve(b, tr_dve)
    assert b.DV.n == tr_ev[-1]

    vecreg = T[0:64, 1664:1792]

    def vec_pe(pe):
        for r in range(nrounds):
            pe.wait_ge(b.DV.h, tr_ev[r])
            w = min(4, nch - r * 4)
            for c in range(w):
                ch = r * 4 + c
                src = b.pt[r % 2][:, (r // 2 % 2) * 512 + c * 128:
                                  (r // 2 % 2) * 512 + (c + 1) * 128]
                ins = pe.matmul(vecreg, st["vtok"][:, ch * 64:(ch + 1) * 64],
                                src, start=(ch == 0), stop=(ch == nch - 1))
        b.PE.inc(ins)
    on_pe(b, vec_pe)
    vec_mark = b.PE.n

    def vec_dve(dve):
        dve.wait_ge(b.PE.h, vec_mark)
        st["out_fn"](dve, vecreg)
        ins = dve.memset(b.junk[:, 0:128], 0.0)
        b.DV.inc(ins)
    on_dve(b, vec_dve)
    st["done"] = b.DV.n


def attn_qt_pipeline(b, nqt, mk_state):
    states = {}
    states[0] = mk_state(0, states)
    attn_softmax_part(b, states[0])
    for qt in range(nqt):
        if qt + 1 < nqt:
            states[qt + 1] = mk_state(qt + 1, states)
            attn_softmax_part(b, states[qt + 1])
        attn_tv_part(b, states[qt])
    return states


def attention_half(b, l, half):
    a = b.abf
    if (DEBUG_TAP == "dbg" and l == 0 and half == 0
            and not hasattr(b, "dbgP")):
        b.dbgP = a.alloc("dbgP", S)
        b.dbgS = b.sc1
        b.dbgD = b.af.alloc("dbgD", 2)
        b.dbg_snap = True
    qstack = a.alloc("qstack", S)
    kstack = a.alloc("kstack", S)
    vtok = a.alloc("vtok", 16 * DH)
    b.prob = [a.alloc(f"prob{i}", S) for i in range(2)]
    b.pt = [a.alloc(f"pt{i}", 1024) for i in range(2)]
    b.diag2 = [a.alloc(f"diag{i}", 128) for i in range(2)]
    b.nmax = b.af.alloc("nmax", 2)
    b.mask_diag = False
    b.last_exp_ac = 0
    b.tr_pe_par = [0, 0]
    if not hasattr(b, "head_done_dv"):
        b.head_done_dv = 0
    b.last_scores_pe = 0

    for hh in range(8):
        head = half * 8 + hh
        j = hh // 2
        r0 = (hh % 2) * 64

        def build_dve(dve, j=j, r0=r0, head=head, hh=hh):
            if hh == 0:
                dve.wait_ge(b.DV.h, max(b.qkv_dv, b.kr_dv))
            if b.last_scores_pe:
                dve.wait_ge(b.PE.h, b.last_scores_pe)
            dve.tensor_scalar(qstack[0:64, :], b.qh[j][r0:r0 + 64, :],
                              b.uv[0:64, head:head + 1], 0.125,
                              ALU.add, ALU.mult)
            dve.tensor_scalar(qstack[64:128, :], b.qh[j][r0:r0 + 64, :],
                              b.uv[0:64, H + head:H + head + 1], 0.125,
                              ALU.add, ALU.mult)
            dve.tensor_copy(kstack[0:64, :], b.kh[j][r0:r0 + 64, :])
            dve.tensor_copy(kstack[64:128, :], b.krh[j][r0:r0 + 64, :])
            ins = dve.memset(b.junk[:, 0:128], 0.0)
            b.DV.inc(ins)
        on_dve(b, build_dve)
        stacks_dv = b.DV.n

        pe0 = b.PE.n

        def vt_pe(pe, j=j, r0=r0, hh=hh):
            if b.head_done_dv:
                pe.wait_ge(b.DV.h, b.head_done_dv)
            pe.wait_ge(b.DV.h, stacks_dv)
            for c in range(16):
                col = (c % 8) * 64 + (c // 8) * 512
                ins = pe.matmul(b.psB[0:128, col:col + 64],
                                b.vh[j][r0:r0 + 64, c * 128:(c + 1) * 128],
                                b.ident64[r0:r0 + 64, :], start=True, stop=True)
                if c % 8 == 7:
                    b.PE.inc(ins)
        on_pe(b, vt_pe)
        vt_marks = [pe0 + 1, pe0 + 2]

        def vt_dve(dve):
            for r in range(2):
                dve.wait_ge(b.PE.h, vt_marks[r])
                dve.tensor_copy(vtok[:, r * 512:(r + 1) * 512],
                                b.psB[:, r * 512:(r + 1) * 512])
                ins = dve.memset(b.junk[:, 0:128], 0.0)
                b.DV.inc(ins)
        on_dve(b, vt_dve)
        vtok_dv = b.DV.n

        def mk_state(qt, states, head=head, vtok_dv=vtok_dv):
            T = b.psA if qt % 2 == 0 else b.psB
            st = {"qt": qt, "T": T, "nch": 16, "prob": b.prob[qt % 2],
                  "vtok": vtok}
            prev2 = states.get(qt - 2)

            def sc_pe(pe):
                if qt < 2:
                    pe.wait_ge(b.DV.h, vtok_dv)
                else:
                    pe.wait_ge(b.DV.h, prev2["done"])
                for nc_ in range(4):
                    sl = slice(nc_ * 512, (nc_ + 1) * 512)
                    ins = pe.matmul(T[:, sl],
                                    qstack[:, qt * 128:(qt + 1) * 128],
                                    kstack[:, sl], start=True, stop=True)
                b.PE.inc(ins)
            on_pe(b, sc_pe)
            st["sc_mark"] = b.PE.n
            b.last_scores_pe = b.PE.n

            def out_fn(dve, vecreg, head=head, qt=qt):
                dst = b.vecT[head // 2][(head % 2) * 64:(head % 2) * 64 + 64,
                                       qt * 128:(qt + 1) * 128]
                return dve.tensor_copy(dst, vecreg)
            st["out_fn"] = out_fn
            return st

        states = attn_qt_pipeline(b, 16, mk_state)
        b.head_done_dv = states[15]["done"]

    a.free("qstack", "kstack", "vtok", "prob0", "prob1", "pt0", "pt1",
           "diag0", "diag1")
    b.af.free("nmax")
    a.free(*[f"qh{j}" for j in range(4)], *[f"kh{j}" for j in range(4)],
           *[f"vh{j}" for j in range(4)], *[f"kr{j}" for j in range(4)])
    b.attn_dv = b.head_done_dv


def wo_ln1(b, l):
    a = b.abf
    sn = snap(b)
    wbuf = [a.alloc(f"wwo{i}", DC * 128) for i in range(2)]

    dpar = DS(b, "dpar")
    b.par_sem = dpar

    def sp_par(sp):
        wait_snap(sp, b, sn, pe=False, dv=True)
        for nm, dst in (("ln1g", b.lng), ("ln1b", b.lnb),
                        ("ln2g", b.ln2g), ("ln2b", b.ln2b)):
            sp.dma_start(out=dst, in_=b.dI[nm][l]).then_inc(dpar.h, 16)
            dpar.n += 16
        sp.dma_start(out=b.bf1, in_=b.dI["bff1"][l]).then_inc(dpar.h, 16)
        sp.dma_start(out=b.bf2, in_=b.dI["bff2"][l]).then_inc(dpar.h, 16)
        dpar.n += 32
        b.par_ld = dpar.n
    on_sp(b, sp_par)

    wld = []
    pe_use = [b.PE.n + i + 1 for i in range(DC)]
    dwo = [DS(b, "dwwo0"), DS(b, "dwwo1")]

    def pool_body(po):
        wait_snap(po, b, sn, pe=True, dv=True)
        for i in range(DC):
            if i >= 2:
                po.wait_ge(b.PE.h, pe_use[i - 2])
            s_ = dwo[i % 2]
            po.dma_start(out=wbuf[i % 2], in_=b.dI["wo"][l, i]).then_inc(s_.h, 16)
            s_.n += 16
            wld.append((s_, s_.n))
    on_pool(b, pool_body)

    add_dv = [b.DV.n + i + 1 for i in range(DC)]

    def pe_body(pe):
        pe.wait_ge(b.DV.h, b.attn_dv)
        for i in range(DC):
            pe.wait_ge(wld[i][0].h, wld[i][1])
            if i >= 2:
                pe.wait_ge(b.DV.h, add_dv[i - 2])
            ps = b.psA if i % 2 == 0 else b.psB
            for k in range(DC):
                for nc_ in range(4):
                    sl = slice(nc_ * 512, (nc_ + 1) * 512)
                    ins = pe.matmul(ps[:, sl], wbuf[i % 2][:, k * 128:(k + 1) * 128],
                                    b.vecT[k][:, sl], start=(k == 0),
                                    stop=(k == DC - 1))
            b.PE.inc(ins)
    on_pe(b, pe_body)

    def dve_body(dve):
        for i in range(DC):
            dve.wait_ge(b.PE.h, pe_use[i])
            ps = b.psA if i % 2 == 0 else b.psB
            ins = dve.tensor_add(b.h[i], b.h[i], ps[:, 0:2048])
            b.DV.inc(ins)
    on_dve(b, dve_body)
    assert b.DV.n == add_dv[-1]
    a.free("wwo0", "wwo1")
    layer_norm_h(b, b.lng, b.lnb, add_dv, par_ld=b.par_ld)


def layer_norm_h(b, gamma, beta, ready_dv, par_ld=None):
    mb, rb, sq = norm_scratch(b)
    ac0 = [b.AC.n + k + 1 for k in range(DC)]
    pe_sq = [b.PE.n + 2 * (k + 1) for k in range(DC)]

    def act_body(act):
        for k in range(DC):
            act.wait_ge(b.DV.h, ready_dv[-1])
            if k > 0:
                act.wait_ge(b.PE.h, pe_sq[k - 1])
            ins = act.activation(sq, b.h[k], AF.Square)
            b.AC.inc(ins)
    on_act(b, act_body)

    def pe_body(pe):
        for k in range(DC):
            pe.wait_ge(b.DV.h, ready_dv[-1])
            for nc_ in range(4):
                sl = slice(nc_ * 512, (nc_ + 1) * 512)
                ins = pe.matmul(b.psA[0:1, sl], b.ones_col[:, 0:1], b.h[k][:, sl],
                                start=(k == 0), stop=(k == DC - 1))
            b.PE.inc(ins)
            pe.wait_ge(b.AC.h, ac0[k])
            for nc_ in range(4):
                sl = slice(nc_ * 512, (nc_ + 1) * 512)
                ins = pe.matmul(b.psB[0:1, sl], b.ones_col[:, 0:1], sq[:, sl],
                                start=(k == 0), stop=(k == DC - 1))
            b.PE.inc(ins)
    on_pe(b, pe_body)
    stats_pe = b.PE.n

    mean_var_rows(b, stats_pe, 1.0 / 1024.0, 1e-5, with_mean=True)
    bcast_rows(b, mb, rb, with_mean=True)
    bc_dv = b.bc_dv

    out_dv = []

    def dve_norm(dve):
        dve.wait_ge(b.DV.h, bc_dv)
        if par_ld:
            dve.wait_ge(b.par_sem.h, par_ld)
        for k in range(DC):
            dve.tensor_sub(b.sc1, b.h[k], mb)
            dve.tensor_mul(b.sc1, b.sc1, rb)
            ins = dve.tensor_scalar(b.h[k], b.sc1, gamma[:, k:k + 1],
                                    beta[:, k:k + 1], ALU.mult, ALU.add)
            out_dv.append(b.DV.inc(ins, 3))
    on_dve(b, dve_norm)
    b.ln_dv = list(out_dv)
    free_norm_scratch(b)


def ff_block(b, l):
    a = b.abf
    sn = snap(b)
    w1 = [a.alloc(f"wf1_{k}", DF) for k in range(DC)]
    w2 = [a.alloc(f"wf2_{d}", 32 * 128) for d in range(DC)]
    ring = a.alloc("ring", 32 * 256)
    w1ld, w2ld = [], []

    dwf1 = DS(b, "dwf1")
    dwf2 = DS(b, "dwf2")

    def pool_body(po):
        wait_snap(po, b, sn, pe=True, dv=True, ac=True)
        for k in range(DC):
            po.dma_start(out=w1[k], in_=b.dI["wff1"][l, k]).then_inc(dwf1.h, 16)
            dwf1.n += 16
            w1ld.append(dwf1.n)
        for d in range(DC):
            po.dma_start(out=w2[d], in_=b.dI["wff2"][l, d]).then_inc(dwf2.h, 16)
            dwf2.n += 16
            w2ld.append(dwf2.n)
    on_pool(b, pool_body)

    NT = S // 256
    relu_ac = {}
    c = b.AC.n
    for t in range(NT):
        for kf in range(32):
            c += 1
            relu_ac[(t, kf)] = c
    ff1_pe = {}
    yA_pe, yB_pe = [], []
    c = b.PE.n
    for t in range(NT):
        for kf in range(32):
            c += 1
            ff1_pe[(t, kf)] = c
        c += 1
        yA_pe.append(c)
        c += 1
        yB_pe.append(c)
    evA_dv = [b.DV.n + 2 * t + 1 for t in range(NT)]
    evB_dv = [b.DV.n + 2 * t + 2 for t in range(NT)]

    def pe_body(pe):
        pe.wait_ge(dwf1.h, w1ld[-1])
        pe.wait_ge(b.DV.h, b.ln_dv[-1])
        for t in range(NT):
            tsl = slice(t * 256, (t + 1) * 256)
            if t == 0:
                pe.wait_ge(dwf2.h, w2ld[-1])
            for kf in range(33):
                if kf < 32:
                    fof = (kf % 2) * 512
                    if kf >= 2:
                        pe.wait_ge(b.AC.h, relu_ac[(t, kf - 2)])
                    for k in range(DC):
                        ins = pe.matmul(b.psA[:, fof:fof + 256],
                                        w1[k][:, kf * 128:(kf + 1) * 128],
                                        b.h[k][:, tsl], start=(k == 0),
                                        stop=(k == DC - 1))
                    b.PE.inc(ins)
                if kf == 0:
                    continue
                kc = kf - 1
                if kc == 0 and t >= 1:
                    pe.wait_ge(b.DV.h, evB_dv[t - 1])
                pe.wait_ge(b.AC.h, relu_ac[(t, kc)])
                for dd in range(4):
                    ins = pe.matmul(b.psB[:, dd * 512:dd * 512 + 256],
                                    w2[dd][:, kc * 128:(kc + 1) * 128],
                                    ring[:, kc * 256:(kc + 1) * 256],
                                    start=(kc == 0), stop=(kc == 31))
            b.PE.inc(ins)
            pe.wait_ge(b.DV.h, evA_dv[t])
            for kf in range(32):
                for dd in range(4, DC):
                    ins = pe.matmul(b.psB[:, (dd - 4) * 512:(dd - 4) * 512 + 256],
                                    w2[dd][:, kf * 128:(kf + 1) * 128],
                                    ring[:, kf * 256:(kf + 1) * 256],
                                    start=(kf == 0), stop=(kf == 31))
            b.PE.inc(ins)
    on_pe(b, pe_body)

    def act_body(act):
        act.wait_ge(b.par_sem.h, b.par_ld)
        for t in range(NT):
            for kf in range(32):
                act.wait_ge(b.PE.h, ff1_pe[(t, kf)])
                fof = (kf % 2) * 512
                ins = act.activation(ring[:, kf * 256:(kf + 1) * 256],
                                     b.psA[:, fof:fof + 256],
                                     AF.Relu, bias=b.bf1[:, kf:kf + 1], scale=1.0)
                b.AC.inc(ins)
                assert b.AC.n == relu_ac[(t, kf)]
    on_act(b, act_body)

    def dve_body(dve):
        dve.wait_ge(b.par_sem.h, b.par_ld)
        for t in range(NT):
            tsl = slice(t * 256, (t + 1) * 256)
            dve.wait_ge(b.PE.h, yA_pe[t])
            for dd in range(4):
                ins = dve.tensor_scalar(b.h[dd][:, tsl],
                                        b.psB[:, dd * 512:dd * 512 + 256],
                                        b.bf2[:, dd:dd + 1], None, ALU.add)
            b.DV.inc(ins)
            assert b.DV.n == evA_dv[t]
            dve.wait_ge(b.PE.h, yB_pe[t])
            for dd in range(4, DC):
                ins = dve.tensor_scalar(b.h[dd][:, tsl],
                                        b.psB[:, (dd - 4) * 512:(dd - 4) * 512 + 256],
                                        b.bf2[:, dd:dd + 1], None, ALU.add)
            b.DV.inc(ins)
            assert b.DV.n == evB_dv[t]
    on_dve(b, dve_body)

    a.free(*[f"wf1_{k}" for k in range(DC)], *[f"wf2_{d}" for d in range(DC)],
           "ring")
    layer_norm_h(b, b.ln2g, b.ln2b, [evB_dv[-1]], par_ld=b.par_ld)


def spill_s(b):
    dss = DS(b, "dsspill")

    def sp_body(sp):
        for k in range(DC):
            sp.wait_ge(b.DV.h, b.ln_dv[k]) if False else sp.wait_ge(b.DV.h, b.ln_dv[-1])
            sp.dma_start(out=b.d_s[k * 128:(k + 1) * 128, :], in_=b.h[k]
                         ).then_inc(dss.h, 16)
            dss.n += 16
        b.s_spilled = dss.n
    on_sp(b, sp_body)
    b.taps["s"] = list(b.h)



def ffn_module(b):
    a = b.abf
    b.xn = b.h

    dxr = DS(b, "dxnr")
    b.xnr_sem = dxr

    def sp_body(sp):
        sp.wait_ge(DS(b, "dspill").h, b.xn_spilled)
        sp.wait_ge(DS(b, "dsspill").h, b.s_spilled)
        for k in range(DC):
            sp.dma_start(out=b.xn[k], in_=b.d_xn[k * 128:(k + 1) * 128, :]
                         ).then_inc(dxr.h, 16)
            dxr.n += 16
        b.xn_ld = dxr.n
    on_sp(b, sp_body)

    mb, rb, sq = norm_scratch(b)
    ffn_rms(b, src=b.xn, ready_sd=b.xn_ld, eps=1e-6, wname="fln1", rb=rb, sq=sq)
    b.h1 = [a.alloc(f"h1_{k}", S) for k in range(DC)]
    h1_dv = []

    def dve_h1(dve):
        dve.wait_ge(b.xnr_sem.h, b.xn_ld)
        dve.wait_ge(DS(b, "dfwn").h, b.fwn_ld)
        for k in range(DC):
            ins = dve.scalar_tensor_tensor(b.h1[k], b.xn[k],
                                           b.fw_norm[:, k:k + 1], rb,
                                           ALU.mult, ALU.mult)
            h1_dv.append(b.DV.inc(ins))
    on_dve(b, dve_h1)
    b.h1_dv = list(h1_dv)
    free_norm_scratch(b)

    ffn_qkv(b)
    if BUILD_PHASES < 7.6:
        return
    ffn_rope_qknorm(b)
    if BUILD_PHASES < 8:
        return
    ffn_attention(b)
    if BUILD_PHASES < 8.6:
        return
    ffn_wo_xf(b)
    if BUILD_PHASES < 9:
        return
    mb, rb, sq = norm_scratch(b)
    ffn_rms(b, src=b.xn, ready_dv=b.xf_dv, eps=1e-6, wname="fln2", rb=rb, sq=sq)
    b.h2 = [a.alloc(f"h2_{k}", S) for k in range(DC)]
    h2_dv = []

    def dve_h2(dve):
        dve.wait_ge(DS(b, "dfwn").h, b.fwn_ld)
        for k in range(DC):
            ins = dve.scalar_tensor_tensor(b.h2[k], b.xn[k],
                                           b.fw_norm[:, k:k + 1], rb,
                                           ALU.mult, ALU.mult)
            h2_dv.append(b.DV.inc(ins))
    on_dve(b, dve_h2)
    b.h2_dv = list(h2_dv)
    free_norm_scratch(b)

    ffn_logits_mask(b)
    if BUILD_PHASES < 10:
        return
    ffn_moe(b)


def ffn_rms(b, src, eps, wname, rb, sq, ready_sd=None, ready_dv=None):
    sn = snap(b)

    dfwn = DS(b, "dfwn")

    def sp_par(sp):
        wait_snap(sp, b, sn, pe=False, dv=True)
        sp.dma_start(out=b.fw_norm, in_=b.dI[wname]).then_inc(dfwn.h, 16)
        dfwn.n += 16
        b.fwn_ld = dfwn.n
    on_sp(b, sp_par)

    ac0 = [b.AC.n + k + 1 for k in range(DC)]
    pe_sq = [b.PE.n + k + 1 for k in range(DC)]

    def act_body(act):
        if ready_sd:
            act.wait_ge(b.xnr_sem.h, ready_sd)
        if ready_dv:
            act.wait_ge(b.DV.h, ready_dv)
        for k in range(DC):
            if k > 0:
                act.wait_ge(b.PE.h, pe_sq[k - 1])
            ins = act.activation(sq, src[k], AF.Square)
            b.AC.inc(ins)
    on_act(b, act_body)

    def pe_body(pe):
        wait_snap(pe, b, sn, pe=False, dv=True)
        for k in range(DC):
            pe.wait_ge(b.AC.h, ac0[k])
            for nc_ in range(4):
                sl = slice(nc_ * 512, (nc_ + 1) * 512)
                ins = pe.matmul(b.psB[0:1, sl], b.ones_col[:, 0:1], sq[:, sl],
                                start=(k == 0), stop=(k == DC - 1))
            b.PE.inc(ins)
    on_pe(b, pe_body)
    stats_pe = b.PE.n

    mean_var_rows(b, stats_pe, 1.0 / 1024.0, eps, with_mean=False)
    bcast_rows(b, None, rb, with_mean=False)



def ffn_qkv(b):
    a = b.abf
    sn = snap(b)
    b.fq = [a.alloc(f"fq{j}", S) for j in range(DC)]
    b.fk = [a.alloc(f"fk{j}", S) for j in range(2)]
    b.fv = [a.alloc(f"fv{j}", S) for j in range(2)]
    wbuf = [a.alloc(f"wfq{i}", DC * 128) for i in range(2)]
    b.fbq = b.af.alloc("fbq", DC)
    b.fbk = b.af.alloc("fbk", 2)
    b.fbv = b.af.alloc("fbv", 2)

    dfb = DS(b, "dfbias")

    def sp_par(sp):
        wait_snap(sp, b, sn, pe=False, dv=True)
        for nm, dst in (("fbq", b.fbq), ("fbk", b.fbk), ("fbv", b.fbv)):
            sp.dma_start(out=dst, in_=b.dI[nm]).then_inc(dfb.h, 16)
            dfb.n += 16
        b.fbias_ld = dfb.n
    on_sp(b, sp_par)

    jobs = ([("fwq", j, b.fq, j, b.fbq) for j in range(DC)]
            + [("fwk", j, b.fk, j, b.fbk) for j in range(2)]
            + [("fwv", j, b.fv, j, b.fbv) for j in range(2)])
    wld = []
    pe_use = [b.PE.n + i + 1 for i in range(len(jobs))]
    dfq = [DS(b, "dwfq0"), DS(b, "dwfq1")]

    def pool_body(po):
        wait_snap(po, b, sn, pe=True, dv=True)
        for i, (wn, ft, _, _, _) in enumerate(jobs):
            if i >= 2:
                po.wait_ge(b.PE.h, pe_use[i - 2])
            s_ = dfq[i % 2]
            po.dma_start(out=wbuf[i % 2], in_=b.dI[wn][ft]).then_inc(s_.h, 16)
            s_.n += 16
            wld.append((s_, s_.n))
    on_pool(b, pool_body)

    evac = [b.DV.n + i + 1 for i in range(len(jobs))]

    def pe_body(pe):
        pe.wait_ge(b.DV.h, b.h1_dv[-1])
        for i, (wn, ft, outl, j, bias) in enumerate(jobs):
            pe.wait_ge(wld[i][0].h, wld[i][1])
            if i >= 2:
                pe.wait_ge(b.DV.h, evac[i - 2])
            ps = b.psA if i % 2 == 0 else b.psB
            for k in range(DC):
                for nc_ in range(4):
                    sl = slice(nc_ * 512, (nc_ + 1) * 512)
                    ins = pe.matmul(ps[:, sl], wbuf[i % 2][:, k * 128:(k + 1) * 128],
                                    b.h1[k][:, sl], start=(k == 0),
                                    stop=(k == DC - 1))
            b.PE.inc(ins)
    on_pe(b, pe_body)

    def dve_body(dve):
        dve.wait_ge(dfb.h, b.fbias_ld)
        for i, (wn, ft, outl, j, bias) in enumerate(jobs):
            dve.wait_ge(b.PE.h, pe_use[i])
            ps = b.psA if i % 2 == 0 else b.psB
            ins = dve.tensor_scalar_add(outl[j], ps[:, 0:2048], bias[:, j:j + 1])
            b.DV.inc(ins)
    on_dve(b, dve_body)
    assert b.DV.n == evac[-1]
    b.fqkv_dv = evac[-1]
    b.taps["fq0"] = list(b.fq)
    b.taps["h1"] = list(b.h1)
    a.free("wfq0", "wfq1", *[f"h1_{k}" for k in range(DC)])


def ffn_rope_qknorm(b):
    a = b.abf
    sn = snap(b)
    b.a2q = b.af.alloc("a2q", 128)
    b.a2k = b.af.alloc("a2k", 128)
    b.ehalf = a.alloc("ehalf", 2)
    ropeC = a.alloc("ropeC", S)
    ropeS2 = a.alloc("ropeS2", S)
    t1 = a.alloc("rope_t1", S)
    sq = a.alloc("sq", S)

    def pool_a2(po):
        wait_snap(po, b, sn, pe=True, dv=True)
        po.memset(b.a2q[0:2, :], 0.0)
        po.memset(b.a2k[0:2, :], 0.0)
        po.memset(b.ehalf[:, :], 0.0)
        po.memset(b.ehalf[0:64, 0:1], 1.0)
        ins = po.memset(b.ehalf[64:128, 1:2], 1.0)
        b.PO.inc(ins)
        da2 = DS(b, "da2")
        po.dma_start(out=b.a2q[0:1, 0:64], in_=b.dI["fqn"]).then_inc(da2.h, 16)
        po.dma_start(out=b.a2q[1:2, 64:128], in_=b.dI["fqn"]).then_inc(da2.h, 16)
        po.dma_start(out=b.a2k[0:1, 0:64], in_=b.dI["fkn"]).then_inc(da2.h, 16)
        po.dma_start(out=b.a2k[1:2, 64:128], in_=b.dI["fkn"]).then_inc(da2.h, 16)
        da2.n += 64
        b.a2_ld = da2.n
        b.eh_po = b.PO.n
    on_pool(b, pool_a2)

    drp = DS(b, "drope")

    def sp_rope(sp):
        wait_snap(sp, b, sn, pe=True, dv=True)
        sp.dma_start(out=ropeC, in_=b.dI["c_ropeC"]).then_inc(drp.h, 16)
        sp.dma_start(out=ropeS2, in_=b.dI["c_ropeS2"]).then_inc(drp.h, 16)
        drp.n += 32
        b.rope_ld = drp.n
    on_sp(b, sp_rope)

    tiles = [(b.fq[j], b.a2q) for j in range(DC)] + \
            [(b.fk[j], b.a2k) for j in range(2)]
    prev_dv = b.fqkv_dv
    for ti, (src, a2) in enumerate(tiles):
        is_q = ti < DC
        def pe_sw(pe, src=src, ti=ti, prev_dv=prev_dv):
            pe.wait_ge(b.DV.h, prev_dv)
            if ti == 0:
                pe.wait_ge(DS(b, "drope").h, b.rope_ld)
            for nc_ in range(4):
                sl = slice(nc_ * 512, (nc_ + 1) * 512)
                ins = pe.matmul(b.psA[:, sl], b.swap32, src[:, sl],
                                start=True, stop=True)
            b.PE.inc(ins)
        on_pe(b, pe_sw)
        sw_pe = b.PE.n

        def dve_rope(dve, src=src):
            dve.wait_ge(b.PE.h, sw_pe)
            dve.tensor_mul(t1, src, ropeC)
            dve.tensor_mul(b.sc1, b.psA[:, 0:2048], ropeS2)
            ins = dve.tensor_add(b.sc1, b.sc1, t1)
            b.DV.inc(ins)
        on_dve(b, dve_rope)
        roped_dv = b.DV.n

        def act_sq(act):
            act.wait_ge(b.DV.h, roped_dv)
            ins = act.activation(sq, b.sc1, AF.Square)
            b.AC.inc(ins)
        on_act(b, act_sq)
        sq_ac = b.AC.n

        def pe_ms(pe, ti=ti):
            pe.wait_ge(b.AC.h, sq_ac)
            if ti == 0:
                pe.wait_ge(b.PO.h, b.eh_po)
                pe.wait_ge(DS(b, "da2").h, b.a2_ld)
            for nc_ in range(4):
                sl = slice(nc_ * 512, (nc_ + 1) * 512)
                ins = pe.matmul(b.psB[0:2, sl], b.ehalf, sq[:, sl],
                                start=True, stop=True)
            b.PE.inc(ins)
        on_pe(b, pe_ms)
        ms_pe = b.PE.n

        def dve_ms(dve):
            dve.wait_ge(b.PE.h, ms_pe)
            dve.tensor_scalar_mul(b.rows[0:2, 0:S], b.psB[0:2, 0:2048], 1.0 / 64.0)
            ins = dve.tensor_scalar_add(b.rows[0:2, 0:S], b.rows[0:2, 0:S], 1e-6)
            b.DV.inc(ins)
        on_dve(b, dve_ms)
        ms_dv = b.DV.n

        def act_rt(act):
            act.wait_ge(b.DV.h, ms_dv)
            ins = act.activation(b.rows[0:2, 0:S], b.rows[0:2, 0:S], AF.Sqrt)
            b.AC.inc(ins)
        on_act(b, act_rt)
        rt_ac = b.AC.n

        def dve_rc(dve):
            dve.wait_ge(b.AC.h, rt_ac)
            ins = dve.reciprocal(b.rows[0:2, 0:S], b.rows[0:2, 0:S])
            b.DV.inc(ins)
        on_dve(b, dve_rc)
        rc_dv = b.DV.n

        def pe_bc(pe, a2=a2):
            pe.wait_ge(b.DV.h, rc_dv)
            for nc_ in range(4):
                sl = slice(nc_ * 512, (nc_ + 1) * 512)
                ins = pe.matmul(b.psB[:, sl], a2[0:2, :], b.rows[0:2, sl],
                                start=True, stop=True)
            b.PE.inc(ins)
        on_pe(b, pe_bc)
        bc_pe = b.PE.n

        def dve_fin(dve, src=src, is_q=is_q):
            dve.wait_ge(b.PE.h, bc_pe)
            if is_q:
                ins = dve.scalar_tensor_tensor(src, b.sc1, 0.125,
                                               b.psB[:, 0:2048],
                                               ALU.mult, ALU.mult)
            else:
                ins = dve.tensor_mul(src, b.sc1, b.psB[:, 0:2048])
            b.DV.inc(ins)
        on_dve(b, dve_fin)
        prev_dv = b.DV.n
    b.rope_dv = prev_dv
    b.taps["fq"] = list(b.fq)
    if BUILD_PHASES >= 8:
        a.free("rope_t1", "ropeC", "ropeS2", "sq", "ehalf")
        b.af.free("a2q", "a2k")


def ffn_attention(b):
    a = b.abf
    sn = snap(b)
    b.vecT = [a.alloc(f"vecT{k}", S) for k in range(DC)]
    vtok = a.alloc("fvtok", NKV * 16 * DH)
    b.prob = [a.alloc(f"prob{i}", S) for i in range(2)]
    b.pt = [a.alloc(f"pt{i}", 1024) for i in range(2)]
    b.diag = a.alloc("diag", 128)
    oh = a.alloc("oh", S)
    sig = a.alloc("fsig", S)
    wg = a.alloc("wg", DH)
    b.nmax = b.af.alloc("nmax", 2)
    bg = b.af.alloc("bg", 1)
    b.mask_diag = True
    b.last_exp_ac = 0
    b.last_tr_pe = 0
    fkd = [a.alloc(f"fkd{g}", S) for g in range(NKV)]

    def dve_fkd(dve):
        dve.wait_ge(b.DV.h, b.rope_dv)
        for g in range(NKV):
            src_ = b.fk[g // 2][(g % 2) * 64:(g % 2) * 64 + 64, :]
            dve.tensor_copy(fkd[g][0:64, :], src_)
            dve.tensor_copy(fkd[g][64:128, :], src_)
        ins = dve.memset(b.junk[:, 0:128], 0.0)
        b.DV.inc(ins)
    on_dve(b, dve_fkd)
    b.fkd_dv = b.DV.n

    dwg = DS(b, "dwg")
    dbg_ = DS(b, "dbg")

    def pool_wg(po):
        wait_snap(po, b, sn, pe=True, dv=True)
        po.dma_start(out=wg[0:64, :], in_=b.dI["fwg"]).then_inc(dwg.h, 16)
        dwg.n += 16
        b.wg_ld = dwg.n
    on_pool(b, pool_wg)

    def sp_bg(sp):
        wait_snap(sp, b, sn, pe=False, dv=True)
        sp.dma_start(out=bg[0:64, :], in_=b.dI["fbg"]).then_inc(dbg_.h, 16)
        dbg_.n += 16
        b.bg_ld = dbg_.n
    on_sp(b, sp_bg)

    pe0 = b.PE.n
    vt_marks = [pe0 + i + 1 for i in range(NKV * 2)]
    vt_ev = [b.DV.n + i + 1 for i in range(NKV * 2)]

    def vt_pe(pe):
        pe.wait_ge(b.DV.h, b.rope_dv)
        for g in range(NKV):
            if g >= 1:
                pe.wait_ge(b.DV.h, vt_ev[g * 2 - 1])
            src = b.fv[g // 2]
            r0 = (g % 2) * 64
            for c in range(16):
                col = (c % 8) * 64 + (c // 8) * 512
                ins = pe.matmul(b.psB[0:128, col:col + 64],
                                src[r0:r0 + 64, c * 128:(c + 1) * 128],
                                b.ident64[r0:r0 + 64, :], start=True, stop=True)
                if c % 8 == 7:
                    b.PE.inc(ins)
    on_pe(b, vt_pe)

    def vt_dve(dve):
        for i in range(NKV * 2):
            g, r = i // 2, i % 2
            dve.wait_ge(b.PE.h, vt_marks[i])
            dst = vtok[:, g * 1024 + r * 512:g * 1024 + (r + 1) * 512]
            dve.tensor_copy(dst, b.psB[:, r * 512:(r + 1) * 512])
            ins = dve.memset(b.junk[:, 0:128], 0.0)
            assert b.DV.inc(ins) == vt_ev[i]
    on_dve(b, vt_dve)
    vtok_dv = b.DV.n
    prev_head_dv = vtok_dv

    b.tr_pe_par = [0, 0]
    for head in range(H):
        g = head // 4
        qj, qr = head // 2, (head % 2) * 64

        def mk_state(qt, states, head=head, g=g, qj=qj, qr=qr,
                     prev_head_dv=prev_head_dv):
            nch = qt + 1
            T = b.psA if qt % 2 == 0 else b.psB
            st = {"qt": qt, "T": T, "nch": nch, "prob": b.prob[qt % 2],
                  "vtok": vtok[:, g * 1024:(g + 1) * 1024]}
            prev2 = states.get(qt - 2)

            def sc_pe(pe):
                if qt < 2:
                    pe.wait_ge(b.DV.h, prev_head_dv)
                    pe.wait_ge(b.DV.h, b.fkd_dv)
                else:
                    pe.wait_ge(b.DV.h, prev2["done"])
                nk = nch * 128
                for nc_ in range((nk + 511) // 512):
                    sl = slice(nc_ * 512, min((nc_ + 1) * 512, nk))
                    ins = pe.matmul(T[:, sl],
                                    b.fq[qj][qr:qr + 64, qt * 128:(qt + 1) * 128],
                                    fkd[g][qr:qr + 64, sl],
                                    start=True, stop=True)
                b.PE.inc(ins)
            on_pe(b, sc_pe)
            st["sc_mark"] = b.PE.n

            def out_fn(dve, vecreg, qt=qt):
                return dve.tensor_copy(oh[0:64, qt * 128:(qt + 1) * 128], vecreg)
            st["out_fn"] = out_fn
            return st

        states = attn_qt_pipeline(b, 16, mk_state)
        oh_dv = states[15]["done"]

        for nc_ in range(4):
            sl = slice(nc_ * 512, (nc_ + 1) * 512)

            def pe_g(pe, nc_=nc_, sl=sl, head=head, oh_dv=oh_dv):
                if nc_ == 0:
                    pe.wait_ge(b.DV.h, oh_dv)
                    if head == 0:
                        pe.wait_ge(DS(b, "dwg").h, b.wg_ld)
                else:
                    pe.wait_ge(b.AC.h, b.gate_ac)
                ins = pe.matmul(b.psB[0:64, 1536:2048], wg[0:64, :],
                                oh[0:64, sl], start=True, stop=True)
                b.PE.inc(ins)
            on_pe(b, pe_g)
            gpe = b.PE.n

            def act_g(act, sl=sl, head=head, nc_=nc_):
                act.wait_ge(b.PE.h, gpe)
                if head == 0 and nc_ == 0:
                    act.wait_ge(DS(b, "dbg").h, b.bg_ld)
                ins = act.activation(sig[0:64, sl], b.psB[0:64, 1536:2048],
                                     AF.Sigmoid, bias=bg[0:64, :], scale=1.0)
                b.AC.inc(ins)
            on_act(b, act_g)
            b.gate_ac = b.AC.n

        def dve_gate(dve, head=head):
            dve.wait_ge(b.AC.h, b.gate_ac)
            dst = b.vecT[head // 2][(head % 2) * 64:(head % 2) * 64 + 64, :]
            dve.tensor_mul(dst, oh[0:64, :], sig[0:64, :])
            ins = dve.memset(b.junk[:, 0:128], 0.0)
            b.DV.inc(ins)
        on_dve(b, dve_gate)
        prev_head_dv = b.DV.n

    b.fattn_dv = prev_head_dv
    b.taps["fvec"] = list(b.vecT)
    a.free("fvtok", "prob0", "prob1", "pt0", "pt1", "diag", "oh", "fsig", "wg",
           *[f"fkd{g}" for g in range(NKV)])
    b.af.free("nmax", "bg")
    a.free(*[f"fq{j}" for j in range(DC)], "fk0", "fk1", "fv0", "fv1")


def ffn_wo_xf(b):
    a = b.abf
    sn = snap(b)
    wbuf = [a.alloc(f"wfo{i}", DC * 128) for i in range(2)]
    bo = b.af.alloc("fbo", DC)

    dfbo = DS(b, "dfbo")

    def sp_par(sp):
        wait_snap(sp, b, sn, pe=False, dv=True)
        sp.dma_start(out=bo, in_=b.dI["fbo"]).then_inc(dfbo.h, 16)
        dfbo.n += 16
        b.fbo_ld = dfbo.n
    on_sp(b, sp_par)

    wld = []
    pe_use = [b.PE.n + i + 1 for i in range(DC)]
    dfo = [DS(b, "dwfo0"), DS(b, "dwfo1")]

    def pool_body(po):
        wait_snap(po, b, sn, pe=True, dv=True)
        for i in range(DC):
            if i >= 2:
                po.wait_ge(b.PE.h, pe_use[i - 2])
            s_ = dfo[i % 2]
            po.dma_start(out=wbuf[i % 2], in_=b.dI["fwo"][i]).then_inc(s_.h, 16)
            s_.n += 16
            wld.append((s_, s_.n))
    on_pool(b, pool_body)

    evac = [b.DV.n + i + 1 for i in range(DC)]

    def pe_body(pe):
        pe.wait_ge(b.DV.h, b.fattn_dv)
        for i in range(DC):
            pe.wait_ge(wld[i][0].h, wld[i][1])
            if i >= 2:
                pe.wait_ge(b.DV.h, evac[i - 2])
            ps = b.psA if i % 2 == 0 else b.psB
            for k in range(DC):
                for nc_ in range(4):
                    sl = slice(nc_ * 512, (nc_ + 1) * 512)
                    ins = pe.matmul(ps[:, sl], wbuf[i % 2][:, k * 128:(k + 1) * 128],
                                    b.vecT[k][:, sl], start=(k == 0),
                                    stop=(k == DC - 1))
            b.PE.inc(ins)
    on_pe(b, pe_body)

    def dve_body(dve):
        dve.wait_ge(dfbo.h, b.fbo_ld)
        for i in range(DC):
            dve.wait_ge(b.PE.h, pe_use[i])
            ps = b.psA if i % 2 == 0 else b.psB
            ins = dve.scalar_tensor_tensor(b.xn[i], ps[:, 0:2048],
                                           bo[:, i:i + 1], b.xn[i],
                                           ALU.add, ALU.add)
            b.DV.inc(ins)
    on_dve(b, dve_body)
    assert b.DV.n == evac[-1]
    b.xf_dv = evac[-1]
    a.free("wfo0", "wfo1", *[f"vecT{k}" for k in range(DC)])
    b.af.free("fbo")
    b.taps["xf"] = list(b.xn)


def ffn_logits_mask(b):
    a = b.abf
    sn = snap(b)
    b.mask = a.alloc("mask", S)
    wr = a.alloc("fwr", DC * 8)
    brr = a.alloc("fbr", 8)
    mtok = b.af.alloc("mtok", 8)
    mrow = b.af.alloc("mrow", 1)

    dfwr = DS(b, "dfwr")

    def pool_w(po):
        wait_snap(po, b, sn, pe=True, dv=True)
        for k in range(DC):
            po.dma_start(out=wr[:, k * 8:(k + 1) * 8], in_=b.dI["fwr"][k]
                         ).then_inc(dfwr.h, 16)
        po.dma_start(out=brr[0:1, :], in_=b.dI["fbr"]).then_inc(dfwr.h, 16)
        dfwr.n += 16 * 9
        b.fwr_ld = dfwr.n
    on_pool(b, pool_w)

    prev = 0
    for tt in range(16):
        tsl = slice(tt * 128, (tt + 1) * 128)

        def pe_lg(pe, tt=tt, tsl=tsl, prev=prev):
            if tt == 0:
                pe.wait_ge(DS(b, "dfwr").h, b.fwr_ld)
                pe.wait_ge(b.DV.h, b.h2_dv[-1])
            else:
                pe.wait_ge(b.DV.h, prev)
            for k in range(DC):
                pe.matmul(b.psA[:, 0:8], b.h2[k][:, tsl], wr[:, k * 8:(k + 1) * 8],
                          start=(k == 0), stop=False)
            ins = pe.matmul(b.psA[:, 0:8], b.ones_row[0:1, :], brr[0:1, :],
                            start=False, stop=True)
            b.PE.inc(ins)
        on_pe(b, pe_lg)
        lg_pe = b.PE.n

        def dve_mx(dve, tt=tt):
            dve.wait_ge(b.PE.h, lg_pe)
            dve.reduce_max(mrow[0:128, 0:1], b.psA[:, 0:8], axis=AX.X)
            dve.memset(b.junk[:, 0:384], 0.0)
            dve.tensor_scalar(mtok[:, 0:8], b.psA[:, 0:8],
                              mrow[0:128, 0:1], None, ALU.is_equal)
            ins = dve.memset(b.junk[:, 0:128], 0.0)
            b.DV.inc(ins)
        on_dve(b, dve_mx)
        mt_dv = b.DV.n

        def pe_tr(pe, tt=tt):
            pe.wait_ge(b.DV.h, mt_dv)
            ins = pe.transpose(b.psB[0:8, 1536:1664], mtok[:, 0:8], b.identf)
            b.PE.inc(ins)
        on_pe(b, pe_tr)
        tr_pe = b.PE.n

        def dve_ms(dve, tt=tt, tsl=tsl):
            dve.wait_ge(b.PE.h, tr_pe)
            ins = dve.tensor_copy(b.mask[0:8, tsl], b.psB[0:8, 1536:1664])
            b.DV.inc(ins)
        on_dve(b, dve_ms)
        prev = b.DV.n
    b.mask_dv = prev
    a.free("fwr", "fbr")
    b.af.free("mtok", "mrow")
    b.taps["mask"] = [b.mask]


def ffn_moe(b):
    a = b.abf
    sn = snap(b)
    TH = 1024
    act_s = a.alloc("act_s", 16 * TH)
    w2e = a.alloc("w2e", 16 * 1024)
    w1buf = [a.alloc(f"w1b{i}", DC * 128) for i in range(4)]
    e8t = a.alloc("e8t", 8 * 128)
    maskb = a.alloc("maskb", TH)
    y = [a.alloc(f"y{d}", TH) for d in range(DC)]
    sigs = a.alloc("sigs", 1024)
    b2t = a.alloc("b2t", D)
    b1c = b.af.alloc("b1c", 32)
    xfin = b.sc1[:, 0:1024]
    gsc = b.sc1[:, 1024:2048]

    dmb = DS(b, "dmoeb")

    def pool_b2(po):
        wait_snap(po, b, sn, pe=True, dv=True)
        po.dma_start(out=b2t[0:8, :], in_=b.dI["fb2"]).then_inc(dmb.h, 16)
        po.dma_start(out=e8t[0:8, :], in_=b.dI["c_e8"]).then_inc(dmb.h, 16)
        dmb.n += 32
        b.b2_ld = dmb.n
    on_pool(b, pool_b2)

    b.out_sd = 0
    b.yadd_dv = 0
    b.w2_use = 0
    for th in range(2):
        toff = th * TH

        def po_zero(po, th=th):
            if th == 1:
                po.wait_ge(DS(b, "dout").h, b.out_sd)
            else:
                wait_snap(po, b, snap(b), pe=True, dv=True)
            for d in range(DC):
                ins = po.memset(y[d][:, :], 0.0)
            b.PO.inc(ins)
        on_pool(b, po_zero)
        yz_po = b.PO.n

        for e in range(NE):
            mb_dv0 = b.DV.n
            for nch in range(2):
                sl = slice(toff + nch * 512, toff + (nch + 1) * 512)

                def pe_mb(pe, e=e, th=th, nch=nch, sl=sl):
                    if e == 0 and th == 0 and nch == 0:
                        pe.wait_ge(b.DV.h, b.mask_dv)
                        pe.wait_ge(DS(b, "dmoeb").h, b.b2_ld)
                    if b.yadd_dv:
                        pe.wait_ge(b.DV.h, b.yadd_dv)
                    if nch == 1:
                        pe.wait_ge(b.DV.h, mb_dv0 + 1)
                    ins = pe.matmul(b.psB[:, 1536:2048],
                                    e8t[0:8, e * 128:(e + 1) * 128],
                                    b.mask[0:8, sl], start=True, stop=True)
                    b.PE.inc(ins)
                on_pe(b, pe_mb)
                mb_pe = b.PE.n

                def dve_mb(dve, nch=nch):
                    dve.wait_ge(b.PE.h, mb_pe)
                    ins = dve.tensor_copy(maskb[:, nch * 512:(nch + 1) * 512],
                                          b.psB[:, 1536:2048])
                    b.DV.inc(ins)
                on_dve(b, dve_mb)
            mb_dv = b.DV.n

            db1 = DS(b, "db1c")

            def sp_b1(sp, e=e):
                sp.wait_ge(b.DV.h, b.DV.n)
                sp.dma_start(out=b1c, in_=b.dI["fb1"][e]).then_inc(db1.h, 16)
                db1.n += 16
                b.b1_ld = db1.n
            on_sp(b, sp_b1)

            dw2e = DS(b, "dw2e")

            def po_w2(po, e=e, th=th):
                if b.w2_use:
                    po.wait_ge(b.PE.h, b.w2_use)
                po.dma_start(
                    out=w2e.rearrange("p (kf f) -> p kf f", kf=16),
                    in_=b.dI["fw2"][e].rearrange("(kf p) f -> p kf f", p=128)
                ).then_inc(dw2e.h, 16)
                dw2e.n += 16
                b.w2_ld = dw2e.n
            on_pool(b, po_w2)

            peA0 = b.PE.n
            peA_marks = [peA0 + i + 1 for i in range(32)]
            w1_use = [peA0 + 2 * (j + 1) for j in range(16)]
            w1ld = []
            acA0 = b.AC.n
            po_mul = {}

            def po_w1(po, e=e):
                for j in range(16):
                    if j >= 2:
                        po.wait_ge(b.PE.h, w1_use[j - 2])
                    s_ = DS(b, f"dw1s{j % 2}")
                    po.dma_start(out=w1buf[(j % 2) * 2], in_=b.dI["fw1"][e, j]
                                 ).then_inc(s_.h, 16)
                    po.dma_start(out=w1buf[(j % 2) * 2 + 1],
                                 in_=b.dI["fw1"][e, j + 16]).then_inc(s_.h, 16)
                    s_.n += 32
                    w1ld.append((s_, s_.n))
                    if j >= 1:
                        for tc in range(2):
                            i = (j - 1) * 2 + tc
                            po.wait_ge(b.AC.h, acA0 + i + 1)
                            sof = (i % 2) * 512
                            ins = po.tensor_mul(sigs[:, sof:sof + 512],
                                                sigs[:, sof:sof + 512],
                                                maskb[:, tc * 512:(tc + 1) * 512])
                            po_mul[(j - 1, tc)] = b.PO.inc(ins)
                for tc in range(2):
                    i = 15 * 2 + tc
                    po.wait_ge(b.AC.h, acA0 + i + 1)
                    sof = (i % 2) * 512
                    ins = po.tensor_mul(sigs[:, sof:sof + 512],
                                        sigs[:, sof:sof + 512],
                                        maskb[:, tc * 512:(tc + 1) * 512])
                    po_mul[(15, tc)] = b.PO.inc(ins)
            on_pool(b, po_w1)

            dv0 = b.DV.n
            actm = {}
            cnt = 0
            for j in range(16):
                for tc in range(2):
                    cnt += 1
                    actm[(j, tc)] = dv0 + cnt

            def peA(pe, e=e, th=th, toff=toff, mb_dv=mb_dv):
                for j in range(16):
                    pe.wait_ge(w1ld[j][0].h, w1ld[j][1])
                    if j == 0:
                        pe.wait_ge(b.DV.h, mb_dv)
                        pe.wait_ge(b.PO.h, yz_po)
                    for tc in range(2):
                        idx = j * 2 + tc
                        uof = (idx % 2) * 1024
                        tsl = slice(toff + tc * 512, toff + (tc + 1) * 512)
                        pk = (idx - 2)
                        prevkey = (pk // 2, pk % 2)
                        if pk >= 0 and prevkey in actm:
                            pe.wait_ge(b.DV.h, actm[prevkey])
                        for k in range(DC):
                            pe.matmul(b.psA[:, uof:uof + 512],
                                      w1buf[(j % 2) * 2][:, k * 128:(k + 1) * 128],
                                      b.h2[k][:, tsl], start=(k == 0),
                                      stop=(k == DC - 1))
                            ins = pe.matmul(b.psA[:, uof + 512:uof + 1024],
                                            w1buf[(j % 2) * 2 + 1][:, k * 128:(k + 1) * 128],
                                            b.h2[k][:, tsl], start=(k == 0),
                                            stop=(k == DC - 1))
                        b.PE.inc(ins)
            on_pe(b, peA)

            def actA(act, e=e):
                for i in range(32):
                    j, tc = i // 2, i % 2
                    uof = (i % 2) * 1024
                    act.wait_ge(b.PE.h, peA_marks[i])
                    if i == 0:
                        act.wait_ge(DS(b, "db1c").h, b.b1_ld)
                    sof = (i % 2) * 512
                    ins = act.activation(sigs[:, sof:sof + 512],
                                         b.psA[:, uof + 512:uof + 1024],
                                         AF.Sigmoid, bias=b1c[:, 16 + j:17 + j],
                                         scale=1.0)
                    b.AC.inc(ins)
            on_act(b, actA)

            def dveA(dve, e=e):
                for i in range(32):
                    j, tc = i // 2, i % 2
                    uof = (i % 2) * 1024
                    dve.wait_ge(b.PO.h, po_mul[(j, tc)])
                    sof = (i % 2) * 512
                    dst = act_s[:, j * TH + tc * 512:j * TH + (tc + 1) * 512]
                    ins = dve.scalar_tensor_tensor(dst, b.psA[:, uof:uof + 512],
                                                   b1c[:, j:j + 1],
                                                   sigs[:, sof:sof + 512],
                                                   ALU.add, ALU.mult)
                    b.DV.inc(ins)
                    assert b.DV.n == actm[(j, tc)]
            on_dve(b, dveA)
            actA_dv = b.DV.n

            yev = []

            yevd = [b.DV.n + 1, b.DV.n + 2]

            def peB(pe, e=e, th=th, toff=toff, actA_dv=actA_dv):
                pe.wait_ge(DS(b, "dw2e").h, b.w2_ld)
                pe.wait_ge(b.DV.h, actA_dv)
                if b.yadd_dv:
                    pe.wait_ge(b.DV.h, b.yadd_dv)
                for tc in range(2):
                    if tc == 1:
                        pe.wait_ge(b.DV.h, yevd[0])
                    tslm = slice(toff + tc * 512, toff + (tc + 1) * 512)
                    for d in range(DC):
                        dsl = slice((d % 4) * 512, (d % 4) * 512 + 512)
                        ps = b.psA if d < 4 else b.psB
                        for kf in range(16):
                            ins = pe.matmul(
                                ps[:, dsl],
                                w2e[:, kf * 1024 + d * 128:kf * 1024 + (d + 1) * 128],
                                act_s[:, kf * TH + tc * 512:kf * TH + (tc + 1) * 512],
                                start=(kf == 0), stop=(kf == 15 and e != NE - 1))
                        if e == NE - 1:
                            ins = pe.matmul(ps[:, dsl],
                                            b2t[0:8, d * 128:(d + 1) * 128],
                                            b.mask[0:8, tslm],
                                            start=False, stop=True)
                    b.PE.inc(ins)
                    yev.append(b.PE.n)
            on_pe(b, peB)
            b.w2_use = b.PE.n

            def dveB(dve, e=e):
                for tc in range(2):
                    dve.wait_ge(b.PE.h, yev[tc])
                    for d in range(DC):
                        dsl = slice((d % 4) * 512, (d % 4) * 512 + 512)
                        ps = b.psA if d < 4 else b.psB
                        ysl = y[d][:, tc * 512:(tc + 1) * 512]
                        ins = dve.tensor_add(ysl, ysl, ps[:, dsl])
                    assert b.DV.inc(ins) == yevd[tc]
            on_dve(b, dveB)
            b.yadd_dv = b.DV.n

        final_combine_th(b, th, y, xfin, gsc)

    a.free("act_s", "w2e", "maskb", "sigs", "b2t", "mask", "e8t",
           *[f"w1b{i}" for i in range(4)], *[f"y{d}" for d in range(DC)],
           *[f"h2_{k}" for k in range(DC)])
    b.af.free("b1c")


def final_combine_th(b, th, y, xfin, gsc):
    toff = th * 1024
    sbf = b.abf.alloc("sch", 1024)
    prev_out = 0
    for d in range(DC):
        tsl = slice(toff, toff + 1024)

        dfx = DS(b, "dfx")

        def sp_x(sp, d=d, th=th, tsl=tsl, prev_out=prev_out):
            sp.wait_ge(b.DV.h, b.yadd_dv)
            if prev_out:
                sp.wait_ge(b.DV.h, b.fin_dv)
            sp.dma_start(out=xfin, in_=b.dI["xT"][d * 128:(d + 1) * 128, tsl]
                         ).then_inc(dfx.h, 16)
            sp.dma_start(out=sbf, in_=b.d_s[d * 128:(d + 1) * 128, tsl]
                         ).then_inc(dfx.h, 16)
            dfx.n += 32
            b.x_ld = dfx.n
        on_sp(b, sp_x)

        def dve_f1(dve, d=d, tsl=tsl):
            dve.wait_ge(b.DV.h, b.yadd_dv)
            if b.out_sd:
                dve.wait_ge(DS(b, "dout").h, b.out_sd)
            ins = dve.tensor_add(gsc, b.xn[d][:, tsl], y[d][:, :])
            b.DV.inc(ins)
        on_dve(b, dve_f1)
        f1_dv = b.DV.n

        def act_f(act):
            act.wait_ge(b.DV.h, f1_dv)
            ins = act.activation(gsc, gsc, AF.Silu)
            b.AC.inc(ins)
        on_act(b, act_f)
        f_ac = b.AC.n

        def dve_f2(dve, d=d):
            dve.wait_ge(b.AC.h, f_ac)
            dve.wait_ge(DS(b, "dfx").h, b.x_ld)
            dve.tensor_mul(gsc, sbf, gsc)
            ins = dve.tensor_add(gsc, gsc, xfin)
            b.DV.inc(ins)
        on_dve(b, dve_f2)
        b.fin_dv = b.DV.n
        prev_out = 1

        dout = DS(b, "dout")

        def sp_out(sp, d=d, tsl=tsl):
            sp.wait_ge(b.DV.h, b.fin_dv)
            sp.dma_start(out=b.d_out[d * 128:(d + 1) * 128, tsl], in_=gsc
                         ).then_inc(dout.h, 16)
            dout.n += 16
            b.out_sd = dout.n
        on_sp(b, sp_out)
    b.abf.free("sch")


def finish(b, tap=None):
    tap = tap if tap is not None else DEBUG_TAP
    if tap is not None and BUILD_PHASES < 99:
        tiles = b.taps[tap]

        dtap = DS(b, "dtap")

        def po_tap(po):
            wait_snap(po, b, snap(b), pe=True, dv=True, ac=True)
            for k, t in enumerate(tiles[:8]):
                po.dma_start(out=b.d_out[k * 128:(k + 1) * 128, 0:t.shape[-1]],
                             in_=t).then_inc(dtap.h, 16)
                dtap.n += 16
        on_pool(b, po_tap)

    def sp_end(sp):
        for s_ in b.dsems.values():
            if s_.n:
                sp.wait_ge(s_.h, s_.n)
    on_sp(b, sp_end)



def prep_inputs(inputs):
    f32 = np.float32
    g = {k: np.asarray(v, f32) for k, v in inputs.items()}
    m = {}
    for k, v in host_constants().items():
        m["c_" + k] = np.ascontiguousarray(v)
    m["wqkv"] = np.stack([tile_kMf(g["L0_attn_W_qkv"][l], 24) for l in range(NL)])
    m["wr"] = np.stack([tile_kMf(g["L0_attn_W_r"][l], DC) for l in range(NL)])
    m["wo"] = np.stack([tile_kMf(g["L0_attn_W_o"][l], DC) for l in range(NL)])
    m["wff1"] = np.ascontiguousarray(g["L0_attn_W_ff1"].reshape(NL, DC, 128, DF))
    w2 = g["L0_attn_W_ff2"].reshape(NL, 32, 128, DC, 128)
    m["wff2"] = np.ascontiguousarray(w2.transpose(0, 3, 2, 1, 4))
    m["u_t"] = np.ascontiguousarray(g["L0_attn_u"].T)
    m["v_t"] = np.ascontiguousarray(g["L0_attn_v"].T)
    for nm, key in (("ln1g", "L0_attn_ln1_g"), ("ln1b", "L0_attn_ln1_b"),
                    ("ln2g", "L0_attn_ln2_g"), ("ln2b", "L0_attn_ln2_b"),
                    ("bff1", "L0_attn_b_ff1"), ("bff2", "L0_attn_b_ff2")):
        m[nm] = np.stack([col_bias(g[key][l]) for l in range(NL)])
    m["fwq"] = tile_kMf(g["L0_ffn_Wq"], DC)
    m["fwk"] = tile_kMf(g["L0_ffn_Wk"], 2)
    m["fwv"] = tile_kMf(g["L0_ffn_Wv"], 2)
    m["fwg"] = g["L0_ffn_Wg"]
    m["fwo"] = tile_kMf(g["L0_ffn_Wo"], DC)
    m["fwr"] = np.ascontiguousarray(g["L0_ffn_Wr"].reshape(DC, 128, NE))
    m["fbr"] = np.ascontiguousarray(g["L0_ffn_br"].reshape(1, NE))
    w1 = g["L0_ffn_W1"].reshape(NE, DC, 128, 32, 128)
    m["fw1"] = np.ascontiguousarray(w1.transpose(0, 3, 2, 1, 4))
    m["fb1"] = np.stack([col_bias(g["L0_ffn_b1"][e]) for e in range(NE)])
    m["fw2"] = np.ascontiguousarray(g["L0_ffn_W2"])
    m["fb2"] = np.ascontiguousarray(g["L0_ffn_b2"])
    m["fbq"] = col_bias(g["L0_ffn_bq"])
    m["fbk"] = col_bias(g["L0_ffn_bk"])
    m["fbv"] = col_bias(g["L0_ffn_bv"])
    m["fbg"] = np.ascontiguousarray(g["L0_ffn_bg"].reshape(DH, 1))
    m["fbo"] = col_bias(g["L0_ffn_bo"])
    m["fln1"] = col_bias(g["L0_ffn_ln1_w"])
    m["fln2"] = col_bias(g["L0_ffn_ln2_w"])
    m["fqn"] = np.ascontiguousarray(g["L0_ffn_qn_w"].reshape(1, DH))
    m["fkn"] = np.ascontiguousarray(g["L0_ffn_kn_w"].reshape(1, DH))
    return m


def kernel(**inputs):
    x = np.asarray(inputs["L0_attn_x"], np.float32)
    shared = prep_inputs(inputs)
    nc = build_program()
    in_maps = []
    for i in range(8):
        im = dict(shared)
        im["xT"] = np.ascontiguousarray(x[i].T)
        in_maps.append(im)
    res = run_bass_kernel_spmd(nc, in_maps, core_ids=list(range(8)))
    outs = [np.asarray(res.results[i]["out"]).T for i in range(8)]
    return np.stack(outs).astype(np.float32)
